# version 11
# speedup vs baseline: 1.2118x; 1.0050x over previous
import math

import numpy as np

try:
    import concourse.bass as bass
    import concourse.mybir as mybir
    from concourse.bass_utils import run_bass_kernel_spmd
except ImportError:
    import sys

    sys.path.insert(0, "/opt/trn_rl_repo")
    import concourse.bass as bass
    import concourse.mybir as mybir
    from concourse.bass_utils import run_bass_kernel_spmd

import ml_dtypes

F32 = mybir.dt.float32
BF16 = mybir.dt.bfloat16
AF = mybir.ActivationFunctionType
ALU = mybir.AluOpType
AX = mybir.AxisListType
BF16NP = ml_dtypes.bfloat16

B = 2
L = 8192
D = 1024
TPC = 2048
NCH = 8
SUP = 512
NSUP = 16
PAT = [0, 1, 2, 3, 3, 2, 1, 0, 0, 1, 2, 3, 3, 2, 1, 0]
SLOTS = [[s for s in range(NSUP) if PAT[s] == p] for p in range(4)]
LMAX = [max(SLOTS[p][m] + 1 for p in range(4)) for m in range(4)]
_BT_OFF = [0]
for m in range(4):
    _BT_OFF.append(_BT_OFF[-1] + (LMAX[m] - 1))
NBT_NONDIAG = _BT_OFF[-1]
NBT = NBT_NONDIAG + 4


def bt_col(m, t):
    return _BT_OFF[m] + t


def bt_diag(m):
    return NBT_NONDIAG + m


def sup_addr(s):
    r = PAT[s]
    return r, SLOTS[r].index(s)


H = 16
HD = 64
DC = 256
DR = 128
EH = 896
DIN = 2048
DST = 16
DCONV = 4
CHPC = 512
KCH = 4
FF = 4096
LN_EPS = 1e-5
MAGIC = 12582912.0


def _rope_tables():
    inv = 1.0 / 10000.0 ** (np.arange(0, DR, 2, dtype=np.float64) / DR)
    t = np.arange(L, dtype=np.float64)
    fr = t[:, None] * inv[None, :]
    emb = np.concatenate([fr, fr], axis=1)
    return (np.cos(emb).T.astype(np.float32).copy(),
            np.sin(emb).T.astype(np.float32).copy())


def _diag_masks():
    out = np.zeros((4, 128, SUP), dtype=np.float32)
    for i in range(4):
        kk = 128 * i + np.arange(128)[:, None]
        qq = np.arange(SUP)[None, :]
        out[i] = np.where(kk <= qq, 0.0, -1e9).astype(np.float32)
    return out


def _padtab(p):
    tab = np.zeros((128, NBT), dtype=np.float32)
    for m in range(4):
        s_mine = SLOTS[p][m]
        for t in range(LMAX[m] - 1):
            if t >= s_mine:
                tab[:, bt_col(m, t)] = -1e9
    return tab


def _tokens_of(p):
    idx = []
    for m in range(4):
        s = SLOTS[p][m]
        idx.extend(range(SUP * s, SUP * s + SUP))
    return np.array(idx, dtype=np.int64)


def _rot_lhsT():
    R = np.zeros((DR, DR), dtype=np.float32)
    for i in range(64):
        R[i, i + 64] = -1.0
        R[i + 64, i] = 1.0
    return np.ascontiguousarray(R.T)


def make_inputs(inp):
    cosT, sinT = _rope_tables()
    masks = _diag_masks()
    rotT = _rot_lhsT()
    ident = np.eye(128, dtype=np.float32)
    x = np.asarray(inp["L0_attn_x"])

    def T(a):
        return np.ascontiguousarray(np.asarray(a).T)

    def _pp(a):
        K_, c = a.shape[0] // 128, a.shape[1]
        return np.ascontiguousarray(a.reshape(K_, 128, c).transpose(1, 0, 2).reshape(128, K_ * c))

    w0 = np.asarray(inp["L0_ffn_W"])
    maps = []
    for c in range(8):
        b, p = c // 4, c % 4
        tok = _tokens_of(p)
        m = {
            "xin": np.ascontiguousarray(x[b][tok].T),
            "w0slice": np.ascontiguousarray(w0[:, 128 * c:128 * (c + 1), :]),
            "b0all": np.asarray(inp["L0_ffn_b"]),
            "logits": np.asarray(inp["L0_ffn_logits"]).reshape(1, 8),
            "uu": np.asarray(inp["L0_ffn_u"]).reshape(1, 8),
            "wq": np.asarray(inp["L0_attn_wq"]),
            "bq": np.asarray(inp["L0_attn_bq"]).reshape(1, -1),
            "wqc": np.asarray(inp["L0_attn_wqc"]),
            "bqc": np.asarray(inp["L0_attn_bqc"]).reshape(1, -1),
            "wqr": np.asarray(inp["L0_attn_wqr"]),
            "bqr": np.asarray(inp["L0_attn_bqr"]).reshape(1, -1),
            "wkv": np.asarray(inp["L0_attn_wkv"]),
            "bkv": np.asarray(inp["L0_attn_bkv"]).reshape(1, -1),
            "wkr": np.asarray(inp["L0_attn_wkr"]),
            "bkr": np.asarray(inp["L0_attn_bkr"]).reshape(1, -1),
            "wkc": np.asarray(inp["L0_attn_wkc"]),
            "bkc": np.asarray(inp["L0_attn_bkc"]).reshape(1, -1),
            "wv": np.asarray(inp["L0_attn_wv"]),
            "bv": np.asarray(inp["L0_attn_bv"]).reshape(1, -1),
            "wo": np.asarray(inp["L0_attn_wo"]),
            "bo": np.asarray(inp["L0_attn_bo"]).reshape(1, -1),
            "cosT": np.ascontiguousarray(cosT[:, tok]),
            "sinT": np.ascontiguousarray(sinT[:, tok]),
            "masks": masks.astype(BF16NP),
            "padtab": _padtab(p),
            "ident": ident.astype(BF16NP),
            "rotT": rotT,
            "w1Ts": np.ascontiguousarray(T(inp["L1_ffn_W1"])[:, 512 * c:512 * (c + 1)]),
            "w2Ts": np.ascontiguousarray(T(inp["L1_ffn_W2"])[512 * c:512 * (c + 1), :]),
            "b1row": np.asarray(inp["L1_ffn_b1"]).reshape(1, -1),
            "b2col": _pp(np.asarray(inp["L1_ffn_b2"]).reshape(-1, 1)),
            "inxbw": np.ascontiguousarray(T(inp["L1_attn_in_proj_w"])[:, 512 * p:512 * (p + 1)]),
            "inzw": np.ascontiguousarray(T(inp["L1_attn_in_proj_w"])[:, 2048 + 512 * p:2048 + 512 * (p + 1)]),
            "xprojT": np.ascontiguousarray(T(inp["L1_attn_x_proj_w"])[512 * p:512 * (p + 1), :]),
            "convw": _pp(np.asarray(inp["L1_attn_conv_w"])[512 * p:512 * (p + 1), :]),
            "convb": _pp(np.asarray(inp["L1_attn_conv_b"])[512 * p:512 * (p + 1)].reshape(-1, 1)),
            "Alog": _pp(np.asarray(inp["L1_attn_A_log"])[512 * p:512 * (p + 1), :]),
            "Dpar": _pp(np.asarray(inp["L1_attn_D_param"])[512 * p:512 * (p + 1)].reshape(-1, 1)),
            "dtw": _pp(np.asarray(inp["L1_attn_dt_w"])[512 * p:512 * (p + 1), :]),
            "dtb": _pp(np.asarray(inp["L1_attn_dt_b"])[512 * p:512 * (p + 1)].reshape(-1, 1)),
            "lng": _pp(np.asarray(inp["L1_attn_ln_g"])[512 * p:512 * (p + 1)].reshape(-1, 1)),
            "lnb": _pp(np.asarray(inp["L1_attn_ln_b"])[512 * p:512 * (p + 1)].reshape(-1, 1)),
            "outwT": np.ascontiguousarray(T(inp["L1_attn_out_w"])[512 * p:512 * (p + 1), :]),
        }
        maps.append(m)
    return maps


def assemble_output(results):
    out = np.empty((B, L, D), dtype=np.float32)
    for c in range(8):
        b, p = c // 4, c % 4
        tok = _tokens_of(p)
        out[b][tok] = np.asarray(results[c]["xout"]).T
    return out


_EMIT_OPS = {
    "tensor_copy", "tensor_tensor", "tensor_add", "tensor_sub", "tensor_mul",
    "tensor_max", "tensor_scalar", "tensor_single_scalar", "tensor_scalar_add",
    "tensor_scalar_sub", "tensor_scalar_mul", "tensor_scalar_max",
    "tensor_scalar_min", "tensor_relu", "tensor_reduce", "reduce_max",
    "reduce_sum", "scalar_tensor_tensor", "tensor_tensor_scan", "memset",
    "memzero", "reciprocal", "select", "copy_predicated", "activation",
    "copy", "add", "mul", "square", "sqrt", "sign", "bn_stats", "bn_aggr",
    "transpose", "tensor_tensor_reduce",
}


class _SyncedEng:

    def __init__(self, eng, pb, semname):
        self._eng = eng
        self._pb = pb
        self._sem = semname

    def __getattr__(self, a):
        f = getattr(self._eng, a)
        if a not in _EMIT_OPS:
            return f
        pb, sem = self._pb, self._sem

        def wrapped(*args, **kw):
            s = pb.sem(sem)
            if pb.cnt[sem] > 0:
                self._eng.wait_ge(s, pb.cnt[sem])
            inst = f(*args, **kw)
            pb.cnt[sem] += 1
            inst.then_inc(s, 1)
            return inst
        return wrapped


class PB:

    ENGS = ("tensor", "vector", "scalar", "gpsimd", "sync")

    def __init__(self, nc, stack):
        self.nc = nc
        self.stack = stack
        self.block = stack.enter_context(nc.Block())
        self.sems = {}
        self.cnt = {}

    def sem(self, name):
        if name not in self.sems:
            self.sems[name] = self.stack.enter_context(self.nc.semaphore(name))
            self.cnt[name] = 0
        return self.sems[name]

    def bump(self, inst, name, amt=1):
        if name in ("V", "S"):
            self.sem(name)
            return self.cnt[name]
        s = self.sem(name)
        self.cnt[name] += amt
        inst.then_inc(s, amt)
        return self.cnt[name]

    def val(self, name):
        self.sem(name)
        return self.cnt[name]

    def wait(self, eng, name, v=None):
        s = self.sem(name)
        eng.wait_ge(s, self.cnt[name] if v is None else v)

    def on(self, which, fn):
        if which == "vector":
            getattr(self.block, which)(lambda e: fn(_SyncedEng(e, self, "V")))
        elif which == "scalar":
            getattr(self.block, which)(lambda e: fn(_SyncedEng(e, self, "S")))
        else:
            getattr(self.block, which)(fn)

    def barrier(self):
        snap = dict(self.cnt)

        def mk(eng):
            def f(e):
                for name, v in snap.items():
                    if v > 0:
                        e.wait_ge(self.sems[name], v)
            return f
        for w in self.ENGS:
            self.on(w, mk(w))


def ceil_div(a, b):
    return (a + b - 1) // b



def emit_consts(pb, io, sb):
    def f(sync):
        for a4 in range(4):
            i = sync.dma_start(out=sb.masks.ap()[:, SUP * a4:SUP * (a4 + 1)],
                               in_=io["masks"][a4])
            pb.bump(i, "LD", 16)
        for dst, src in [
            (sb.padtab.ap(), io["padtab"]), (sb.ident.ap(), io["ident"]),
            (sb.rotT.ap(), io["rotT"]),
        ]:
            i = sync.dma_start(out=dst, in_=src)
            pb.bump(i, "LD", 16)
    pb.on("sync", f)

    def g(vector):
        vector.memset(sb.onesr.ap(), 1.0)
        vector.memset(sb.onescol.ap(), 1.0)
        for val, col in sb.CVALS.items():
            i = vector.memset(sb.cpool.ap()[:, col:col + 1], val)
        pb.bump(i, "V")
    pb.on("vector", g)


def emit_xin(pb, io, sb):
    def f(sync):
        i = sync.dma_start(out=sb.xs.ap().rearrange("p (k t) -> p k t", t=TPC), in_=io["xin"].rearrange("(k p) t -> p k t", p=128))
        pb.bump(i, "LD", 16)
    pb.on("sync", f)


def emit_expert_prep(pb, io, dr, sb):
    nc = pb.nc
    lrow = sb.pool_row(8)
    urow = sb.pool_row(8)
    grow = sb.pool_row(8)
    srow = sb.pool_row(8)
    ohrow = sb.pool_row(8)
    mx = sb.pool_row(1)
    negmx = sb.pool_row(1)
    lse = sb.pool_row(1)
    shift = sb.pool_row(1)
    b0sb = sb.BC.f32()[0:8, 2048:3072]

    def f(sync):
        i = sync.dma_start(out=lrow, in_=io["logits"])
        pb.bump(i, "LD", 16)
        i = sync.dma_start(out=urow, in_=io["uu"])
        pb.bump(i, "LD", 16)
        i = sync.dma_start(out=b0sb, in_=io["b0all"])
        pb.bump(i, "LD", 16)
    pb.on("sync", f)

    def v(vector):
        pb.wait(vector, "LD")
        i = vector.reduce_max(mx, lrow, AX.X, negate=False)
        i = vector.tensor_scalar_mul(negmx, mx, -1.0)
        pb.bump(i, "V")
    pb.on("vector", v)

    erow = sb.pool_row(8)
    sesum = sb.pool_row(1)

    def s(scalar):
        pb.wait(scalar, "V")
        scalar.activation(erow, lrow, AF.Exp, bias=negmx, accum_out=sesum)
        i = scalar.activation(lse, sesum, AF.Ln)
        scalar.activation(grow, urow, AF.Ln)
        pb.bump(i, "S")
    pb.on("scalar", s)

    def v2(vector):
        pb.wait(vector, "S")
        i = vector.tensor_scalar_mul(grow, grow, -1.0)
        pb.bump(i, "V")
    pb.on("vector", v2)

    def s2(scalar):
        pb.wait(scalar, "V")
        i = scalar.activation(grow, grow, AF.Ln)
        pb.bump(i, "S")
    pb.on("scalar", s2)

    mxs = sb.pool_row(1)

    def v3(vector):
        pb.wait(vector, "S")
        vector.tensor_sub(srow, lrow, grow)
        vector.tensor_add(shift, mx, lse)
        vector.tensor_scalar_mul(shift, shift, -1.0)
        vector.tensor_scalar_add(srow, srow, shift)
        vector.reduce_max(mxs, srow, AX.X)
        i = vector.tensor_single_scalar(ohrow, srow, mxs, ALU.is_ge)
        pb.bump(i, "V")
    pb.on("vector", v3)

    def g1(gp):
        pb.wait(gp, "V")
        i = gp.dma_start(out=dr.ohd.ap(), in_=ohrow)
        pb.bump(i, "ST", 16)
    pb.on("gpsimd", g1)

    ohcol = sb.ohcol
    ohb = sb.ohb

    def f2(sync):
        pb.wait(sync, "ST")
        i = sync.dma_start(out=ohcol.ap(), in_=dr.ohd.ap().rearrange("a b -> b a"))
        pb.bump(i, "LD", 16)
    pb.on("sync", f2)

    def t1(tensor):
        pb.wait(tensor, "LD")
        pb.wait(tensor, "V")
        tensor.matmul(pb.PSA[0:1, 0:512], ohcol.ap(), b0sb[:, 0:512], start=True, stop=True)
        i = tensor.matmul(pb.PSA[0:1, 512:1024], ohcol.ap(), b0sb[:, 512:1024], start=True, stop=True)
        i = tensor.matmul(pb.PSB[:, 0:8], sb.onesr.ap()[:, 0:128], ohrow, start=True, stop=True)
        pb.bump(i, "T")
    pb.on("tensor", t1)

    beffr = sb.pool_row_at(1024, 1024)

    def v4(vector):
        pb.wait(vector, "T")
        vector.tensor_copy(beffr, pb.PSA[0:1, 0:1024])
        i = vector.tensor_copy(ohb.ap(), pb.PSB[:, 0:8])
        pb.bump(i, "V")
    pb.on("vector", v4)

    def gbe(gp):
        pb.wait(gp, "V")
        i = gp.dma_start(out=dr.beffd.ap(), in_=beffr)
        pb.bump(i, "ST", 16)
    pb.on("gpsimd", gbe)

    wstage = sb.BC.f32()
    acc = wstage[:, 1024:2048]
    for i8 in range(8):
        st = wstage[:, 0:1024]

        def fl(sync, i8=i8, st=st):
            if i8 >= 1:
                pb.wait(sync, "V")
            i = sync.dma_start(out=st, in_=io["w0slice"][i8])
            pb.bump(i, "LD", 16)
        pb.on("sync", fl)

        def vv(vector, i8=i8, st=st):
            pb.wait(vector, "LD")
            if i8 == 0:
                i = vector.tensor_scalar(acc, st, ohb.ap()[:, 0:1], None, ALU.mult)
            else:
                i = vector.scalar_tensor_tensor(acc, st, ohb.ap()[:, i8:i8 + 1], acc,
                                                ALU.mult, ALU.add)
            pb.bump(i, "V")
        pb.on("vector", vv)

    weffbf = sb.BD.bf16()[:, 0:1024]

    def v5(vector):
        i = vector.tensor_copy(weffbf, acc)
        pb.bump(i, "V")
    pb.on("vector", v5)

    def g2(gp):
        pb.wait(gp, "V")
        i = gp.dma_start(out=dr.weffs.ap(), in_=weffbf)
        pb.bump(i, "ST", 16)
        i = gp.collective_compute(
            "AllGather", ALU.bypass, replica_groups=[list(range(8))],
            ins=[dr.weffs.ap()], outs=[dr.weffg.ap()])
        pb.bump(i, "ccweff", 1)
    pb.on("gpsimd", g2)


def emit_ternarize(pb, io, dr, sb, stop=99):
    w1st = sb.BA.f32()[:, 0:4096]
    w2st = sb.BB.f32()[:, 0:4096]
    asum = sb.pool_vec(4)
    s1p = sb.s1p

    def f(sync):
        i = sync.dma_start(out=w1st.rearrange("p (k c) -> p k c", c=512), in_=io["w1Ts"].rearrange("(k p) c -> p k c", p=128))
        pb.bump(i, "LD", 16)
        i = sync.dma_start(out=w2st.rearrange("p (k c) -> p k c", c=1024), in_=io["w2Ts"].rearrange("(k p) c -> p k c", p=128))
        pb.bump(i, "LD", 16)
    pb.on("sync", f)

    if stop <= 1:
        return

    def v(vector):
        pb.wait(vector, "LD")
        vector.memset(s1p.ap(), 0.0)
        vector.tensor_reduce(asum[:, 0:1], w1st, AX.X, ALU.add, apply_absolute_value=True)
        i = vector.tensor_reduce(asum[:, 1:2], w2st, AX.X, ALU.add, apply_absolute_value=True)
        pb.bump(i, "V")
    pb.on("vector", v)

    def t(tensor):
        pb.wait(tensor, "V")
        tensor.matmul(pb.PSA[0:1, 0:1], asum[:, 0:1], sb.onescol.ap(), start=True, stop=True)
        i = tensor.matmul(pb.PSA[0:1, 1:2], asum[:, 1:2], sb.onescol.ap(), start=True, stop=True)
        pb.bump(i, "T")
    pb.on("tensor", t)

    def v2(vector):
        pb.wait(vector, "T")
        i = vector.tensor_copy(s1p.ap()[:, 0:2], pb.PSA[0:1, 0:2])
        pb.bump(i, "V")
    pb.on("vector", v2)

    if stop <= 2:
        return

    def g(gp):
        pb.wait(gp, "V")
        i = gp.dma_start(out=dr.s1pd.ap(), in_=s1p.ap())
        pb.bump(i, "ST", 16)
        i = gp.collective_compute(
            "AllReduce", ALU.add, replica_groups=[list(range(8))],
            ins=[dr.s1pd.ap()], outs=[dr.s1rd.ap()])
        pb.bump(i, "ccs1", 1)
    pb.on("gpsimd", g)

    if stop <= 3:
        return
    stot = sb.pool_row(2)

    def f2(sync):
        pb.wait(sync, "ccs1", 1)
        i = sync.dma_start(out=stot, in_=dr.s1rd.ap()[:, 0:2])
        pb.bump(i, "LD", 16)
    pb.on("sync", f2)

    srow = sb.pool_row(2)

    def v3(vector):
        pb.wait(vector, "LD")
        vector.tensor_scalar_mul(srow, stot, 1.0 / (FF * D))
        i = vector.tensor_scalar_max(srow, srow, 1e-8)
        pb.bump(i, "V")
    pb.on("vector", v3)

    def t2(tensor):
        pb.wait(tensor, "V")
        i = tensor.matmul(pb.PSB[:, 0:2], sb.onesr.ap()[:, 0:128], srow, start=True, stop=True)
        pb.bump(i, "T")
    pb.on("tensor", t2)

    sv = sb.pool_vec(4)

    def v4(vector):
        pb.wait(vector, "T")
        vector.tensor_copy(sv[:, 0:2], pb.PSB[:, 0:2])
        i = vector.reciprocal(sv[:, 2:4], sv[:, 0:2])
        pb.bump(i, "V")
    pb.on("vector", v4)

    if stop <= 4:
        return
    w1bf = sb.BD.bf16()[:, 0:4096]
    w2bf = sb.BD.bf16()[:, 4096:8192]
    for (wst, wbf, scol, icol) in [(w1st, w1bf, 0, 2), (w2st, w2bf, 1, 3)]:
        def v5(vector, wst=wst, wbf=wbf, scol=scol, icol=icol):
            vector.tensor_scalar(wst, wst, sv[:, icol:icol + 1], MAGIC, ALU.mult, ALU.add)
            vector.tensor_scalar_add(wst, wst, -MAGIC)
            vector.tensor_scalar_min(wst, wst, 1.0)
            vector.tensor_scalar_max(wst, wst, -1.0)
            i = vector.tensor_copy(wbf, wst)
            pb.bump(i, "V")
        pb.on("vector", v5)

    if stop <= 5:
        return

    def g2(gp):
        pb.wait(gp, "V")
        i = gp.dma_start(out=dr.w1ts.ap().rearrange("(k p) c -> p k c", p=128), in_=w1bf.rearrange("p (k c) -> p k c", c=512))
        pb.bump(i, "ST", 16)
        i = gp.dma_start(out=dr.w2ts.ap().rearrange("(k p) c -> p k c", p=128), in_=w2bf.rearrange("p (k c) -> p k c", c=1024))
        pb.bump(i, "ST", 16)
        if stop <= 6:
            return
        gp.wait_ge(pb.sems["ST"], pb.cnt["ST"])
        i = gp.collective_compute(
            "AllGather", ALU.bypass, replica_groups=[list(range(8))],
            ins=[dr.w1ts.ap()], outs=[dr.w1tg.ap()])
        pb.bump(i, "ccw1", 1)
        if stop <= 7:
            return
        i = gp.collective_compute(
            "AllGather", ALU.bypass, replica_groups=[list(range(8))],
            ins=[dr.w2ts.ap()], outs=[dr.w2tg.ap()])
        pb.bump(i, "ccw2", 1)
    pb.on("gpsimd", g2)


class Carve:

    def __init__(self, nc, name, cols_f32):
        self.t = nc.alloc_sbuf_tensor("sb_" + name, [128, cols_f32], F32)
        self.cols = cols_f32

    def f32(self):
        return self.t.ap()

    def bf16(self):
        return self.t.ap().bitcast(BF16)


class SB:

    def __init__(self, nc):
        self.xs = nc.alloc_sbuf_tensor("sb_xs", [128, NCH * TPC], F32)
        self.BA = Carve(nc, "BA", 8192)
        self.BB = Carve(nc, "BB", 8192)
        self.WBUF = nc.alloc_sbuf_tensor("sb_WBUF", [128, 8192], BF16)
        self.BC = Carve(nc, "BC", 3584)
        self.BD = Carve(nc, "BD", 4096)
        self.onesr = nc.alloc_sbuf_tensor("sb_onesr", [1, 512], F32)
        self.onescol = nc.alloc_sbuf_tensor("sb_onescol", [128, 1], F32)
        self.masks = nc.alloc_sbuf_tensor("sb_masks", [128, 4 * SUP], BF16)
        self.padtab = nc.alloc_sbuf_tensor("sb_padtab", [128, NBT], F32)
        self.btab = nc.alloc_sbuf_tensor("sb_btab", [128, NBT], F32)
        self.ident = nc.alloc_sbuf_tensor("sb_ident", [128, 128], BF16)
        self.rotT = nc.alloc_sbuf_tensor("sb_rotT", [128, 128], F32)
        self.ohcol = nc.alloc_sbuf_tensor("sb_ohcol", [8, 1], F32)
        self.ohb = nc.alloc_sbuf_tensor("sb_ohb", [128, 8], F32)
        self.s1p = nc.alloc_sbuf_tensor("sb_s1p", [1, 128], F32)
        self.rows = nc.alloc_sbuf_tensor("sb_rows", [1, 2048], F32)
        self._rowoff = 0
        self.vecs = nc.alloc_sbuf_tensor("sb_vecs", [128, 2048], F32)
        self._vecoff = 0
        self.states = nc.alloc_sbuf_tensor("sb_states", [128, KCH * DST], F32)
        self.Aneg = nc.alloc_sbuf_tensor("sb_Aneg", [128, KCH * DST], F32)
        self.pvec = nc.alloc_sbuf_tensor("sb_pvec", [128, 256], F32)
        self._pvoff = 0
        self.cpool = nc.alloc_sbuf_tensor("sb_cpool", [128, 8], F32)
        self.rows2 = nc.alloc_sbuf_tensor("sb_rows2", [1, TPC], F32)
        self.CVALS = {1e-5: 0, -0.5: 1, -1.0: 2, 1.0: 3, 0.125: 4, 0.5: 5}

    def pool_row(self, n):
        ap = self.rows.ap()[:, self._rowoff:self._rowoff + n]
        self._rowoff += n
        assert self._rowoff <= 2048
        return ap

    def pool_row_at(self, off, n):
        return self.rows.ap()[:, off:off + n]

    def row_reset(self):
        self._rowoff = 0

    def vec_carve(self, off, n):
        return self.vecs.ap()[:, off:off + n]

    def cv(self, val, P=128):
        return self.cpool.ap()[0:P, self.CVALS[val]:self.CVALS[val] + 1]

    def pool_vec(self, n):
        ap = self.vecs.ap()[:, self._vecoff:self._vecoff + n]
        self._vecoff += n
        assert self._vecoff <= 512
        return ap

    def pv(self, n):
        ap = self.pvec.ap()[:, self._pvoff:self._pvoff + n]
        self._pvoff += n
        assert self._pvoff <= 256
        return ap


class DRT:

    def __init__(self, nc, grp):
        g8 = [list(range(8))]
        self.grp = grp
        self.ohd = nc.dram_tensor("ohd", [1, 8], F32)
        self.beffd = nc.dram_tensor("beffd", [1, 1024], F32)
        self.weffs = nc.dram_tensor("weffs", [128, 1024], BF16)
        self.weffg = nc.dram_tensor("weffg", [8 * 128, 1024], BF16)
        self.s1pd = nc.dram_tensor("s1pd", [1, 128], F32)
        self.s1rd = nc.dram_tensor("s1rd", [1, 128], F32)
        self.w1ts = nc.dram_tensor("w1ts", [1024, 512], BF16)
        self.w1tg = nc.dram_tensor("w1tg", [8 * 1024, 512], BF16)
        self.w2ts = nc.dram_tensor("w2ts", [512, 1024], BF16)
        self.w2tg = nc.dram_tensor("w2tg", [8 * 512, 1024], BF16)
        self.ks = nc.dram_tensor("ks", [1024, TPC], BF16)
        self.kgat = nc.dram_tensor("kgat", [4 * 1024, TPC], BF16)
        self.vs = nc.dram_tensor("vs", [TPC, 1024], BF16)
        self.vgat = nc.dram_tensor("vgat", [4 * TPC, 1024], BF16)
        self.ln5s = nc.dram_tensor("ln5s", [1024, TPC], BF16)
        self.ln5g = nc.dram_tensor("ln5g", [4 * 1024, TPC], BF16)
        self.xbcst = nc.dram_tensor("xbcst", [KCH, 128, L], F32)
        self.dtst = nc.dram_tensor("dtst", [KCH, 128, L], F32)
        self.zst = nc.dram_tensor("zst", [KCH, 128, L], F32)
        self.yst = nc.dram_tensor("yst", [KCH, 128, L], F32)
        self.xpk = nc.dram_tensor("xpk", [KCH, 33, L], F32)
        self.xprojp = nc.dram_tensor("xprojp", [33, L], F32)
        self.xprojr = nc.dram_tensor("xprojr", [33, L], F32)
        self.statsp = nc.dram_tensor("statsp", [1, 2 * L], F32)
        self.statsr = nc.dram_tensor("statsr", [1, 2 * L], F32)
        self.poutb = nc.dram_tensor("poutb", [4, 1024, TPC], F32)
        self.prsout = nc.dram_tensor("prsout", [1024, TPC], F32)


def emit_ln(pb, sb, src_ap, dst_ap, dim=D, nch=NCH):
    sq0 = sb.BC.f32()[:, 0:2048]
    sqs = [sq0, sq0]
    NT = TPC // SUP

    mmq_bump = [None] * nch

    for k in range(nch):
        def s(scalar, k=k):
            if k == 0:
                pb.wait(scalar, "V")
                pb.wait(scalar, "T")
            if k >= 1 and mmq_bump[k - 1] is not None:
                pb.wait(scalar, "T", mmq_bump[k - 1])
            i = scalar.activation(sqs[k % 2], src_ap[:, TPC * k:TPC * (k + 1)], AF.Square)
            pb.bump(i, "S")
        pb.on("scalar", s)

        def t(tensor, k=k):
            pb.wait(tensor, "S")
            for tt in range(NT):
                tensor.matmul(pb.PSA[0:1, SUP * tt:SUP * (tt + 1)], sb.onescol.ap(),
                              src_ap[:, TPC * k + SUP * tt: TPC * k + SUP * (tt + 1)],
                              start=(k == 0), stop=(k == nch - 1))
                i = tensor.matmul(pb.PSB[0:1, SUP * tt:SUP * (tt + 1)], sb.onescol.ap(),
                                  sqs[k % 2][:, SUP * tt:SUP * (tt + 1)],
                                  start=(k == 0), stop=(k == nch - 1))
            mmq_bump[k] = pb.bump(i, "T")
        pb.on("tensor", t)

    rA = sb.rows.ap()[:, 0:TPC]
    rB = sb.vecs.ap()[0:1, 0:TPC]
    rC = sb.rows2.ap()

    def v(vector):
        pb.wait(vector, "T")
        vector.tensor_copy(rA, pb.PSA[0:1, 0:TPC])
        vector.tensor_copy(rB, pb.PSB[0:1, 0:TPC])
        vector.tensor_scalar_mul(rA, rA, 1.0 / dim)
        vector.tensor_scalar_mul(rB, rB, 1.0 / dim)
        vector.tensor_mul(rC, rA, rA)
        i = vector.tensor_sub(rB, rB, rC)
        pb.bump(i, "V")
    pb.on("vector", v)

    def s2(scalar):
        pb.wait(scalar, "V")
        scalar.activation(rB, rB, AF.Ln, bias=sb.cv(1e-5, rB.shape[0]))
        i = scalar.activation(rB, rB, AF.Exp, scale=sb.cv(-0.5, rB.shape[0]))
        pb.bump(i, "S")
    pb.on("scalar", s2)

    def v2(vector):
        pb.wait(vector, "S")
        i = vector.tensor_mul(rC, rA, rB)
        pb.bump(i, "V")
    pb.on("vector", v2)

    for tt in range(NT):
        def t2(tensor, tt=tt):
            pb.wait(tensor, "V")
            tensor.matmul(pb.PSA[:, SUP * tt:SUP * (tt + 1)], sb.onesr.ap()[:, 0:128],
                          rB[:, SUP * tt:SUP * (tt + 1)], start=True, stop=True)
            i = tensor.matmul(pb.PSB[:, SUP * tt:SUP * (tt + 1)], sb.onesr.ap()[:, 0:128],
                              rC[:, SUP * tt:SUP * (tt + 1)], start=True, stop=True)
            pb.bump(i, "T")
        pb.on("tensor", t2)

    t512a = sb.BC.f32()[:, 0:512]
    t512b = sb.BC.f32()[:, 512:1024]

    def v3(vector):
        pb.wait(vector, "T")
        for k in range(nch):
            for tt in range(NT):
                sl = slice(TPC * k + SUP * tt, TPC * k + SUP * (tt + 1))
                tb = t512a if (k % 2 == 0) else t512b
                vector.tensor_mul(tb, src_ap[:, sl], pb.PSA[:, SUP * tt:SUP * (tt + 1)])
                i = vector.tensor_sub(dst_ap[:, sl], tb, pb.PSB[:, SUP * tt:SUP * (tt + 1)])
        pb.bump(i, "V")
    pb.on("vector", v3)


def emit_ffn0(pb, sb, lnsrc_bf16, beff):
    NT = TPC // SUP
    vlastf = [None] * 4
    for tt in range(NT):
        for j in range(NCH):
            bank = (tt * NCH + j) % 4

            def t(tensor, tt=tt, j=j, bank=bank):
                if tt == 0 and j == 0:
                    pb.wait(tensor, "ccweff", 1)
                    pb.wait(tensor, "LD")
                    pb.wait(tensor, "V")
                if vlastf[bank] is not None:
                    pb.wait(tensor, "V", vlastf[bank])
                ps = pb.PSA[:, SUP * bank:SUP * (bank + 1)]
                for k in range(NCH):
                    tensor.matmul(ps, sb.WBUF.ap()[:, 1024 * k + 128 * j:1024 * k + 128 * (j + 1)],
                                  lnsrc_bf16[:, TPC * k + SUP * tt:TPC * k + SUP * (tt + 1)],
                                  start=(k == 0), stop=False)
                i = tensor.matmul(ps, beff[:, 128 * j:128 * (j + 1)],
                                  sb.onesr.ap(), start=False, stop=True)
                pb.bump(i, "T")
            pb.on("tensor", t)

            def v(vector, tt=tt, j=j, bank=bank):
                pb.wait(vector, "T")
                ps = pb.PSA[:, SUP * bank:SUP * (bank + 1)]
                xsl = sb.xs.ap()[:, TPC * j + SUP * tt:TPC * j + SUP * (tt + 1)]
                i = vector.scalar_tensor_tensor(xsl, ps, 0.5, xsl, ALU.mult, ALU.add)
                vlastf[bank] = pb.bump(i, "V")
            pb.on("vector", v)


def load_w_bf16(pb, sb, dram_ap, KK, C, dst_cols_off):
    src3 = dram_ap.rearrange("(k p) c -> p k c", p=128)
    kq = max(1, 3584 // C)
    k0 = 0
    while k0 < KK:
        kn = min(kq, KK - k0)
        n = kn * C
        st = sb.BC.f32()[:, 0:n]

        def f(sync, st=st, k0=k0, kn=kn, n=n):
            pb.wait(sync, "V")
            i = sync.dma_start(out=st.rearrange("p (k c) -> p k c", c=C),
                               in_=src3[:, k0:k0 + kn, :])
            pb.bump(i, "LD", 16)
        pb.on("sync", f)

        def v(vector, st=st, k0=k0, n=n):
            pb.wait(vector, "LD")
            off = dst_cols_off + k0 * C
            i = vector.tensor_copy(sb.WBUF.ap()[:, off:off + n], st)
            pb.bump(i, "V")
        pb.on("vector", v)
        k0 += kn


def load_brow(pb, sb, src, n):
    dst = sb.pool_row(n)

    def f(sync):
        i = sync.dma_start(out=dst, in_=src)
        pb.bump(i, "LD", 16)
    pb.on("sync", f)
    return dst


def proj_mm(pb, sb, dst_bf16_cols, rhs_bf16, rhs_nch, wb_off, ncol_chunks,
            brow_ap, act, extra_ps=None):
    NT = TPC // SUP
    vlast = [None] * 4
    for j in range(ncol_chunks):
        for tt in range(NT):
            bank = (j * NT + tt) % 4

            def t(tensor, j=j, tt=tt, bank=bank):
                if vlast[bank] is not None:
                    pb.wait(tensor, "S", vlast[bank])
                else:
                    pb.wait(tensor, "S")
                    pb.wait(tensor, "V")
                ps = pb.PSA[:, SUP * bank:SUP * (bank + 1)]
                for k in range(rhs_nch):
                    tensor.matmul(ps, sb.WBUF.ap()[:, wb_off + (k * ncol_chunks + j) * 128:
                                                   wb_off + (k * ncol_chunks + j) * 128 + 128],
                                  rhs_bf16[:, TPC * k + SUP * tt:TPC * k + SUP * (tt + 1)],
                                  start=(k == 0), stop=False)
                i = tensor.matmul(ps, brow_ap[:, 128 * j:128 * (j + 1)],
                                  sb.onesr.ap(), start=False, stop=True)
                pb.bump(i, "T")
            pb.on("tensor", t)

            def s(scalar, j=j, tt=tt, bank=bank):
                pb.wait(scalar, "T")
                ps = pb.PSA[:, SUP * bank:SUP * (bank + 1)]
                dst = dst_bf16_cols[:, TPC * j + SUP * tt:TPC * j + SUP * (tt + 1)]
                if act is None:
                    i = scalar.copy(dst, ps)
                else:
                    i = scalar.activation(dst, ps, act)
                vlast[bank] = pb.bump(i, "S")
            pb.on("scalar", s)


def emit_rot_weights(pb, sb, src_off, dst_off, nk=8):
    def v(vector):
        pb.wait(vector, "LD")
        for k in range(nk):
            s0 = sb.WBUF.ap()[:, src_off + 128 * k:src_off + 128 * k + 64]
            s1 = sb.WBUF.ap()[:, src_off + 128 * k + 64:src_off + 128 * k + 128]
            d0 = sb.WBUF.ap()[:, dst_off + 128 * k:dst_off + 128 * k + 64]
            d1 = sb.WBUF.ap()[:, dst_off + 128 * k + 64:dst_off + 128 * k + 128]
            vector.tensor_scalar(d0, s1, -1.0, None, ALU.mult)
            i = vector.tensor_copy(d1, s0)
        pb.bump(i, "V")
    pb.on("vector", v)


def emit_rot_brow(pb, sb, brow):
    out = sb.pool_row(128)

    def v(vector):
        pb.wait(vector, "LD")
        vector.tensor_scalar(out[:, 0:64], brow[:, 64:128], -1.0, None, ALU.mult)
        i = vector.tensor_copy(out[:, 64:128], brow[:, 0:64])
        pb.bump(i, "V")
    pb.on("vector", v)
    return out


def emit_rope_chunk(pb, sb, dst_bf16, wb_off, wb_rot_off, brow_ap, brow_rot_ap,
                    lnsrc, cosT, sinT):
    NT = TPC // SUP
    t1 = sb.vec_carve(0, 512)
    t2s = sb.vec_carve(512, 512)
    for tt in range(NT):
        def t(tensor, tt=tt):
            pb.wait(tensor, "V")
            for k in range(NCH):
                tensor.matmul(pb.PSA[:, 0:512],
                              sb.WBUF.ap()[:, wb_off + k * 128:wb_off + k * 128 + 128],
                              lnsrc[:, TPC * k + SUP * tt:TPC * k + SUP * (tt + 1)],
                              start=(k == 0), stop=False)
            tensor.matmul(pb.PSA[:, 0:512], brow_ap[:, 0:128], sb.onesr.ap(),
                          start=False, stop=True)
            for k in range(NCH):
                tensor.matmul(pb.PSA[:, 512:1024],
                              sb.WBUF.ap()[:, wb_rot_off + k * 128:wb_rot_off + k * 128 + 128],
                              lnsrc[:, TPC * k + SUP * tt:TPC * k + SUP * (tt + 1)],
                              start=(k == 0), stop=False)
            i = tensor.matmul(pb.PSA[:, 512:1024], brow_rot_ap[:, 0:128], sb.onesr.ap(),
                              start=False, stop=True)
            pb.bump(i, "T")
        pb.on("tensor", t)

        def v(vector, tt=tt):
            pb.wait(vector, "T")
            pb.wait(vector, "S")
            cs = slice(SUP * tt, SUP * (tt + 1))
            vector.tensor_mul(t1, pb.PSA[:, 0:512], cosT[:, cs])
            vector.tensor_mul(t2s, pb.PSA[:, 512:1024], sinT[:, cs])
            i = vector.tensor_add(t1, t1, t2s)
            pb.bump(i, "V")
        pb.on("vector", v)

        def s2(scalar, tt=tt):
            pb.wait(scalar, "V")
            i = scalar.activation(dst_bf16[:, SUP * tt:SUP * (tt + 1)], t1, AF.Gelu)
            pb.bump(i, "S")
        pb.on("scalar", s2)
        if DBG_ROPE_BARRIER:
            pb.barrier()


def emit_mla(pb, io, dr, sb, stop=99):
    nc = pb.nc
    NT = TPC // SUP
    lnb = sb.BA.bf16()
    cq = sb.BD.bf16()[:, 0:4096]
    ckv = sb.BD.bf16()[:, 4096:8192]
    qT = sb.BB.bf16()

    emit_ln(pb, sb, sb.xs.ap(), lnb)
    pb.barrier()
    sb.row_reset()

    load_w_bf16(pb, sb, io["wq"], 8, 256, 0)
    load_w_bf16(pb, sb, io["wkv"], 8, 256, 2048)
    load_w_bf16(pb, sb, io["wqr"], 8, 128, 4096)
    load_w_bf16(pb, sb, io["wkr"], 8, 128, 5120)
    bq = load_brow(pb, sb, io["bq"], 256)
    bkv = load_brow(pb, sb, io["bkv"], 256)
    bqr = load_brow(pb, sb, io["bqr"], 128)
    bkr = load_brow(pb, sb, io["bkr"], 128)
    cosT = sb.BB.f32()[:, 0:TPC]
    sinT = sb.BB.f32()[:, TPC:2 * TPC]
    emit_rot_weights(pb, sb, 4096, 6144)
    emit_rot_weights(pb, sb, 5120, 7168)

    def fcs(sync):
        i = sync.dma_start(out=cosT, in_=io["cosT"])
        pb.bump(i, "LD", 16)
        i = sync.dma_start(out=sinT, in_=io["sinT"])
        pb.bump(i, "LD", 16)
    pb.on("sync", fcs)
    pb.barrier()
    if stop <= 1:
        return
    proj_mm(pb, sb, cq, lnb, NCH, 0, 2, bq, AF.Gelu)
    proj_mm(pb, sb, ckv, lnb, NCH, 2048, 2, bkv, AF.Gelu)
    if stop <= 2:
        return
    bqr_rot = emit_rot_brow(pb, sb, bqr)
    bkr_rot = emit_rot_brow(pb, sb, bkr)
    pb.barrier()
    emit_rope_chunk(pb, sb, qT[:, 7 * TPC:8 * TPC], 4096, 6144, bqr, bqr_rot, lnb, cosT, sinT)
    k7 = sb.BC.bf16()[:, 0:TPC]
    emit_rope_chunk(pb, sb, k7, 5120, 7168, bkr, bkr_rot, lnb, cosT, sinT)
    pb.barrier()
    sb.row_reset()

    def g(gp):
        pb.wait(gp, "S")
        i = gp.dma_start(out=dr.ks.ap()[128 * 7:128 * 8, :], in_=k7)
        pb.bump(i, "ST", 16)
    pb.on("gpsimd", g)
    pb.barrier()

    if stop <= 3:
        return
    if stop <= 35:
        return
    load_w_bf16(pb, sb, io["wqc"], 2, 896, 0)
    load_w_bf16(pb, sb, io["wkc"], 2, 896, 1792)
    load_w_bf16(pb, sb, io["wv"], 2, 1024, 3584)
    bqc = load_brow(pb, sb, io["bqc"], 896)
    bkc = load_brow(pb, sb, io["bkc"], 896)
    bv = sb.vecs.ap()[0:1, 0:1024]

    def fbv(sync):
        pb.wait(sync, "S")
        i = sync.dma_start(out=bv, in_=io["bv"])
        pb.bump(i, "LD", 16)
    pb.on("sync", fbv)
    pb.barrier()
    if stop <= 31:
        return
    proj_mm(pb, sb, qT, cq, 2, 0, 7, bqc, AF.Gelu)
    if stop <= 32:
        return
    ksc = sb.BC.bf16()[:, 0:TPC]
    vlast = [None]
    for j in range(7):
        for tt in range(NT):
            def t(tensor, j=j, tt=tt):
                pb.wait(tensor, "S")
                if vlast[0] is not None:
                    pb.wait(tensor, "ST", vlast[0])
                ps = pb.PSA[:, 1024:1536]
                for k in range(2):
                    tensor.matmul(ps, sb.WBUF.ap()[:, 1792 + (k * 7 + j) * 128:1792 + (k * 7 + j) * 128 + 128],
                                  ckv[:, TPC * k + SUP * tt:TPC * k + SUP * (tt + 1)],
                                  start=(k == 0), stop=False)
                i = tensor.matmul(ps, bkc[:, 128 * j:128 * (j + 1)], sb.onesr.ap(),
                                  start=False, stop=True)
                pb.bump(i, "T")
            pb.on("tensor", t)

            def s(scalar, j=j, tt=tt):
                pb.wait(scalar, "T")
                i = scalar.activation(ksc[:, SUP * tt:SUP * (tt + 1)], pb.PSA[:, 1024:1536], AF.Gelu)
                pb.bump(i, "S")
            pb.on("scalar", s)

        def g2(gp, j=j):
            pb.wait(gp, "S")
            i = gp.dma_start(out=dr.ks.ap()[128 * j:128 * (j + 1), :], in_=ksc)
            vlast[0] = pb.bump(i, "ST", 16)
        pb.on("gpsimd", g2)

    if stop <= 4:
        return
    vsc = sb.BC.bf16()[:, 2048:2560]
    stlast = [None]
    for tc in range(TPC // 128):
        for dv in range(2):
            def t(tensor, tc=tc, dv=dv):
                pb.wait(tensor, "S")
                if stlast[0] is not None:
                    pb.wait(tensor, "ST", stlast[0])
                ps = pb.PSB[:, 0:512]
                for k in range(2):
                    tensor.matmul(ps, ckv[:, TPC * k + 128 * tc:TPC * k + 128 * (tc + 1)],
                                  sb.WBUF.ap()[:, 3584 + k * 1024 + 512 * dv:3584 + k * 1024 + 512 * (dv + 1)],
                                  start=(k == 0), stop=False)
                i = tensor.matmul(ps, sb.onesr.ap()[:, 0:128],
                                  bv[:, 512 * dv:512 * (dv + 1)], start=False, stop=True)
                pb.bump(i, "T")
            pb.on("tensor", t)

            def s(scalar, tc=tc, dv=dv):
                pb.wait(scalar, "T")
                i = scalar.copy(vsc, pb.PSB[:, 0:512])
                pb.bump(i, "S")
            pb.on("scalar", s)

            def g3(gp, tc=tc, dv=dv):
                pb.wait(gp, "S")
                i = gp.dma_start(out=dr.vs.ap()[128 * tc:128 * (tc + 1), 512 * dv:512 * (dv + 1)],
                                 in_=vsc)
                stlast[0] = pb.bump(i, "ST", 16)
            pb.on("gpsimd", g3)

    if stop <= 5:
        return
    def g4(gp):
        pb.wait(gp, "ST")
        i = gp.collective_compute("AllGather", ALU.bypass, replica_groups=dr.grp,
                                  ins=[dr.ks.ap()], outs=[dr.kgat.ap()])
        pb.bump(i, "cckg", 1)
        i = gp.collective_compute("AllGather", ALU.bypass, replica_groups=dr.grp,
                                  ins=[dr.vs.ap()], outs=[dr.vgat.ap()])
        pb.bump(i, "ccvg", 1)
    pb.on("gpsimd", g4)
    pb.barrier()
    sb.row_reset()


def emit_attention(pb, io, dr, sb):
    NT = TPC // SUP
    ostack = sb.BA.bf16()
    qT = sb.BB.bf16()
    vth = sb.BC.bf16()[:, 0:4160]
    pT = sb.BC.bf16()[:, 4160:6208]
    obuf = sb.vecs.ap().bitcast(BF16)[0:64, 3204:3716]
    rlb = sb.vec_carve(512, 512)
    negmv = sb.vecs.ap()[:, 1024:1025]
    rr = sb.vecs.ap()[64:65, 1026:1538]
    ones64 = sb.vecs.ap()[64:65, 1538:1602]

    def vv(vector):
        pb.wait(vector, "ST")
        pb.wait(vector, "T")
        pb.wait(vector, "S")
        vector.memset(ones64, 1.0)
        vector.memset(sb.BC.bf16()[:, 6720:6980].rearrange("p (j c) -> p j c", c=65)[:, :, 64:65], 1.0)
        i = vector.memset(vth.rearrange("p (j c) -> p j c", c=65)[:, :, 64:65], 1.0)
        pb.bump(i, "V")
    pb.on("vector", vv)

    kdg = sb.BC.bf16()
    vdg = sb.BC.bf16()[:, 6720:6980]
    for h in range(H):
        ch, hh = h // 2, h % 2
        kc = sb.BD.bf16()[64 * hh:64 * hh + 64, 0:8192]
        kdiag = kdg[64 * hh:64 * hh + 64, 6208:6720]
        nsc = sb.vec_carve(0, 512)[64 * hh:64 * hh + 64, :]
        olast = [None]

        def ldh(sync, h=h, ch=ch, hh=hh, kc=kc):
            if h == 0:
                pb.wait(sync, "cckg", 1)
                pb.wait(sync, "ccvg", 1)
                pb.wait(sync, "V")
            pb.wait(sync, "T")
            pb.wait(sync, "S")
            for r in range(4):
                i = sync.dma_start(out=kc[:, 2048 * r:2048 * (r + 1)],
                                   in_=dr.kgat.ap()[1024 * r + 128 * ch + 64 * hh:
                                                    1024 * r + 128 * ch + 64 * hh + 64, :])
                pb.bump(i, "LD", 16)
                vsrc = dr.vgat.ap()[TPC * r:TPC * (r + 1), 64 * h:64 * (h + 1)]
                vdst = vth.rearrange("p (j c) -> p j c", c=65)[:, 16 * r:16 * (r + 1), 0:64]
                i = sync.dma_start(out=vdst, in_=vsrc.rearrange("(j p) d -> p j d", p=128))
                pb.bump(i, "LD", 16)
        pb.on("sync", ldh)

        for m in (3, 2, 1, 0):
            nst = LMAX[m]

            def ld(sync, h=h, ch=ch, hh=hh, m=m, kdiag=kdiag):
                pb.wait(sync, "T")
                i = sync.dma_start(out=kdiag,
                                   in_=dr.ks.ap()[128 * ch + 64 * hh:128 * ch + 64 * hh + 64,
                                                  SUP * m:SUP * (m + 1)])
                pb.bump(i, "LD", 16)
                vsrc = dr.vs.ap()[SUP * m:SUP * (m + 1), 64 * h:64 * (h + 1)]
                vdst = vdg.rearrange("p (j c) -> p j c", c=65)[:, :, 0:64]
                i = sync.dma_start(out=vdst, in_=vsrc.rearrange("(j p) d -> p j d", p=128))
                pb.bump(i, "LD", 16)
            pb.on("sync", ld)

            if m == 3:
                r_rk = sb.pool_row_at(0, 512)
                r_mx = sb.pool_row_at(512, 2)
                r_sc = sb.pool_row_at(514, 6)
                for kind, ntile in (("k", 16), ("q", 4)):
                    for tile in range(ntile):
                        def s(scalar, kind=kind, tile=tile, ch=ch, hh=hh, kc=kc, nsc=nsc):
                            if kind == "k" and tile == 0:
                                pb.wait(scalar, "LD")
                            pb.wait(scalar, "T")
                            if kind == "k":
                                srcq = kc[:, SUP * tile:SUP * (tile + 1)]
                            else:
                                srcq = qT[64 * hh:64 * hh + 64,
                                          TPC * ch + SUP * tile:TPC * ch + SUP * (tile + 1)]
                            i = scalar.activation(nsc, srcq, AF.Square)
                            pb.bump(i, "S")
                        pb.on("scalar", s)

                        def t(tensor, hh=hh, nsc=nsc):
                            pb.wait(tensor, "S")
                            pb.wait(tensor, "V")
                            i = tensor.matmul(pb.PSB[0:1, 1024:1536],
                                              sb.onescol.ap()[64 * hh:64 * hh + 64, :],
                                              nsc, start=True, stop=True)
                            pb.bump(i, "T")
                        pb.on("tensor", t)

                        def v(vector, kind=kind, tile=tile, r_rk=r_rk):
                            pb.wait(vector, "T")
                            if tile == 0:
                                i = vector.tensor_copy(r_rk, pb.PSB[0:1, 1024:1536])
                            else:
                                i = vector.tensor_max(r_rk, r_rk, pb.PSB[0:1, 1024:1536])
                            pb.bump(i, "V")
                        pb.on("vector", v)

                    col = 0 if kind == "k" else 1

                    def v2(vector, col=col, r_rk=r_rk, r_mx=r_mx):
                        i = vector.reduce_max(r_mx[:, col:col + 1], r_rk, AX.X)
                        pb.bump(i, "V")
                    pb.on("vector", v2)

                def s3(scalar, r_mx=r_mx, r_sc=r_sc):
                    pb.wait(scalar, "V")
                    i = scalar.activation(r_sc[:, 0:2], r_mx, AF.Ln)
                    pb.bump(i, "S")
                pb.on("scalar", s3)

                def v3(vector, r_sc=r_sc):
                    pb.wait(vector, "S")
                    i = vector.tensor_add(r_sc[:, 2:3], r_sc[:, 0:1], r_sc[:, 1:2])
                    pb.bump(i, "V")
                pb.on("vector", v3)

                def s4(scalar, r_sc=r_sc):
                    pb.wait(scalar, "V")
                    i = scalar.activation(r_sc[:, 3:4], r_sc[:, 2:3], AF.Exp, scale=sb.cv(0.5, 1))
                    pb.bump(i, "S")
                pb.on("scalar", s4)

                def v4(vector, r_sc=r_sc):
                    pb.wait(vector, "S")
                    i = vector.tensor_scalar_mul(r_sc[:, 4:5], r_sc[:, 3:4], -0.125)
                    pb.bump(i, "V")
                pb.on("vector", v4)

                def t3(tensor, r_sc=r_sc):
                    pb.wait(tensor, "V")
                    i = tensor.matmul(pb.PSB[:, 1536:1537], sb.onesr.ap()[:, 0:128],
                                      r_sc[:, 4:5], start=True, stop=True)
                    pb.bump(i, "T")
                pb.on("tensor", t3)

                def v5(vector):
                    pb.wait(vector, "T")
                    pb.wait(vector, "S")
                    vector.tensor_copy(negmv, pb.PSB[:, 1536:1537])
                    i = vector.tensor_scalar_add(sb.btab.ap(), sb.padtab.ap(), negmv)
                    pb.bump(i, "V")
                pb.on("vector", v5)

                if DBG_NORM is not None:
                    def vdbg(gp, h=h, r_sc=r_sc, r_mx=r_mx):
                        pb.wait(gp, "V")
                        i = gp.dma_start(out=DBG_NORM[0][0:1, 8 * h:8 * h + 6], in_=r_sc)
                        pb.bump(i, "ST", 16)
                        i = gp.dma_start(out=DBG_NORM[0][0:1, 128 + 8 * h:128 + 8 * h + 2], in_=r_mx)
                        pb.bump(i, "ST", 16)
                    pb.on("gpsimd", vdbg)

            nsteps = 2 * nst
            expb = [None] * nsteps
            pvb = [None] * nsteps
            qkb = [None] * nsteps
            qrhs = qT[64 * hh:64 * hh + 64, TPC * ch + SUP * m:TPC * ch + SUP * (m + 1)]

            def emit_pv(tensor, j_s, m=m, nst=nst, nsteps=nsteps, expb=expb,
                        pvb=pvb, olast=olast):
                pb.wait(tensor, "S", expb[j_s])
                if j_s == 0 and olast[0] is not None:
                    pb.wait(tensor, "V", olast[0])
                tp, hp = j_s // 2, j_s % 2
                dg = (tp == nst - 1)
                pt = pT[:, 1024 * (j_s % 2):1024 * (j_s % 2) + 1024]
                for i2 in range(2):
                    kb = 2 * hp + i2
                    if dg:
                        vt = vdg[:, 65 * kb:65 * kb + 65]
                    else:
                        r, mi = sup_addr(tp)
                        pos = 4 * r + mi
                        vt = vth[:, 65 * (4 * pos + kb):65 * (4 * pos + kb) + 65]
                    i = tensor.matmul(pb.PSB[0:65, 0:512], vt, pt[:, 512 * i2:512 * (i2 + 1)],
                                      start=(j_s == 0 and i2 == 0),
                                      stop=(j_s == nsteps - 1 and i2 == 1))
                pvb[j_s] = pb.bump(i, "T")

            for i_s in range(nsteps):
                t_sup, half = i_s // 2, i_s % 2
                diag = (t_sup == nst - 1)
                sbank = pb.PSA[:, 1024 * (i_s % 2):1024 * (i_s % 2) + 1024]
                ptile = pT[:, 1024 * (i_s % 2):1024 * (i_s % 2) + 1024]

                def tq(tensor, i_s=i_s, t_sup=t_sup, half=half, diag=diag,
                       sbank=sbank, m=m, qrhs=qrhs, expb=expb, emit_pv=emit_pv,
                       kc=kc, kdiag=kdiag, qkb=qkb):
                    if i_s == 0:
                        pb.wait(tensor, "LD")
                        pb.wait(tensor, "V")
                    if i_s >= 2 and expb[i_s - 2] is not None:
                        pb.wait(tensor, "S", expb[i_s - 2])
                    for i2 in range(2):
                        kb = 2 * half + i2
                        if diag:
                            lhs = kdiag[:, 128 * kb:128 * (kb + 1)]
                        else:
                            r, mi = sup_addr(t_sup)
                            pos = 4 * r + mi
                            lhs = kc[:, SUP * pos + 128 * kb:SUP * pos + 128 * (kb + 1)]
                        i = tensor.matmul(sbank[:, 512 * i2:512 * (i2 + 1)], lhs, qrhs,
                                          start=True, stop=(not diag))
                        if diag:
                            i = tensor.matmul(sbank[:, 512 * i2:512 * (i2 + 1)], sb.ident.ap(),
                                              sb.masks.ap()[:, SUP * kb:SUP * (kb + 1)],
                                              start=False, stop=True)
                    qkb[i_s] = pb.bump(i, "T")
                    if i_s >= 1:
                        emit_pv(tensor, i_s - 1)
                pb.on("tensor", tq)

                def se(scalar, i_s=i_s, t_sup=t_sup, diag=diag, sbank=sbank,
                       ptile=ptile, m=m, expb=expb, qkb=qkb, pvb=pvb):
                    pb.wait(scalar, "T", qkb[i_s])
                    if i_s >= 2 and pvb[i_s - 2] is not None:
                        pb.wait(scalar, "T", pvb[i_s - 2])
                    col = bt_diag(m) if diag else bt_col(m, t_sup)
                    btab_src = sb.padtab.ap() if DBG_CONST_BIAS else sb.btab.ap()
                    i = scalar.activation(ptile, sbank, AF.Exp, scale=sb.cv(0.125),
                                          bias=btab_src[:, col:col + 1])
                    expb[i_s] = pb.bump(i, "S")
                pb.on("scalar", se)

            def tl(tensor, nsteps=nsteps, emit_pv=emit_pv):
                emit_pv(tensor, nsteps - 1)
            pb.on("tensor", tl)

            def vo(vector):
                pb.wait(vector, "T")
                vector.tensor_scalar_add(rr, pb.PSB[64:65, 0:512], 1e-30)
                i = vector.reciprocal(rr, rr)
                pb.bump(i, "V")
            pb.on("vector", vo)

            def to(tensor):
                pb.wait(tensor, "V")
                i = tensor.matmul(pb.PSB[0:64, 512:1024], ones64[:, 0:64],
                                  rr, start=True, stop=True)
                pb.bump(i, "T")
            pb.on("tensor", to)

            def vo2(vector, m=m, ch=ch, hh=hh, olast=olast):
                pb.wait(vector, "T")
                vector.tensor_copy(rlb[0:64, :], pb.PSB[0:64, 512:1024])
                if hh == 0:
                    dst = ostack[0:64, TPC * ch + SUP * m:TPC * ch + SUP * (m + 1)]
                else:
                    dst = obuf[0:64, :]
                i = vector.tensor_mul(dst, pb.PSB[0:64, 0:512], rlb[0:64, :])
                olast[0] = pb.bump(i, "V")
            pb.on("vector", vo2)

            if hh == 1:
                def gmove(gp, m=m, ch=ch):
                    pb.wait(gp, "V")
                    i = gp.dma_start(
                        out=ostack[64:128, TPC * ch + SUP * m:TPC * ch + SUP * (m + 1)],
                        in_=obuf[0:64, :])
                    pb.bump(i, "ST", 16)
                pb.on("gpsimd", gmove)

                def vguard3(vector):
                    pb.wait(vector, "ST")
                pb.on("vector", vguard3)

    pb.barrier()
    pb.barrier()
    load_w_bf16(pb, sb, io["wo"], 8, 1024, 0)
    bo = load_brow(pb, sb, io["bo"], 1024)
    pb.barrier()
    vlast = [None] * 4
    for tt in range(NT):
        for j in range(NCH):
            bank = (tt * NCH + j) % 4

            def t(tensor, tt=tt, j=j, bank=bank):
                if vlast[bank] is not None:
                    pb.wait(tensor, "V", vlast[bank])
                ps = pb.PSA[:, SUP * bank:SUP * (bank + 1)]
                for k in range(NCH):
                    tensor.matmul(ps, sb.WBUF.ap()[:, 1024 * k + 128 * j:1024 * k + 128 * (j + 1)],
                                  ostack[:, TPC * k + SUP * tt:TPC * k + SUP * (tt + 1)],
                                  start=(k == 0), stop=False)
                i = tensor.matmul(ps, bo[:, 128 * j:128 * (j + 1)], sb.onesr.ap(),
                                  start=False, stop=True)
                pb.bump(i, "T")
            pb.on("tensor", t)

            def v(vector, tt=tt, j=j, bank=bank):
                pb.wait(vector, "T")
                ps = pb.PSA[:, SUP * bank:SUP * (bank + 1)]
                xsl = sb.xs.ap()[:, TPC * j + SUP * tt:TPC * j + SUP * (tt + 1)]
                i = vector.tensor_add(xsl, xsl, ps)
                vlast[bank] = pb.bump(i, "V")
            pb.on("vector", v)


def emit_ffn1(pb, io, dr, sb):
    NT = TPC // SUP
    lnb = sb.BA.bf16()
    w2buf = sb.BB.bf16()[:, 0:8192]
    htile = sb.BC.bf16()[:, 0:4096]
    b2sb = sb.pv(8)

    def f0(sync):
        i = sync.dma_start(out=b2sb, in_=io["b2col"])
        pb.bump(i, "LD", 16)
    pb.on("sync", f0)

    def v0(vector):
        pb.wait(vector, "LD")
        vector.tensor_scalar_mul(b2sb, b2sb, 0.5)
        for k in range(NCH):
            i = vector.tensor_scalar_add(sb.xs.ap()[:, TPC * k:TPC * (k + 1)],
                                         sb.xs.ap()[:, TPC * k:TPC * (k + 1)],
                                         b2sb[:, k:k + 1])
        pb.bump(i, "V")
    pb.on("vector", v0)

    b1sl = sb.pool_row_at(1024, 1024)
    for ffq in range(4):
        def fw(sync, ffq=ffq):
            pb.wait(sync, "T")
            for f in range(8):
                F = 8 * ffq + f
                r, bl = F // 4, F % 4
                src = dr.w1tg.ap()[1024 * r:1024 * (r + 1), 128 * bl:128 * (bl + 1)]
                src = src.rearrange("(k p) c -> p k c", p=128)
                dst = sb.WBUF.ap().rearrange("p (k f c) -> p k f c", k=8, f=8)[:, :, f, :]
                i = sync.dma_start(out=dst, in_=src)
                pb.bump(i, "LD", 16)
                r2, bl2 = F // 4, F % 4
                src2 = dr.w2tg.ap()[512 * r2 + 128 * bl2:512 * r2 + 128 * (bl2 + 1), :]
                src2 = src2.rearrange("p (e c) -> p e c", c=128)
                dst2 = w2buf.rearrange("p (f e c) -> p f e c", f=8, e=8)[:, f, :, :]
                i = sync.dma_start(out=dst2, in_=src2)
                pb.bump(i, "LD", 16)
            i = sync.dma_start(out=b1sl, in_=io["b1row"][:, 1024 * ffq:1024 * (ffq + 1)])
            pb.bump(i, "LD", 16)
        pb.on("sync", fw)

        for tt in range(NT):
            slast = [None] * 4
            for f in range(8):
                bank = f % 4

                def t(tensor, ffq=ffq, tt=tt, f=f, bank=bank, slast=slast):
                    if ffq == 0 and tt == 0 and f == 0:
                        pb.wait(tensor, "ccw1", 1)
                        pb.wait(tensor, "ccw2", 1)
                        pb.wait(tensor, "V")
                    if f == 0 and tt == 0:
                        pb.wait(tensor, "LD")
                    if slast[bank] is not None:
                        pb.wait(tensor, "S", slast[bank])
                    ps = pb.PSA[:, SUP * bank:SUP * (bank + 1)]
                    for k in range(NCH):
                        tensor.matmul(ps, sb.WBUF.ap()[:, (k * 8 + f) * 128:(k * 8 + f) * 128 + 128],
                                      lnb[:, TPC * k + SUP * tt:TPC * k + SUP * (tt + 1)],
                                      start=(k == 0), stop=False)
                    i = tensor.matmul(ps, b1sl[:, 128 * f:128 * (f + 1)], sb.onesr.ap(),
                                      start=False, stop=True)
                    pb.bump(i, "T")
                pb.on("tensor", t)

                def s(scalar, tt=tt, f=f, bank=bank, slast=slast):
                    pb.wait(scalar, "T")
                    ps = pb.PSA[:, SUP * bank:SUP * (bank + 1)]
                    i = scalar.activation(htile[:, 512 * f:512 * (f + 1)], ps, AF.Gelu)
                    slast[bank] = pb.bump(i, "S")
                pb.on("scalar", s)

            vlast2 = [None] * 2
            for e in range(8):
                bank = e % 2

                def t2(tensor, tt=tt, e=e, bank=bank, vlast2=vlast2):
                    pb.wait(tensor, "S")
                    if vlast2[bank] is not None:
                        pb.wait(tensor, "V", vlast2[bank])
                    ps = pb.PSB[:, SUP * bank:SUP * (bank + 1)]
                    for f in range(8):
                        i = tensor.matmul(ps, w2buf[:, (f * 8 + e) * 128:(f * 8 + e) * 128 + 128],
                                          htile[:, 512 * f:512 * (f + 1)],
                                          start=(f == 0), stop=(f == 7))
                    pb.bump(i, "T")
                pb.on("tensor", t2)

                def v2(vector, tt=tt, e=e, bank=bank, vlast2=vlast2):
                    pb.wait(vector, "T")
                    ps = pb.PSB[:, SUP * bank:SUP * (bank + 1)]
                    xsl = sb.xs.ap()[:, TPC * e + SUP * tt:TPC * e + SUP * (tt + 1)]
                    i = vector.scalar_tensor_tensor(xsl, ps, 0.5, xsl, ALU.mult, ALU.add)
                    vlast2[bank] = pb.bump(i, "V")
                pb.on("vector", v2)


def emit_ssm(pb, io, dr, sb):
    nc = pb.nc
    lnb = sb.BA.bf16()

    emit_ln(pb, sb, sb.xs.ap(), lnb)

    def g0(gp):
        pb.wait(gp, "V")
        i = gp.dma_start(out=dr.ln5s.ap().rearrange("(k p) t -> p k t", p=128), in_=lnb.rearrange("p (k t) -> p k t", t=TPC))
        pb.bump(i, "ST", 16)
        i = gp.collective_compute("AllGather", ALU.bypass, replica_groups=dr.grp,
                                  ins=[dr.ln5s.ap()], outs=[dr.ln5g.ap()])
        pb.bump(i, "ccln5", 1)
    pb.on("gpsimd", g0)
    pb.barrier()
    sb.row_reset()

    prm = sb.pv(36)
    prm2 = sb.pv(8)
    xproj_sb = sb.pv(132 + 4)

    def fp(sync):
        i = sync.dma_start(out=prm[:, 0:16], in_=io["convw"])
        pb.bump(i, "LD", 16)
        for name, off in [("convb", 16), ("dtw", 20), ("dtb", 24), ("Dpar", 28), ("lng", 32)]:
            i = sync.dma_start(out=prm[:, off:off + 4], in_=io[name])
            pb.bump(i, "LD", 16)
        i = sync.dma_start(out=prm2[:, 0:4], in_=io["lnb"])
        pb.bump(i, "LD", 16)
        i = sync.dma_start(out=sb.Aneg.ap(), in_=io["Alog"])
        pb.bump(i, "LD", 16)
        i = sync.dma_start(out=xproj_sb[:, 0:132].rearrange("p (k c) -> p k c", c=33), in_=io["xprojT"].rearrange("(k p) c -> p k c", p=128))
        pb.bump(i, "LD", 16)
    pb.on("sync", fp)

    def sA(scalar):
        pb.wait(scalar, "LD")
        i = scalar.activation(sb.Aneg.ap(), sb.Aneg.ap(), AF.Exp)
        pb.bump(i, "S")
    pb.on("scalar", sA)

    def vA(vector):
        pb.wait(vector, "S")
        vector.tensor_scalar_mul(sb.Aneg.ap(), sb.Aneg.ap(), -1.0)
        i = vector.memset(sb.states.ap(), 0.0)
        pb.bump(i, "V")
    pb.on("vector", vA)

    load_w_bf16(pb, sb, io["inxbw"], 8, 512, 0)
    load_w_bf16(pb, sb, io["inzw"], 8, 512, 4096)
    pb.barrier()

    xbp = sb.BA.f32()
    xc = sb.BB.f32()
    rhs0 = sb.BD.bf16()[:, 0:4096]
    rhs1 = sb.BD.bf16()[:, 4096:8192]
    zev = sb.BC.f32()[:, 0:512]
    rhsb = [rhs0, rhs1]
    frb = {}
    for kx in range(KCH):
        mmb = {}
        sevb = {}
        for tt in range(16):
            r_tt, m_tt = sup_addr(tt)
            buf = rhsb[tt % 2]

            def fr(sync, kx=kx, tt=tt, buf=buf, r_tt=r_tt, m_tt=m_tt, mmb=mmb, frb=frb):
                if kx == 0 and tt == 0:
                    pb.wait(sync, "ccln5", 1)
                prev = mmb.get(tt - 2) if tt >= 2 else (frb.get(kx - 1) if kx >= 1 else None)
                if prev is not None:
                    pb.wait(sync, "T", prev)
                src3 = dr.ln5g.ap()[1024 * r_tt:1024 * (r_tt + 1),
                                    SUP * m_tt:SUP * (m_tt + 1)]
                i = sync.dma_start(out=buf.rearrange("p (k t) -> p k t", t=SUP),
                                   in_=src3.rearrange("(k p) t -> p k t", p=128))
                pb.bump(i, "LD", 16)
            pb.on("sync", fr)

            def tm(tensor, kx=kx, tt=tt, buf=buf, mmb=mmb, sevb=sevb, frb=frb):
                pb.wait(tensor, "LD")
                if tt >= 2 and (tt - 2) in sevb:
                    pb.wait(tensor, "S", sevb[tt - 2])
                psx = pb.PSA[:, SUP * (tt % 2):SUP * (tt % 2 + 1)]
                psz = pb.PSA[:, 1024 + SUP * (tt % 2):1024 + SUP * (tt % 2 + 1)]
                for kdm in range(NCH):
                    tensor.matmul(psx, sb.WBUF.ap()[:, (kdm * 4 + kx) * 128:(kdm * 4 + kx) * 128 + 128],
                                  buf[:, 512 * kdm:512 * (kdm + 1)], start=(kdm == 0), stop=(kdm == 7))
                    i = tensor.matmul(psz, sb.WBUF.ap()[:, 4096 + (kdm * 4 + kx) * 128:4096 + (kdm * 4 + kx) * 128 + 128],
                                      buf[:, 512 * kdm:512 * (kdm + 1)], start=(kdm == 0), stop=(kdm == 7))
                mmb[tt] = pb.bump(i, "T")
                if tt == 15:
                    frb[kx] = mmb[tt]
            pb.on("tensor", tm)

            def sev(scalar, kx=kx, tt=tt, sevb=sevb):
                pb.wait(scalar, "T")
                if tt == 0:
                    pb.wait(scalar, "ST")
                psx = pb.PSA[:, SUP * (tt % 2):SUP * (tt % 2 + 1)]
                psz = pb.PSA[:, 1024 + SUP * (tt % 2):1024 + SUP * (tt % 2 + 1)]
                scalar.copy(xbp[:, SUP * tt:SUP * (tt + 1)], psx)
                i = scalar.copy(zev, psz)
                sevb[tt] = pb.bump(i, "S")
            pb.on("scalar", sev)

            def gz(gp, kx=kx, tt=tt):
                pb.wait(gp, "S")
                i = gp.dma_start(out=dr.zst.ap()[kx, :, SUP * tt:SUP * (tt + 1)], in_=zev)
                pb.bump(i, "ST", 16)
            pb.on("gpsimd", gz)
            def sguard(scalar):
                pb.wait(scalar, "ST")
            pb.on("scalar", sguard)

        def vc(vector, kx=kx):
            pb.wait(vector, "S")
            vector.tensor_scalar(xc, xbp, prm[:, 4 * kx + 3:4 * kx + 4],
                                 prm[:, 16 + kx:17 + kx], ALU.mult, ALU.add)
            vector.scalar_tensor_tensor(xc[:, 1:L], xbp[:, 0:L - 1], prm[:, 4 * kx + 2:4 * kx + 3],
                                        xc[:, 1:L], ALU.mult, ALU.add)
            vector.scalar_tensor_tensor(xc[:, 2:L], xbp[:, 0:L - 2], prm[:, 4 * kx + 1:4 * kx + 2],
                                        xc[:, 2:L], ALU.mult, ALU.add)
            i = vector.scalar_tensor_tensor(xc[:, 3:L], xbp[:, 0:L - 3], prm[:, 4 * kx + 0:4 * kx + 1],
                                            xc[:, 3:L], ALU.mult, ALU.add)
            pb.bump(i, "V")
        pb.on("vector", vc)

        def sc(scalar, kx=kx):
            pb.wait(scalar, "V")
            i = scalar.activation(xbp, xc, AF.Silu)
            pb.bump(i, "S")
        pb.on("scalar", sc)

        def gx(gp, kx=kx):
            pb.wait(gp, "S")
            i = gp.dma_start(out=dr.xbcst.ap()[kx], in_=xbp)
            pb.bump(i, "ST", 16)
        pb.on("gpsimd", gx)

        xpe = sb.BC.f32()[0:33, 512:1024]
        for tt in range(16):
            def tp(tensor, kx=kx, tt=tt):
                pb.wait(tensor, "S")
                i = tensor.matmul(pb.PSB[0:33, 0:512], xproj_sb[:, 33 * kx:33 * (kx + 1)],
                                  xbp[:, SUP * tt:SUP * (tt + 1)], start=True, stop=True)
                pb.bump(i, "T")
            pb.on("tensor", tp)

            def sxe(scalar, kx=kx, tt=tt, xpe=xpe):
                pb.wait(scalar, "T")
                pb.wait(scalar, "ST")
                i = scalar.copy(xpe, pb.PSB[0:33, 0:512])
                pb.bump(i, "S")
            pb.on("scalar", sxe)

            def gp2(gp, kx=kx, tt=tt, xpe=xpe):
                pb.wait(gp, "S")
                i = gp.dma_start(out=dr.xpk.ap()[kx, :, SUP * tt:SUP * (tt + 1)], in_=xpe)
                pb.bump(i, "ST", 16)
            pb.on("gpsimd", gp2)

    pb.barrier()

    acc = sb.BC.f32()[0:33, 0:2048]
    ld = sb.BD.f32()[0:33, 0:2048]
    for lc in range(4):
        for kx in range(KCH):
            def fl(sync, lc=lc, kx=kx):
                pb.wait(sync, "V")
                i = sync.dma_start(out=ld, in_=dr.xpk.ap()[kx, :, TPC * lc:TPC * (lc + 1)])
                pb.bump(i, "LD", 16)
            pb.on("sync", fl)

            def va(vector, lc=lc, kx=kx):
                pb.wait(vector, "LD")
                if kx == 0:
                    i = vector.tensor_copy(acc, ld)
                else:
                    i = vector.tensor_add(acc, acc, ld)
                pb.bump(i, "V")
            pb.on("vector", va)

        def gs(gp, lc=lc):
            pb.wait(gp, "V")
            i = gp.dma_start(out=dr.xprojp.ap()[:, TPC * lc:TPC * (lc + 1)], in_=acc)
            pb.bump(i, "ST", 16)
        pb.on("gpsimd", gs)

        def vguard(vector):
            pb.wait(vector, "ST")
        pb.on("vector", vguard)

    def gar(gp):
        pb.wait(gp, "ST")
        i = gp.collective_compute("AllReduce", ALU.add, replica_groups=dr.grp,
                                  ins=[dr.xprojp.ap()], outs=[dr.xprojr.ap()])
        pb.bump(i, "ccxp", 1)
    pb.on("gpsimd", gar)
    pb.barrier()

    dtinrow = sb.pool_row_at(0, 2048)
    ubuf = sb.BB.f32()[:, 0:2048]
    rbuf = sb.BB.f32()[:, 2048:4096]
    abuf = sb.BA.f32()[:, 0:2048]
    for lc in range(4):
        def fd(sync, lc=lc):
            pb.wait(sync, "ccxp", 1)
            pb.wait(sync, "ST")
            pb.wait(sync, "V")
            i = sync.dma_start(out=dtinrow, in_=dr.xprojr.ap()[0:1, TPC * lc:TPC * (lc + 1)])
            pb.bump(i, "LD", 16)
        pb.on("sync", fd)

        def td(tensor, lc=lc):
            pb.wait(tensor, "LD")
            pb.wait(tensor, "V")
            for q in range(4):
                i = tensor.matmul(pb.PSA[:, 512 * q:512 * (q + 1)], sb.onesr.ap()[:, 0:128],
                                  dtinrow[:, 512 * q:512 * (q + 1)], start=True, stop=True)
            pb.bump(i, "T")
        pb.on("tensor", td)

        for kx in range(KCH):
            def vu(vector, lc=lc, kx=kx):
                pb.wait(vector, "T")
                pb.wait(vector, "ST")
                i = vector.tensor_scalar(ubuf, pb.PSA[:, 0:2048], prm[:, 20 + kx:21 + kx],
                                         prm[:, 24 + kx:25 + kx], ALU.mult, ALU.add)
                pb.bump(i, "V")
            pb.on("vector", vu)

            def vd0(vector, lc=lc, kx=kx):
                i = vector.tensor_scalar_max(rbuf, ubuf, 0.0)
                pb.bump(i, "V")
            pb.on("vector", vd0)

            def sd(scalar, lc=lc, kx=kx):
                pb.wait(scalar, "V")
                scalar.activation(abuf, ubuf, AF.Abs)
                scalar.activation(abuf, abuf, AF.Exp, scale=sb.cv(-1.0))
                i = scalar.activation(abuf, abuf, AF.Ln, bias=sb.cv(1.0))
                pb.bump(i, "S")
            pb.on("scalar", sd)

            def vd(vector, lc=lc, kx=kx):
                pb.wait(vector, "S")
                i = vector.tensor_add(ubuf, rbuf, abuf)
                pb.bump(i, "V")
            pb.on("vector", vd)

            def gd(gp, lc=lc, kx=kx):
                pb.wait(gp, "V")
                i = gp.dma_start(out=dr.dtst.ap()[kx, :, TPC * lc:TPC * (lc + 1)], in_=ubuf)
                pb.bump(i, "ST", 16)
            pb.on("gpsimd", gd)
    pb.barrier()

    LC = 1024
    xbct = [sb.BA.f32()[:, LC * k:LC * (k + 1)] for k in range(4)]
    dtt = [sb.BA.f32()[:, 4096 + LC * k:4096 + LC * (k + 1)] for k in range(4)]
    wt = [sb.BB.f32()[:, LC * k:LC * (k + 1)] for k in range(4)]
    yt = [sb.BB.f32()[:, 4096 + LC * k:4096 + LC * (k + 1)] for k in range(4)]
    dA0 = sb.BC.f32()[:, 0:1024]
    dA1 = sb.BC.f32()[:, 1024:2048]
    dAb = [dA0, dA1]
    xst = sb.BC.f32()[:, 2048:3072]
    ht = sb.BD.f32()[:, 2048:3072]
    hctb = [sb.BD.f32()[:, 0:1024], sb.BD.f32()[:, 3072:4096]]
    sqt = sb.BD.f32()[:, 1024:2048]
    rowB = sb.pool_row_at(0, 1024)
    rowC = sb.pool_row_at(1024, 1024)

    for lc in range(8):
        def fl2(sync, lc=lc):
            pb.wait(sync, "T")
            pb.wait(sync, "V")
            pb.wait(sync, "G2")
            pb.wait(sync, "ST")
            for k in range(4):
                i = sync.dma_start(out=xbct[k], in_=dr.xbcst.ap()[k, :, LC * lc:LC * (lc + 1)])
                pb.bump(i, "LD", 16)
                i = sync.dma_start(out=dtt[k], in_=dr.dtst.ap()[k, :, LC * lc:LC * (lc + 1)])
                pb.bump(i, "LD", 16)
        pb.on("sync", fl2)

        def vw(vector, lc=lc):
            pb.wait(vector, "LD")
            for k in range(4):
                vector.tensor_mul(wt[k], xbct[k], dtt[k])
                i = vector.tensor_scalar(yt[k], xbct[k], prm[:, 28 + k:29 + k], None, ALU.mult)
            pb.bump(i, "V")
        pb.on("vector", vw)

        for n in range(DST):
            def fr2(sync, lc=lc, n=n):
                pb.wait(sync, "T")
                i = sync.dma_start(out=rowB, in_=dr.xprojr.ap()[1 + n:2 + n, LC * lc:LC * (lc + 1)])
                pb.bump(i, "LD", 16)
                i = sync.dma_start(out=rowC, in_=dr.xprojr.ap()[17 + n:18 + n, LC * lc:LC * (lc + 1)])
                pb.bump(i, "LD", 16)
            pb.on("sync", fr2)

            def tb(tensor, lc=lc, n=n):
                pb.wait(tensor, "LD")
                pb.wait(tensor, "V")
                tensor.matmul(pb.PSA[:, 0:512], sb.onesr.ap()[:, 0:128], rowB[:, 0:512],
                              start=True, stop=True)
                tensor.matmul(pb.PSA[:, 512:1024], sb.onesr.ap()[:, 0:128], rowB[:, 512:1024],
                              start=True, stop=True)
                tensor.matmul(pb.PSA[:, 1024:1536], sb.onesr.ap()[:, 0:128], rowC[:, 0:512],
                              start=True, stop=True)
                i = tensor.matmul(pb.PSA[:, 1536:2048], sb.onesr.ap()[:, 0:128], rowC[:, 512:1024],
                                  start=True, stop=True)
                pb.bump(i, "T")
            pb.on("tensor", tb)

            vsb = [None] * 4
            for k in range(4):
                col = DST * k + n

                def sda(scalar, lc=lc, n=n, k=k, col=col, vsb=vsb):
                    if k == 0:
                        pb.wait(scalar, "LD")
                    if k >= 2 and vsb[k - 2] is not None:
                        pb.wait(scalar, "V", vsb[k - 2])
                    elif k < 2:
                        pb.wait(scalar, "V")
                    i = scalar.activation(dAb[k % 2], dtt[k], AF.Exp,
                                          scale=sb.Aneg.ap()[:, col:col + 1])
                    pb.bump(i, "S")
                pb.on("scalar", sda)

                def vs(vector, lc=lc, n=n, k=k, col=col, vsb=vsb):
                    pb.wait(vector, "T")
                    pb.wait(vector, "S")
                    if pb.cnt.get("G2", 0) >= 2:
                        pb.wait(vector, "G2", pb.cnt["G2"] - 1)
                    vector.tensor_mul(xst, wt[k], pb.PSA[:, 0:1024])
                    vector.tensor_tensor_scan(ht, dAb[k % 2], xst,
                                              sb.states.ap()[:, col:col + 1],
                                              ALU.mult, ALU.add)
                    vector.tensor_copy(sb.states.ap()[:, col:col + 1], ht[:, LC - 1:LC])
                    i = vector.tensor_mul(hctb[k % 2], ht, pb.PSA[:, 1024:2048])
                    vsb[k] = pb.bump(i, "V")
                pb.on("vector", vs)

                def gadd(gp, lc=lc, n=n, k=k, vsb=vsb):
                    pb.wait(gp, "V", vsb[k])
                    i = gp.tensor_add(yt[k], yt[k], hctb[k % 2])
                    pb.bump(i, "G2")
                pb.on("gpsimd", gadd)

        for k in range(4):
            def sy(scalar, lc=lc, k=k):
                pb.wait(scalar, "V")
                pb.wait(scalar, "T")
                pb.wait(scalar, "G2")
                i = scalar.activation(sqt, yt[k], AF.Square)
                pb.bump(i, "S")
            pb.on("scalar", sy)

            def ty(tensor, lc=lc, k=k):
                pb.wait(tensor, "S")
                for q in range(2):
                    tensor.matmul(pb.PSB[0:1, 512 * q:512 * (q + 1)], sb.onescol.ap(),
                                  yt[k][:, 512 * q:512 * (q + 1)], start=(k == 0), stop=(k == 3))
                    i = tensor.matmul(pb.PSB[0:1, 1024 + 512 * q:1024 + 512 * (q + 1)], sb.onescol.ap(),
                                      sqt[:, 512 * q:512 * (q + 1)], start=(k == 0), stop=(k == 3))
                pb.bump(i, "T")
            pb.on("tensor", ty)

            def gy(gp, lc=lc, k=k):
                pb.wait(gp, "V")
                i = gp.dma_start(out=dr.yst.ap()[k, :, LC * lc:LC * (lc + 1)], in_=yt[k])
                pb.bump(i, "ST", 16)
            pb.on("gpsimd", gy)

        statsrow = sb.vecs.ap()[0:1, 0:2048]

        def vst(vector, lc=lc):
            pb.wait(vector, "T")
            pb.wait(vector, "ST")
            vector.tensor_copy(statsrow[:, 0:1024], pb.PSB[0:1, 0:1024])
            i = vector.tensor_copy(statsrow[:, 1024:2048], pb.PSB[0:1, 1024:2048])
            pb.bump(i, "V")
        pb.on("vector", vst)

        def gst(gp, lc=lc, statsrow=statsrow):
            pb.wait(gp, "V")
            i = gp.dma_start(out=dr.statsp.ap()[0:1, LC * lc:LC * (lc + 1)], in_=statsrow[:, 0:1024])
            pb.bump(i, "ST", 16)
            i = gp.dma_start(out=dr.statsp.ap()[0:1, L + LC * lc:L + LC * (lc + 1)],
                             in_=statsrow[:, 1024:2048])
            pb.bump(i, "ST", 16)
        pb.on("gpsimd", gst)

        def tguard2(tensor):
            pb.wait(tensor, "V")
        pb.on("tensor", tguard2)

    def gar2(gp):
        pb.wait(gp, "ST")
        i = gp.collective_compute("AllReduce", ALU.add, replica_groups=dr.grp,
                                  ins=[dr.statsp.ap()], outs=[dr.statsr.ap()])
        pb.bump(i, "ccst", 1)
    pb.on("gpsimd", gar2)
    pb.barrier()
    sb.row_reset()

    load_w_bf16(pb, sb, io["outwT"], 4, 1024, 0)
    pb.barrier()

    rsum = sb.pool_row_at(0, 2048)
    yti = [sb.BA.f32()[:, 512 * k:512 * (k + 1)] for k in range(4)]
    zti = [sb.BA.f32()[:, 2048 + 512 * k:2048 + 512 * (k + 1)] for k in range(4)]
    yzb = [sb.BB.bf16()[:, 512 * k:512 * (k + 1)] for k in range(4)]
    szt = sb.BA.f32()[:, 4096:4608]
    ynt = sb.BA.f32()[:, 4608:5120]
    oev = sb.BA.f32()[:, 5120:5632]
    mrow = sb.vecs.ap()[0:1, 0:2048]
    vrow = sb.rows2.ap()

    for lc in range(4):
        def fs(sync, lc=lc):
            pb.wait(sync, "ccst", 1)
            pb.wait(sync, "V")
            pb.wait(sync, "T")
            i = sync.dma_start(out=rsum, in_=dr.statsr.ap()[0:1, TPC * lc:TPC * (lc + 1)])
            pb.bump(i, "LD", 16)
            i = sync.dma_start(out=vrow, in_=dr.statsr.ap()[0:1, L + TPC * lc:L + TPC * (lc + 1)])
            pb.bump(i, "LD", 16)
        pb.on("sync", fs)

        def vr(vector, lc=lc):
            pb.wait(vector, "LD")
            vector.tensor_scalar_mul(mrow, rsum, 1.0 / DIN)
            vector.tensor_scalar_mul(vrow, vrow, 1.0 / DIN)
            vector.tensor_mul(rsum, mrow, mrow)
            i = vector.tensor_sub(vrow, vrow, rsum)
            pb.bump(i, "V")
        pb.on("vector", vr)

        def sr(scalar, lc=lc):
            pb.wait(scalar, "V")
            scalar.activation(vrow, vrow, AF.Ln, bias=sb.cv(1e-5, 1))
            i = scalar.activation(vrow, vrow, AF.Exp, scale=sb.cv(-0.5, 1))
            pb.bump(i, "S")
        pb.on("scalar", sr)

        for tile in range(4):
            s_glob = 4 * lc + tile
            r_s, m_s = sup_addr(s_glob)

            def tbm(tensor, lc=lc, tile=tile):
                pb.wait(tensor, "S")
                pb.wait(tensor, "V")
                tensor.matmul(pb.PSA[:, 0:512], sb.onesr.ap()[:, 0:128],
                              mrow[:, 512 * tile:512 * (tile + 1)], start=True, stop=True)
                i = tensor.matmul(pb.PSA[:, 512:1024], sb.onesr.ap()[:, 0:128],
                                  vrow[:, 512 * tile:512 * (tile + 1)], start=True, stop=True)
                pb.bump(i, "T")
            pb.on("tensor", tbm)

            for k in range(KCH):
                def fy(sync, lc=lc, tile=tile, k=k):
                    pb.wait(sync, "V")
                    i = sync.dma_start(out=yti[k], in_=dr.yst.ap()[k, :, TPC * lc + 512 * tile:TPC * lc + 512 * (tile + 1)])
                    pb.bump(i, "LD", 16)
                    i = sync.dma_start(out=zti[k], in_=dr.zst.ap()[k, :, TPC * lc + 512 * tile:TPC * lc + 512 * (tile + 1)])
                    pb.bump(i, "LD", 16)
                pb.on("sync", fy)

                def sz(scalar, lc=lc, tile=tile, k=k):
                    pb.wait(scalar, "LD")
                    pb.wait(scalar, "V")
                    i = scalar.activation(szt, zti[k], AF.Silu)
                    pb.bump(i, "S")
                pb.on("scalar", sz)

                def vn(vector, lc=lc, tile=tile, k=k):
                    pb.wait(vector, "T")
                    pb.wait(vector, "S")
                    vector.tensor_sub(ynt, yti[k], pb.PSA[:, 0:512])
                    vector.tensor_mul(ynt, ynt, pb.PSA[:, 512:1024])
                    vector.tensor_scalar(ynt, ynt, prm[:, 32 + k:33 + k],
                                         prm2[:, k:k + 1], ALU.mult, ALU.add)
                    i = vector.tensor_mul(yzb[k], ynt, szt)
                    pb.bump(i, "V")
                pb.on("vector", vn)

            for e in range(8):
                def te(tensor, lc=lc, tile=tile, e=e, r_s=r_s, m_s=m_s):
                    pb.wait(tensor, "V")
                    pb.wait(tensor, "S")
                    ps = pb.PSB[:, 512 * (e % 2):512 * (e % 2) + 512]
                    for k in range(KCH):
                        i = tensor.matmul(ps, sb.WBUF.ap()[:, (k * 8 + e) * 128:(k * 8 + e) * 128 + 128],
                                          yzb[k], start=(k == 0), stop=(k == 3))
                    pb.bump(i, "T")
                pb.on("tensor", te)

                def so(scalar, lc=lc, tile=tile, e=e):
                    pb.wait(scalar, "T")
                    pb.wait(scalar, "ST")
                    ps = pb.PSB[:, 512 * (e % 2):512 * (e % 2) + 512]
                    i = scalar.copy(oev, ps)
                    pb.bump(i, "S")
                pb.on("scalar", so)

                def go(gp, lc=lc, tile=tile, e=e, r_s=r_s, m_s=m_s):
                    pb.wait(gp, "S")
                    i = gp.dma_start(out=dr.poutb.ap()[r_s, 128 * e:128 * (e + 1),
                                                       SUP * m_s:SUP * (m_s + 1)], in_=oev)
                    pb.bump(i, "ST", 16)
                pb.on("gpsimd", go)

    def grs(gp):
        pb.wait(gp, "ST")
        i = gp.collective_compute("ReduceScatter", ALU.add, replica_groups=dr.grp,
                                  ins=[dr.poutb.ap().rearrange("r e t -> (r e) t")],
                                  outs=[dr.prsout.ap()])
        pb.bump(i, "ccrs", 1)
    pb.on("gpsimd", grs)
    pb.barrier()

    for half in range(2):
        buf = (sb.BA.f32() if half == 0 else sb.BB.f32())

        def fres(sync, half=half, buf=buf):
            pb.wait(sync, "ccrs", 1)
            i = sync.dma_start(out=buf.rearrange("p (k t) -> p k t", t=TPC),
                               in_=dr.prsout.ap()[512 * half:512 * (half + 1), :]
                               .rearrange("(k p) t -> p k t", p=128))
            pb.bump(i, "LD", 16)
        pb.on("sync", fres)

        def vres(vector, half=half, buf=buf):
            pb.wait(vector, "LD")
            for k in range(4):
                kk = 4 * half + k
                i = vector.tensor_add(sb.xs.ap()[:, TPC * kk:TPC * (kk + 1)],
                                      sb.xs.ap()[:, TPC * kk:TPC * (kk + 1)],
                                      buf[:, TPC * k:TPC * (k + 1)])
            pb.bump(i, "V")
        pb.on("vector", vres)


def emit_weff_load(pb, dr, sb):
    beff = sb.pool_row_at(1024, 1024)

    def f(sync):
        pb.wait(sync, "ccweff", 1)
        pb.wait(sync, "ST")
        pb.wait(sync, "T")
        pb.wait(sync, "V")
        for r in range(8):
            i = sync.dma_start(out=sb.WBUF.ap()[:, 1024 * r:1024 * (r + 1)],
                               in_=dr.weffg.ap()[128 * r:128 * (r + 1), :])
            pb.bump(i, "LD", 16)
        i = sync.dma_start(out=beff, in_=dr.beffd.ap())
        pb.bump(i, "LD", 16)
    pb.on("sync", f)
    return beff


IO_SPECS = {
    "xin": ([D, TPC], F32), "w0slice": ([8, 128, 1024], F32), "b0all": ([8, 1024], F32),
    "logits": ([1, 8], F32), "uu": ([1, 8], F32),
    "wq": ([1024, 256], F32), "bq": ([1, 256], F32),
    "wqc": ([256, 896], F32), "bqc": ([1, 896], F32),
    "wqr": ([1024, 128], F32), "bqr": ([1, 128], F32),
    "wkv": ([1024, 256], F32), "bkv": ([1, 256], F32),
    "wkr": ([1024, 128], F32), "bkr": ([1, 128], F32),
    "wkc": ([256, 896], F32), "bkc": ([1, 896], F32),
    "wv": ([256, 1024], F32), "bv": ([1, 1024], F32),
    "wo": ([1024, 1024], F32), "bo": ([1, 1024], F32),
    "cosT": ([128, TPC], F32), "sinT": ([128, TPC], F32),
    "masks": ([4, 128, SUP], BF16), "padtab": ([128, NBT], F32),
    "ident": ([128, 128], BF16), "rotT": ([128, 128], F32),
    "w1Ts": ([1024, 512], F32), "w2Ts": ([512, 1024], F32),
    "b1row": ([1, 4096], F32), "b2col": ([128, 8], F32),
    "inxbw": ([1024, 512], F32), "inzw": ([1024, 512], F32),
    "xprojT": ([512, 33], F32),
    "convw": ([128, 16], F32), "convb": ([128, 4], F32),
    "Alog": ([128, 64], F32), "Dpar": ([128, 4], F32),
    "dtw": ([128, 4], F32), "dtb": ([128, 4], F32),
    "lng": ([128, 4], F32), "lnb": ([128, 4], F32),
    "outwT": ([512, 1024], F32),
}


def build_program(upto=6):
    import contextlib

    nc = bass.Bass("TRN2", target_bir_lowering=False, dynamic_dma_scratch_size=16384)
    stack = contextlib.ExitStack()
    io = {}
    for name, (shape, dt) in IO_SPECS.items():
        io[name] = nc.dram_tensor(name, shape, dt, kind="ExternalInput").ap()
    xout = nc.dram_tensor("xout", [D, TPC], F32, kind="ExternalOutput").ap()

    sb = SB(nc)
    dr = DRT(nc, [[0, 1, 2, 3], [4, 5, 6, 7]])
    pb = PB(nc, stack)
    pb.PSA = nc.alloc_psum_tensor("PSA", [128, 2048], F32).ap()
    pb.PSB = nc.alloc_psum_tensor("PSB", [128, 2048], F32).ap()

    emit_consts(pb, io, sb)
    emit_xin(pb, io, sb)
    emit_expert_prep(pb, io, dr, sb)
    pb.barrier()
    emit_ternarize(pb, io, dr, sb)
    pb.barrier()
    sb.row_reset()

    if upto >= 1:
        emit_ln(pb, sb, sb.xs.ap(), sb.BA.bf16())
        pb.barrier()
        beff = emit_weff_load(pb, dr, sb)
        pb.barrier()
        emit_ffn0(pb, sb, sb.BA.bf16(), beff)
        pb.barrier()

    if upto >= 2:
        emit_mla(pb, io, dr, sb)
        emit_attention(pb, io, dr, sb)
        pb.barrier()
        sb.row_reset()

    if upto >= 3:
        sb.row_reset()
        emit_ln(pb, sb, sb.xs.ap(), sb.BA.bf16())
        pb.barrier()
        beff = emit_weff_load(pb, dr, sb)
        pb.barrier()
        emit_ffn0(pb, sb, sb.BA.bf16(), beff)
        pb.barrier()

    if upto >= 4:
        emit_ln(pb, sb, sb.xs.ap(), sb.BA.bf16())
        pb.barrier()
        sb.row_reset()
        emit_ffn1(pb, io, dr, sb)
        pb.barrier()
        sb.row_reset()

    if upto >= 5:
        emit_ssm(pb, io, dr, sb)
        pb.barrier()
        sb.row_reset()

    if upto >= 6:
        emit_ln(pb, sb, sb.xs.ap(), sb.BA.bf16())
        pb.barrier()
        emit_ffn1(pb, io, dr, sb)
        pb.barrier()

    snap = dict(pb.cnt)

    def fo(gp):
        for name, v in snap.items():
            if v > 0:
                gp.wait_ge(pb.sems[name], v)
        i = gp.dma_start(out=xout.rearrange("(k p) t -> p k t", p=128), in_=sb.xs.ap().rearrange("p (k t) -> p k t", t=TPC))
        pb.bump(i, "ST", 16)
        pb.wait(gp, "ST")
    pb.on("gpsimd", fo)

    stack.close()
    return nc


DBG_NORM = None
DBG_CONST_BIAS = False
DBG_ROPE_BARRIER = False
_NC_CACHE = {}


def kernel(**inputs):
    maps = make_inputs(inputs)
    upto = int(_NC_CACHE.get("upto", 6))
    if _NC_CACHE.get("nc_upto") != upto:
        _NC_CACHE["nc"] = build_program(upto)
        _NC_CACHE["nc_upto"] = upto
    nc = _NC_CACHE["nc"]
    res = run_bass_kernel_spmd(nc, maps, core_ids=list(range(8)))
    return assemble_output(res.results)


# version 12
# speedup vs baseline: 1.2145x; 1.0022x over previous
import math

import numpy as np

try:
    import concourse.bass as bass
    import concourse.mybir as mybir
    from concourse.bass_utils import run_bass_kernel_spmd
except ImportError:
    import sys

    sys.path.insert(0, "/opt/trn_rl_repo")
    import concourse.bass as bass
    import concourse.mybir as mybir
    from concourse.bass_utils import run_bass_kernel_spmd

import ml_dtypes

F32 = mybir.dt.float32
BF16 = mybir.dt.bfloat16
AF = mybir.ActivationFunctionType
ALU = mybir.AluOpType
AX = mybir.AxisListType
BF16NP = ml_dtypes.bfloat16

B = 2
L = 8192
D = 1024
TPC = 2048
NCH = 8
SUP = 512
NSUP = 16
PAT = [0, 1, 2, 3, 3, 2, 1, 0, 0, 1, 2, 3, 3, 2, 1, 0]
SLOTS = [[s for s in range(NSUP) if PAT[s] == p] for p in range(4)]
LMAX = [max(SLOTS[p][m] + 1 for p in range(4)) for m in range(4)]
_BT_OFF = [0]
for m in range(4):
    _BT_OFF.append(_BT_OFF[-1] + (LMAX[m] - 1))
NBT_NONDIAG = _BT_OFF[-1]
NBT = NBT_NONDIAG + 4


def bt_col(m, t):
    return _BT_OFF[m] + t


def bt_diag(m):
    return NBT_NONDIAG + m


def sup_addr(s):
    r = PAT[s]
    return r, SLOTS[r].index(s)


H = 16
HD = 64
DC = 256
DR = 128
EH = 896
DIN = 2048
DST = 16
DCONV = 4
CHPC = 512
KCH = 4
FF = 4096
LN_EPS = 1e-5
MAGIC = 12582912.0


def _rope_tables():
    inv = 1.0 / 10000.0 ** (np.arange(0, DR, 2, dtype=np.float64) / DR)
    t = np.arange(L, dtype=np.float64)
    fr = t[:, None] * inv[None, :]
    emb = np.concatenate([fr, fr], axis=1)
    return (np.cos(emb).T.astype(np.float32).copy(),
            np.sin(emb).T.astype(np.float32).copy())


def _diag_masks():
    out = np.zeros((4, 128, SUP), dtype=np.float32)
    for i in range(4):
        kk = 128 * i + np.arange(128)[:, None]
        qq = np.arange(SUP)[None, :]
        out[i] = np.where(kk <= qq, 0.0, -1e9).astype(np.float32)
    return out


def _padtab(p):
    tab = np.zeros((128, NBT), dtype=np.float32)
    for m in range(4):
        s_mine = SLOTS[p][m]
        for t in range(LMAX[m] - 1):
            if t >= s_mine:
                tab[:, bt_col(m, t)] = -1e9
    return tab


def _tokens_of(p):
    idx = []
    for m in range(4):
        s = SLOTS[p][m]
        idx.extend(range(SUP * s, SUP * s + SUP))
    return np.array(idx, dtype=np.int64)


def _rot_lhsT():
    R = np.zeros((DR, DR), dtype=np.float32)
    for i in range(64):
        R[i, i + 64] = -1.0
        R[i + 64, i] = 1.0
    return np.ascontiguousarray(R.T)


def make_inputs(inp):
    cosT, sinT = _rope_tables()
    masks = _diag_masks()
    rotT = _rot_lhsT()
    ident = np.eye(128, dtype=np.float32)
    x = np.asarray(inp["L0_attn_x"])

    def T(a):
        return np.ascontiguousarray(np.asarray(a).T)

    def _pp(a):
        K_, c = a.shape[0] // 128, a.shape[1]
        return np.ascontiguousarray(a.reshape(K_, 128, c).transpose(1, 0, 2).reshape(128, K_ * c))

    w0 = np.asarray(inp["L0_ffn_W"])
    maps = []
    for c in range(8):
        b, p = c // 4, c % 4
        tok = _tokens_of(p)
        m = {
            "xin": np.ascontiguousarray(x[b][tok].T),
            "w0slice": np.ascontiguousarray(w0[:, 128 * c:128 * (c + 1), :]),
            "b0all": np.asarray(inp["L0_ffn_b"]),
            "logits": np.asarray(inp["L0_ffn_logits"]).reshape(1, 8),
            "uu": np.asarray(inp["L0_ffn_u"]).reshape(1, 8),
            "wq": np.asarray(inp["L0_attn_wq"]),
            "bq": np.asarray(inp["L0_attn_bq"]).reshape(1, -1),
            "wqc": np.asarray(inp["L0_attn_wqc"]),
            "bqc": np.asarray(inp["L0_attn_bqc"]).reshape(1, -1),
            "wqr": np.asarray(inp["L0_attn_wqr"]),
            "bqr": np.asarray(inp["L0_attn_bqr"]).reshape(1, -1),
            "wkv": np.asarray(inp["L0_attn_wkv"]),
            "bkv": np.asarray(inp["L0_attn_bkv"]).reshape(1, -1),
            "wkr": np.asarray(inp["L0_attn_wkr"]),
            "bkr": np.asarray(inp["L0_attn_bkr"]).reshape(1, -1),
            "wkc": np.asarray(inp["L0_attn_wkc"]),
            "bkc": np.asarray(inp["L0_attn_bkc"]).reshape(1, -1),
            "wv": np.asarray(inp["L0_attn_wv"]),
            "bv": np.asarray(inp["L0_attn_bv"]).reshape(1, -1),
            "wo": np.asarray(inp["L0_attn_wo"]),
            "bo": np.asarray(inp["L0_attn_bo"]).reshape(1, -1),
            "cosT": np.ascontiguousarray(cosT[:, tok]),
            "sinT": np.ascontiguousarray(sinT[:, tok]),
            "masks": masks.astype(BF16NP),
            "padtab": _padtab(p),
            "ident": ident.astype(BF16NP),
            "rotT": rotT,
            "w1Ts": np.ascontiguousarray(T(inp["L1_ffn_W1"])[:, 512 * c:512 * (c + 1)]),
            "w2Ts": np.ascontiguousarray(T(inp["L1_ffn_W2"])[512 * c:512 * (c + 1), :]),
            "b1row": np.asarray(inp["L1_ffn_b1"]).reshape(1, -1),
            "b2col": _pp(np.asarray(inp["L1_ffn_b2"]).reshape(-1, 1)),
            "inxbw": np.ascontiguousarray(T(inp["L1_attn_in_proj_w"])[:, 512 * p:512 * (p + 1)]),
            "inzw": np.ascontiguousarray(T(inp["L1_attn_in_proj_w"])[:, 2048 + 512 * p:2048 + 512 * (p + 1)]),
            "xprojT": np.ascontiguousarray(T(inp["L1_attn_x_proj_w"])[512 * p:512 * (p + 1), :]),
            "convw": _pp(np.asarray(inp["L1_attn_conv_w"])[512 * p:512 * (p + 1), :]),
            "convb": _pp(np.asarray(inp["L1_attn_conv_b"])[512 * p:512 * (p + 1)].reshape(-1, 1)),
            "Alog": _pp(np.asarray(inp["L1_attn_A_log"])[512 * p:512 * (p + 1), :]),
            "Dpar": _pp(np.asarray(inp["L1_attn_D_param"])[512 * p:512 * (p + 1)].reshape(-1, 1)),
            "dtw": _pp(np.asarray(inp["L1_attn_dt_w"])[512 * p:512 * (p + 1), :]),
            "dtb": _pp(np.asarray(inp["L1_attn_dt_b"])[512 * p:512 * (p + 1)].reshape(-1, 1)),
            "lng": _pp(np.asarray(inp["L1_attn_ln_g"])[512 * p:512 * (p + 1)].reshape(-1, 1)),
            "lnb": _pp(np.asarray(inp["L1_attn_ln_b"])[512 * p:512 * (p + 1)].reshape(-1, 1)),
            "outwT": np.ascontiguousarray(T(inp["L1_attn_out_w"])[512 * p:512 * (p + 1), :]),
        }
        maps.append(m)
    return maps


def assemble_output(results):
    out = np.empty((B, L, D), dtype=np.float32)
    for c in range(8):
        b, p = c // 4, c % 4
        tok = _tokens_of(p)
        out[b][tok] = np.asarray(results[c]["xout"]).T
    return out


_EMIT_OPS = {
    "tensor_copy", "tensor_tensor", "tensor_add", "tensor_sub", "tensor_mul",
    "tensor_max", "tensor_scalar", "tensor_single_scalar", "tensor_scalar_add",
    "tensor_scalar_sub", "tensor_scalar_mul", "tensor_scalar_max",
    "tensor_scalar_min", "tensor_relu", "tensor_reduce", "reduce_max",
    "reduce_sum", "scalar_tensor_tensor", "tensor_tensor_scan", "memset",
    "memzero", "reciprocal", "select", "copy_predicated", "activation",
    "copy", "add", "mul", "square", "sqrt", "sign", "bn_stats", "bn_aggr",
    "transpose", "tensor_tensor_reduce",
}


class _SyncedEng:

    def __init__(self, eng, pb, semname):
        self._eng = eng
        self._pb = pb
        self._sem = semname

    def __getattr__(self, a):
        f = getattr(self._eng, a)
        if a not in _EMIT_OPS:
            return f
        pb, sem = self._pb, self._sem

        def wrapped(*args, **kw):
            s = pb.sem(sem)
            if pb.cnt[sem] > 0:
                self._eng.wait_ge(s, pb.cnt[sem])
            inst = f(*args, **kw)
            pb.cnt[sem] += 1
            inst.then_inc(s, 1)
            return inst
        return wrapped


class PB:

    ENGS = ("tensor", "vector", "scalar", "gpsimd", "sync")

    def __init__(self, nc, stack):
        self.nc = nc
        self.stack = stack
        self.block = stack.enter_context(nc.Block())
        self.sems = {}
        self.cnt = {}

    def sem(self, name):
        if name not in self.sems:
            self.sems[name] = self.stack.enter_context(self.nc.semaphore(name))
            self.cnt[name] = 0
        return self.sems[name]

    def bump(self, inst, name, amt=1):
        if name in ("V", "S"):
            self.sem(name)
            return self.cnt[name]
        s = self.sem(name)
        self.cnt[name] += amt
        inst.then_inc(s, amt)
        return self.cnt[name]

    def val(self, name):
        self.sem(name)
        return self.cnt[name]

    def wait(self, eng, name, v=None):
        s = self.sem(name)
        eng.wait_ge(s, self.cnt[name] if v is None else v)

    def on(self, which, fn):
        if which == "vector":
            getattr(self.block, which)(lambda e: fn(_SyncedEng(e, self, "V")))
        elif which == "scalar":
            getattr(self.block, which)(lambda e: fn(_SyncedEng(e, self, "S")))
        else:
            getattr(self.block, which)(fn)

    def barrier(self):
        snap = dict(self.cnt)

        def mk(eng):
            def f(e):
                for name, v in snap.items():
                    if v > 0:
                        e.wait_ge(self.sems[name], v)
            return f
        for w in self.ENGS:
            self.on(w, mk(w))


def ceil_div(a, b):
    return (a + b - 1) // b



def emit_consts(pb, io, sb):
    def f(sync):
        for a4 in range(4):
            i = sync.dma_start(out=sb.masks.ap()[:, SUP * a4:SUP * (a4 + 1)],
                               in_=io["masks"][a4])
            pb.bump(i, "LD", 16)
        for dst, src in [
            (sb.padtab.ap(), io["padtab"]), (sb.ident.ap(), io["ident"]),
            (sb.rotT.ap(), io["rotT"]),
        ]:
            i = sync.dma_start(out=dst, in_=src)
            pb.bump(i, "LD", 16)
    pb.on("sync", f)

    def g(vector):
        vector.memset(sb.onesr.ap(), 1.0)
        vector.memset(sb.onescol.ap(), 1.0)
        for val, col in sb.CVALS.items():
            i = vector.memset(sb.cpool.ap()[:, col:col + 1], val)
        pb.bump(i, "V")
    pb.on("vector", g)


def emit_xin(pb, io, sb):
    def f(sync):
        i = sync.dma_start(out=sb.xs.ap().rearrange("p (k t) -> p k t", t=TPC), in_=io["xin"].rearrange("(k p) t -> p k t", p=128))
        pb.bump(i, "LD", 16)
    pb.on("sync", f)


def emit_expert_prep(pb, io, dr, sb):
    nc = pb.nc
    lrow = sb.pool_row(8)
    urow = sb.pool_row(8)
    grow = sb.pool_row(8)
    srow = sb.pool_row(8)
    ohrow = sb.pool_row(8)
    mx = sb.pool_row(1)
    negmx = sb.pool_row(1)
    lse = sb.pool_row(1)
    shift = sb.pool_row(1)
    b0sb = sb.BC.f32()[0:8, 2048:3072]

    def f(sync):
        i = sync.dma_start(out=lrow, in_=io["logits"])
        pb.bump(i, "LD", 16)
        i = sync.dma_start(out=urow, in_=io["uu"])
        pb.bump(i, "LD", 16)
        i = sync.dma_start(out=b0sb, in_=io["b0all"])
        pb.bump(i, "LD", 16)
    pb.on("sync", f)

    def v(vector):
        pb.wait(vector, "LD")
        i = vector.reduce_max(mx, lrow, AX.X, negate=False)
        i = vector.tensor_scalar_mul(negmx, mx, -1.0)
        pb.bump(i, "V")
    pb.on("vector", v)

    erow = sb.pool_row(8)
    sesum = sb.pool_row(1)

    def s(scalar):
        pb.wait(scalar, "V")
        scalar.activation(erow, lrow, AF.Exp, bias=negmx, accum_out=sesum)
        i = scalar.activation(lse, sesum, AF.Ln)
        scalar.activation(grow, urow, AF.Ln)
        pb.bump(i, "S")
    pb.on("scalar", s)

    def v2(vector):
        pb.wait(vector, "S")
        i = vector.tensor_scalar_mul(grow, grow, -1.0)
        pb.bump(i, "V")
    pb.on("vector", v2)

    def s2(scalar):
        pb.wait(scalar, "V")
        i = scalar.activation(grow, grow, AF.Ln)
        pb.bump(i, "S")
    pb.on("scalar", s2)

    mxs = sb.pool_row(1)

    def v3(vector):
        pb.wait(vector, "S")
        vector.tensor_sub(srow, lrow, grow)
        vector.tensor_add(shift, mx, lse)
        vector.tensor_scalar_mul(shift, shift, -1.0)
        vector.tensor_scalar_add(srow, srow, shift)
        vector.reduce_max(mxs, srow, AX.X)
        i = vector.tensor_single_scalar(ohrow, srow, mxs, ALU.is_ge)
        pb.bump(i, "V")
    pb.on("vector", v3)

    def g1(gp):
        pb.wait(gp, "V")
        i = gp.dma_start(out=dr.ohd.ap(), in_=ohrow)
        pb.bump(i, "ST", 16)
    pb.on("gpsimd", g1)

    ohcol = sb.ohcol
    ohb = sb.ohb

    def f2(sync):
        pb.wait(sync, "ST")
        i = sync.dma_start(out=ohcol.ap(), in_=dr.ohd.ap().rearrange("a b -> b a"))
        pb.bump(i, "LD", 16)
    pb.on("sync", f2)

    def t1(tensor):
        pb.wait(tensor, "LD")
        pb.wait(tensor, "V")
        tensor.matmul(pb.PSA[0:1, 0:512], ohcol.ap(), b0sb[:, 0:512], start=True, stop=True)
        i = tensor.matmul(pb.PSA[0:1, 512:1024], ohcol.ap(), b0sb[:, 512:1024], start=True, stop=True)
        i = tensor.matmul(pb.PSB[:, 0:8], sb.onesr.ap()[:, 0:128], ohrow, start=True, stop=True)
        pb.bump(i, "T")
    pb.on("tensor", t1)

    beffr = sb.pool_row_at(1024, 1024)

    def v4(vector):
        pb.wait(vector, "T")
        vector.tensor_copy(beffr, pb.PSA[0:1, 0:1024])
        i = vector.tensor_copy(ohb.ap(), pb.PSB[:, 0:8])
        pb.bump(i, "V")
    pb.on("vector", v4)

    def gbe(gp):
        pb.wait(gp, "V")
        i = gp.dma_start(out=dr.beffd.ap(), in_=beffr)
        pb.bump(i, "ST", 16)
    pb.on("gpsimd", gbe)

    wstage = sb.BC.f32()
    acc = wstage[:, 1024:2048]
    for i8 in range(8):
        st = wstage[:, 0:1024]

        def fl(sync, i8=i8, st=st):
            if i8 >= 1:
                pb.wait(sync, "V")
            i = sync.dma_start(out=st, in_=io["w0slice"][i8])
            pb.bump(i, "LD", 16)
        pb.on("sync", fl)

        def vv(vector, i8=i8, st=st):
            pb.wait(vector, "LD")
            if i8 == 0:
                i = vector.tensor_scalar(acc, st, ohb.ap()[:, 0:1], None, ALU.mult)
            else:
                i = vector.scalar_tensor_tensor(acc, st, ohb.ap()[:, i8:i8 + 1], acc,
                                                ALU.mult, ALU.add)
            pb.bump(i, "V")
        pb.on("vector", vv)

    weffbf = sb.BD.bf16()[:, 0:1024]

    def v5(vector):
        i = vector.tensor_copy(weffbf, acc)
        pb.bump(i, "V")
    pb.on("vector", v5)

    def g2(gp):
        pb.wait(gp, "V")
        i = gp.dma_start(out=dr.weffs.ap(), in_=weffbf)
        pb.bump(i, "ST", 16)
        i = gp.collective_compute(
            "AllGather", ALU.bypass, replica_groups=[list(range(8))],
            ins=[dr.weffs.ap()], outs=[dr.weffg.ap()])
        pb.bump(i, "ccweff", 1)
    pb.on("gpsimd", g2)


def emit_ternarize(pb, io, dr, sb, stop=99):
    w1st = sb.BA.f32()[:, 0:4096]
    w2st = sb.BB.f32()[:, 0:4096]
    asum = sb.pool_vec(4)
    s1p = sb.s1p

    def f(sync):
        i = sync.dma_start(out=w1st.rearrange("p (k c) -> p k c", c=512), in_=io["w1Ts"].rearrange("(k p) c -> p k c", p=128))
        pb.bump(i, "LD", 16)
        i = sync.dma_start(out=w2st.rearrange("p (k c) -> p k c", c=1024), in_=io["w2Ts"].rearrange("(k p) c -> p k c", p=128))
        pb.bump(i, "LD", 16)
    pb.on("sync", f)

    if stop <= 1:
        return

    def v(vector):
        pb.wait(vector, "LD")
        vector.memset(s1p.ap(), 0.0)
        vector.tensor_reduce(asum[:, 0:1], w1st, AX.X, ALU.add, apply_absolute_value=True)
        i = vector.tensor_reduce(asum[:, 1:2], w2st, AX.X, ALU.add, apply_absolute_value=True)
        pb.bump(i, "V")
    pb.on("vector", v)

    def t(tensor):
        pb.wait(tensor, "V")
        tensor.matmul(pb.PSA[0:1, 0:1], asum[:, 0:1], sb.onescol.ap(), start=True, stop=True)
        i = tensor.matmul(pb.PSA[0:1, 1:2], asum[:, 1:2], sb.onescol.ap(), start=True, stop=True)
        pb.bump(i, "T")
    pb.on("tensor", t)

    def v2(vector):
        pb.wait(vector, "T")
        i = vector.tensor_copy(s1p.ap()[:, 0:2], pb.PSA[0:1, 0:2])
        pb.bump(i, "V")
    pb.on("vector", v2)

    if stop <= 2:
        return

    def g(gp):
        pb.wait(gp, "V")
        i = gp.dma_start(out=dr.s1pd.ap(), in_=s1p.ap())
        pb.bump(i, "ST", 16)
        i = gp.collective_compute(
            "AllReduce", ALU.add, replica_groups=[list(range(8))],
            ins=[dr.s1pd.ap()], outs=[dr.s1rd.ap()])
        pb.bump(i, "ccs1", 1)
    pb.on("gpsimd", g)

    if stop <= 3:
        return
    stot = sb.pool_row(2)

    def f2(sync):
        pb.wait(sync, "ccs1", 1)
        i = sync.dma_start(out=stot, in_=dr.s1rd.ap()[:, 0:2])
        pb.bump(i, "LD", 16)
    pb.on("sync", f2)

    srow = sb.pool_row(2)

    def v3(vector):
        pb.wait(vector, "LD")
        vector.tensor_scalar_mul(srow, stot, 1.0 / (FF * D))
        i = vector.tensor_scalar_max(srow, srow, 1e-8)
        pb.bump(i, "V")
    pb.on("vector", v3)

    def t2(tensor):
        pb.wait(tensor, "V")
        i = tensor.matmul(pb.PSB[:, 0:2], sb.onesr.ap()[:, 0:128], srow, start=True, stop=True)
        pb.bump(i, "T")
    pb.on("tensor", t2)

    sv = sb.pool_vec(4)

    def v4(vector):
        pb.wait(vector, "T")
        vector.tensor_copy(sv[:, 0:2], pb.PSB[:, 0:2])
        i = vector.reciprocal(sv[:, 2:4], sv[:, 0:2])
        pb.bump(i, "V")
    pb.on("vector", v4)

    if stop <= 4:
        return
    w1bf = sb.BD.bf16()[:, 0:4096]
    w2bf = sb.BD.bf16()[:, 4096:8192]
    for (wst, wbf, scol, icol) in [(w1st, w1bf, 0, 2), (w2st, w2bf, 1, 3)]:
        def v5(vector, wst=wst, wbf=wbf, scol=scol, icol=icol):
            vector.tensor_scalar(wst, wst, sv[:, icol:icol + 1], MAGIC, ALU.mult, ALU.add)
            vector.tensor_scalar_add(wst, wst, -MAGIC)
            vector.tensor_scalar_min(wst, wst, 1.0)
            vector.tensor_scalar_max(wst, wst, -1.0)
            i = vector.tensor_copy(wbf, wst)
            pb.bump(i, "V")
        pb.on("vector", v5)

    if stop <= 5:
        return

    def g2(gp):
        pb.wait(gp, "V")
        i = gp.dma_start(out=dr.w1ts.ap().rearrange("(k p) c -> p k c", p=128), in_=w1bf.rearrange("p (k c) -> p k c", c=512))
        pb.bump(i, "ST", 16)
        i = gp.dma_start(out=dr.w2ts.ap().rearrange("(k p) c -> p k c", p=128), in_=w2bf.rearrange("p (k c) -> p k c", c=1024))
        pb.bump(i, "ST", 16)
        if stop <= 6:
            return
        gp.wait_ge(pb.sems["ST"], pb.cnt["ST"])
        i = gp.collective_compute(
            "AllGather", ALU.bypass, replica_groups=[list(range(8))],
            ins=[dr.w1ts.ap()], outs=[dr.w1tg.ap()])
        pb.bump(i, "ccw1", 1)
        if stop <= 7:
            return
        i = gp.collective_compute(
            "AllGather", ALU.bypass, replica_groups=[list(range(8))],
            ins=[dr.w2ts.ap()], outs=[dr.w2tg.ap()])
        pb.bump(i, "ccw2", 1)
    pb.on("gpsimd", g2)


class Carve:

    def __init__(self, nc, name, cols_f32):
        self.t = nc.alloc_sbuf_tensor("sb_" + name, [128, cols_f32], F32)
        self.cols = cols_f32

    def f32(self):
        return self.t.ap()

    def bf16(self):
        return self.t.ap().bitcast(BF16)


class SB:

    def __init__(self, nc):
        self.xs = nc.alloc_sbuf_tensor("sb_xs", [128, NCH * TPC], F32)
        self.BA = Carve(nc, "BA", 8192)
        self.BB = Carve(nc, "BB", 8192)
        self.WBUF = nc.alloc_sbuf_tensor("sb_WBUF", [128, 8192], BF16)
        self.BC = Carve(nc, "BC", 3584)
        self.BD = Carve(nc, "BD", 4096)
        self.onesr = nc.alloc_sbuf_tensor("sb_onesr", [1, 512], F32)
        self.onescol = nc.alloc_sbuf_tensor("sb_onescol", [128, 1], F32)
        self.masks = nc.alloc_sbuf_tensor("sb_masks", [128, 4 * SUP], BF16)
        self.padtab = nc.alloc_sbuf_tensor("sb_padtab", [128, NBT], F32)
        self.btab = nc.alloc_sbuf_tensor("sb_btab", [128, NBT], F32)
        self.ident = nc.alloc_sbuf_tensor("sb_ident", [128, 128], BF16)
        self.rotT = nc.alloc_sbuf_tensor("sb_rotT", [128, 128], F32)
        self.ohcol = nc.alloc_sbuf_tensor("sb_ohcol", [8, 1], F32)
        self.ohb = nc.alloc_sbuf_tensor("sb_ohb", [128, 8], F32)
        self.s1p = nc.alloc_sbuf_tensor("sb_s1p", [1, 128], F32)
        self.rows = nc.alloc_sbuf_tensor("sb_rows", [1, 2048], F32)
        self._rowoff = 0
        self.vecs = nc.alloc_sbuf_tensor("sb_vecs", [128, 2048], F32)
        self._vecoff = 0
        self.states = nc.alloc_sbuf_tensor("sb_states", [128, KCH * DST], F32)
        self.Aneg = nc.alloc_sbuf_tensor("sb_Aneg", [128, KCH * DST], F32)
        self.pvec = nc.alloc_sbuf_tensor("sb_pvec", [128, 256], F32)
        self._pvoff = 0
        self.cpool = nc.alloc_sbuf_tensor("sb_cpool", [128, 8], F32)
        self.rows2 = nc.alloc_sbuf_tensor("sb_rows2", [1, TPC], F32)
        self.CVALS = {1e-5: 0, -0.5: 1, -1.0: 2, 1.0: 3, 0.125: 4, 0.5: 5}

    def pool_row(self, n):
        ap = self.rows.ap()[:, self._rowoff:self._rowoff + n]
        self._rowoff += n
        assert self._rowoff <= 2048
        return ap

    def pool_row_at(self, off, n):
        return self.rows.ap()[:, off:off + n]

    def row_reset(self):
        self._rowoff = 0

    def vec_carve(self, off, n):
        return self.vecs.ap()[:, off:off + n]

    def cv(self, val, P=128):
        return self.cpool.ap()[0:P, self.CVALS[val]:self.CVALS[val] + 1]

    def pool_vec(self, n):
        ap = self.vecs.ap()[:, self._vecoff:self._vecoff + n]
        self._vecoff += n
        assert self._vecoff <= 512
        return ap

    def pv(self, n):
        ap = self.pvec.ap()[:, self._pvoff:self._pvoff + n]
        self._pvoff += n
        assert self._pvoff <= 256
        return ap


class DRT:

    def __init__(self, nc, grp):
        g8 = [list(range(8))]
        self.grp = grp
        self.ohd = nc.dram_tensor("ohd", [1, 8], F32)
        self.beffd = nc.dram_tensor("beffd", [1, 1024], F32)
        self.weffs = nc.dram_tensor("weffs", [128, 1024], BF16)
        self.weffg = nc.dram_tensor("weffg", [8 * 128, 1024], BF16)
        self.s1pd = nc.dram_tensor("s1pd", [1, 128], F32)
        self.s1rd = nc.dram_tensor("s1rd", [1, 128], F32)
        self.w1ts = nc.dram_tensor("w1ts", [1024, 512], BF16)
        self.w1tg = nc.dram_tensor("w1tg", [8 * 1024, 512], BF16)
        self.w2ts = nc.dram_tensor("w2ts", [512, 1024], BF16)
        self.w2tg = nc.dram_tensor("w2tg", [8 * 512, 1024], BF16)
        self.ks = nc.dram_tensor("ks", [1024, TPC], BF16)
        self.kgat = nc.dram_tensor("kgat", [4 * 1024, TPC], BF16)
        self.vs = nc.dram_tensor("vs", [TPC, 1024], BF16)
        self.vgat = nc.dram_tensor("vgat", [4 * TPC, 1024], BF16)
        self.ln5s = nc.dram_tensor("ln5s", [1024, TPC], BF16)
        self.ln5g = nc.dram_tensor("ln5g", [4 * 1024, TPC], BF16)
        self.xbcst = nc.dram_tensor("xbcst", [KCH, 128, L], F32)
        self.dtst = nc.dram_tensor("dtst", [KCH, 128, L], F32)
        self.zst = nc.dram_tensor("zst", [KCH, 128, L], F32)
        self.yst = nc.dram_tensor("yst", [KCH, 128, L], F32)
        self.xpk = nc.dram_tensor("xpk", [KCH, 33, L], F32)
        self.xprojp = nc.dram_tensor("xprojp", [33, L], F32)
        self.xprojr = nc.dram_tensor("xprojr", [33, L], F32)
        self.statsp = nc.dram_tensor("statsp", [1, 2 * L], F32)
        self.statsr = nc.dram_tensor("statsr", [1, 2 * L], F32)
        self.poutb = nc.dram_tensor("poutb", [4, 1024, TPC], F32)
        self.prsout = nc.dram_tensor("prsout", [1024, TPC], F32)


def emit_ln(pb, sb, src_ap, dst_ap, dim=D, nch=NCH):
    sq0 = sb.BC.f32()[:, 0:2048]
    sqs = [sq0, sq0]
    NT = TPC // SUP

    mmq_bump = [None] * nch

    for k in range(nch):
        def s(scalar, k=k):
            if k == 0:
                pb.wait(scalar, "V")
                pb.wait(scalar, "T")
            if k >= 1 and mmq_bump[k - 1] is not None:
                pb.wait(scalar, "T", mmq_bump[k - 1])
            i = scalar.activation(sqs[k % 2], src_ap[:, TPC * k:TPC * (k + 1)], AF.Square)
            pb.bump(i, "S")
        pb.on("scalar", s)

        def t(tensor, k=k):
            pb.wait(tensor, "S")
            for tt in range(NT):
                tensor.matmul(pb.PSA[0:1, SUP * tt:SUP * (tt + 1)], sb.onescol.ap(),
                              src_ap[:, TPC * k + SUP * tt: TPC * k + SUP * (tt + 1)],
                              start=(k == 0), stop=(k == nch - 1))
                i = tensor.matmul(pb.PSB[0:1, SUP * tt:SUP * (tt + 1)], sb.onescol.ap(),
                                  sqs[k % 2][:, SUP * tt:SUP * (tt + 1)],
                                  start=(k == 0), stop=(k == nch - 1))
            mmq_bump[k] = pb.bump(i, "T")
        pb.on("tensor", t)

    rA = sb.rows.ap()[:, 0:TPC]
    rB = sb.vecs.ap()[0:1, 0:TPC]
    rC = sb.rows2.ap()

    def v(vector):
        pb.wait(vector, "T")
        vector.tensor_copy(rA, pb.PSA[0:1, 0:TPC])
        vector.tensor_copy(rB, pb.PSB[0:1, 0:TPC])
        vector.tensor_scalar_mul(rA, rA, 1.0 / dim)
        vector.tensor_scalar_mul(rB, rB, 1.0 / dim)
        vector.tensor_mul(rC, rA, rA)
        i = vector.tensor_sub(rB, rB, rC)
        pb.bump(i, "V")
    pb.on("vector", v)

    def s2(scalar):
        pb.wait(scalar, "V")
        scalar.activation(rB, rB, AF.Ln, bias=sb.cv(1e-5, rB.shape[0]))
        i = scalar.activation(rB, rB, AF.Exp, scale=sb.cv(-0.5, rB.shape[0]))
        pb.bump(i, "S")
    pb.on("scalar", s2)

    def v2(vector):
        pb.wait(vector, "S")
        i = vector.tensor_mul(rC, rA, rB)
        pb.bump(i, "V")
    pb.on("vector", v2)

    for tt in range(NT):
        def t2(tensor, tt=tt):
            pb.wait(tensor, "V")
            tensor.matmul(pb.PSA[:, SUP * tt:SUP * (tt + 1)], sb.onesr.ap()[:, 0:128],
                          rB[:, SUP * tt:SUP * (tt + 1)], start=True, stop=True)
            i = tensor.matmul(pb.PSB[:, SUP * tt:SUP * (tt + 1)], sb.onesr.ap()[:, 0:128],
                              rC[:, SUP * tt:SUP * (tt + 1)], start=True, stop=True)
            pb.bump(i, "T")
        pb.on("tensor", t2)

    t512a = sb.BC.f32()[:, 0:512]
    t512b = sb.BC.f32()[:, 512:1024]

    def v3(vector):
        pb.wait(vector, "T")
        for k in range(nch):
            for tt in range(NT):
                sl = slice(TPC * k + SUP * tt, TPC * k + SUP * (tt + 1))
                tb = t512a if (k % 2 == 0) else t512b
                vector.tensor_mul(tb, src_ap[:, sl], pb.PSA[:, SUP * tt:SUP * (tt + 1)])
                i = vector.tensor_sub(dst_ap[:, sl], tb, pb.PSB[:, SUP * tt:SUP * (tt + 1)])
        pb.bump(i, "V")
    pb.on("vector", v3)


def emit_ffn0(pb, sb, lnsrc_bf16, beff):
    NT = TPC // SUP
    vlastf = [None] * 4
    for tt in range(NT):
        for j in range(NCH):
            bank = (tt * NCH + j) % 4

            def t(tensor, tt=tt, j=j, bank=bank):
                if tt == 0 and j == 0:
                    pb.wait(tensor, "ccweff", 1)
                    pb.wait(tensor, "LD")
                    pb.wait(tensor, "V")
                if vlastf[bank] is not None:
                    pb.wait(tensor, "V", vlastf[bank])
                ps = pb.PSA[:, SUP * bank:SUP * (bank + 1)]
                for k in range(NCH):
                    tensor.matmul(ps, sb.WBUF.ap()[:, 1024 * k + 128 * j:1024 * k + 128 * (j + 1)],
                                  lnsrc_bf16[:, TPC * k + SUP * tt:TPC * k + SUP * (tt + 1)],
                                  start=(k == 0), stop=False)
                i = tensor.matmul(ps, beff[:, 128 * j:128 * (j + 1)],
                                  sb.onesr.ap(), start=False, stop=True)
                pb.bump(i, "T")
            pb.on("tensor", t)

            def v(vector, tt=tt, j=j, bank=bank):
                pb.wait(vector, "T")
                ps = pb.PSA[:, SUP * bank:SUP * (bank + 1)]
                xsl = sb.xs.ap()[:, TPC * j + SUP * tt:TPC * j + SUP * (tt + 1)]
                i = vector.scalar_tensor_tensor(xsl, ps, 0.5, xsl, ALU.mult, ALU.add)
                vlastf[bank] = pb.bump(i, "V")
            pb.on("vector", v)


def load_w_bf16(pb, sb, dram_ap, KK, C, dst_cols_off):
    src3 = dram_ap.rearrange("(k p) c -> p k c", p=128)
    kq = max(1, 3584 // C)
    k0 = 0
    while k0 < KK:
        kn = min(kq, KK - k0)
        n = kn * C
        st = sb.BC.f32()[:, 0:n]

        def f(sync, st=st, k0=k0, kn=kn, n=n):
            pb.wait(sync, "V")
            i = sync.dma_start(out=st.rearrange("p (k c) -> p k c", c=C),
                               in_=src3[:, k0:k0 + kn, :])
            pb.bump(i, "LD", 16)
        pb.on("sync", f)

        def v(vector, st=st, k0=k0, n=n):
            pb.wait(vector, "LD")
            off = dst_cols_off + k0 * C
            i = vector.tensor_copy(sb.WBUF.ap()[:, off:off + n], st)
            pb.bump(i, "V")
        pb.on("vector", v)
        k0 += kn


def load_brow(pb, sb, src, n):
    dst = sb.pool_row(n)

    def f(sync):
        i = sync.dma_start(out=dst, in_=src)
        pb.bump(i, "LD", 16)
    pb.on("sync", f)
    return dst


def proj_mm(pb, sb, dst_bf16_cols, rhs_bf16, rhs_nch, wb_off, ncol_chunks,
            brow_ap, act, extra_ps=None):
    NT = TPC // SUP
    vlast = [None] * 4
    for j in range(ncol_chunks):
        for tt in range(NT):
            bank = (j * NT + tt) % 4

            def t(tensor, j=j, tt=tt, bank=bank):
                if vlast[bank] is not None:
                    pb.wait(tensor, "S", vlast[bank])
                else:
                    pb.wait(tensor, "S")
                    pb.wait(tensor, "V")
                ps = pb.PSA[:, SUP * bank:SUP * (bank + 1)]
                for k in range(rhs_nch):
                    tensor.matmul(ps, sb.WBUF.ap()[:, wb_off + (k * ncol_chunks + j) * 128:
                                                   wb_off + (k * ncol_chunks + j) * 128 + 128],
                                  rhs_bf16[:, TPC * k + SUP * tt:TPC * k + SUP * (tt + 1)],
                                  start=(k == 0), stop=False)
                i = tensor.matmul(ps, brow_ap[:, 128 * j:128 * (j + 1)],
                                  sb.onesr.ap(), start=False, stop=True)
                pb.bump(i, "T")
            pb.on("tensor", t)

            def s(scalar, j=j, tt=tt, bank=bank):
                pb.wait(scalar, "T")
                ps = pb.PSA[:, SUP * bank:SUP * (bank + 1)]
                dst = dst_bf16_cols[:, TPC * j + SUP * tt:TPC * j + SUP * (tt + 1)]
                if act is None:
                    i = scalar.copy(dst, ps)
                else:
                    i = scalar.activation(dst, ps, act)
                vlast[bank] = pb.bump(i, "S")
            pb.on("scalar", s)


def emit_rot_weights(pb, sb, src_off, dst_off, nk=8):
    def v(vector):
        pb.wait(vector, "LD")
        for k in range(nk):
            s0 = sb.WBUF.ap()[:, src_off + 128 * k:src_off + 128 * k + 64]
            s1 = sb.WBUF.ap()[:, src_off + 128 * k + 64:src_off + 128 * k + 128]
            d0 = sb.WBUF.ap()[:, dst_off + 128 * k:dst_off + 128 * k + 64]
            d1 = sb.WBUF.ap()[:, dst_off + 128 * k + 64:dst_off + 128 * k + 128]
            vector.tensor_scalar(d0, s1, -1.0, None, ALU.mult)
            i = vector.tensor_copy(d1, s0)
        pb.bump(i, "V")
    pb.on("vector", v)


def emit_rot_brow(pb, sb, brow):
    out = sb.pool_row(128)

    def v(vector):
        pb.wait(vector, "LD")
        vector.tensor_scalar(out[:, 0:64], brow[:, 64:128], -1.0, None, ALU.mult)
        i = vector.tensor_copy(out[:, 64:128], brow[:, 0:64])
        pb.bump(i, "V")
    pb.on("vector", v)
    return out


def emit_rope_chunk(pb, sb, dst_bf16, wb_off, wb_rot_off, brow_ap, brow_rot_ap,
                    lnsrc, cosT, sinT):
    NT = TPC // SUP
    t1 = sb.vec_carve(0, 512)
    t2s = sb.vec_carve(512, 512)
    for tt in range(NT):
        def t(tensor, tt=tt):
            pb.wait(tensor, "V")
            for k in range(NCH):
                tensor.matmul(pb.PSA[:, 0:512],
                              sb.WBUF.ap()[:, wb_off + k * 128:wb_off + k * 128 + 128],
                              lnsrc[:, TPC * k + SUP * tt:TPC * k + SUP * (tt + 1)],
                              start=(k == 0), stop=False)
            tensor.matmul(pb.PSA[:, 0:512], brow_ap[:, 0:128], sb.onesr.ap(),
                          start=False, stop=True)
            for k in range(NCH):
                tensor.matmul(pb.PSA[:, 512:1024],
                              sb.WBUF.ap()[:, wb_rot_off + k * 128:wb_rot_off + k * 128 + 128],
                              lnsrc[:, TPC * k + SUP * tt:TPC * k + SUP * (tt + 1)],
                              start=(k == 0), stop=False)
            i = tensor.matmul(pb.PSA[:, 512:1024], brow_rot_ap[:, 0:128], sb.onesr.ap(),
                              start=False, stop=True)
            pb.bump(i, "T")
        pb.on("tensor", t)

        def v(vector, tt=tt):
            pb.wait(vector, "T")
            pb.wait(vector, "S")
            cs = slice(SUP * tt, SUP * (tt + 1))
            vector.tensor_mul(t1, pb.PSA[:, 0:512], cosT[:, cs])
            vector.tensor_mul(t2s, pb.PSA[:, 512:1024], sinT[:, cs])
            i = vector.tensor_add(t1, t1, t2s)
            pb.bump(i, "V")
        pb.on("vector", v)

        def s2(scalar, tt=tt):
            pb.wait(scalar, "V")
            i = scalar.activation(dst_bf16[:, SUP * tt:SUP * (tt + 1)], t1, AF.Gelu)
            pb.bump(i, "S")
        pb.on("scalar", s2)
        if DBG_ROPE_BARRIER:
            pb.barrier()


def emit_mla(pb, io, dr, sb, stop=99):
    nc = pb.nc
    NT = TPC // SUP
    lnb = sb.BA.bf16()
    cq = sb.BD.bf16()[:, 0:4096]
    ckv = sb.BD.bf16()[:, 4096:8192]
    qT = sb.BB.bf16()

    emit_ln(pb, sb, sb.xs.ap(), lnb)
    pb.barrier()
    sb.row_reset()

    load_w_bf16(pb, sb, io["wq"], 8, 256, 0)
    load_w_bf16(pb, sb, io["wkv"], 8, 256, 2048)
    load_w_bf16(pb, sb, io["wqr"], 8, 128, 4096)
    load_w_bf16(pb, sb, io["wkr"], 8, 128, 5120)
    bq = load_brow(pb, sb, io["bq"], 256)
    bkv = load_brow(pb, sb, io["bkv"], 256)
    bqr = load_brow(pb, sb, io["bqr"], 128)
    bkr = load_brow(pb, sb, io["bkr"], 128)
    cosT = sb.BB.f32()[:, 0:TPC]
    sinT = sb.BB.f32()[:, TPC:2 * TPC]
    emit_rot_weights(pb, sb, 4096, 6144)
    emit_rot_weights(pb, sb, 5120, 7168)

    def fcs(sync):
        i = sync.dma_start(out=cosT, in_=io["cosT"])
        pb.bump(i, "LD", 16)
        i = sync.dma_start(out=sinT, in_=io["sinT"])
        pb.bump(i, "LD", 16)
    pb.on("sync", fcs)
    pb.barrier()
    if stop <= 1:
        return
    proj_mm(pb, sb, cq, lnb, NCH, 0, 2, bq, AF.Gelu)
    proj_mm(pb, sb, ckv, lnb, NCH, 2048, 2, bkv, AF.Gelu)
    if stop <= 2:
        return
    bqr_rot = emit_rot_brow(pb, sb, bqr)
    bkr_rot = emit_rot_brow(pb, sb, bkr)
    pb.barrier()
    emit_rope_chunk(pb, sb, qT[:, 7 * TPC:8 * TPC], 4096, 6144, bqr, bqr_rot, lnb, cosT, sinT)
    k7 = sb.BC.bf16()[:, 0:TPC]
    emit_rope_chunk(pb, sb, k7, 5120, 7168, bkr, bkr_rot, lnb, cosT, sinT)
    pb.barrier()
    sb.row_reset()

    def g(gp):
        pb.wait(gp, "S")
        i = gp.dma_start(out=dr.ks.ap()[128 * 7:128 * 8, :], in_=k7)
        pb.bump(i, "ST", 16)
    pb.on("gpsimd", g)
    pb.barrier()

    if stop <= 3:
        return
    if stop <= 35:
        return
    load_w_bf16(pb, sb, io["wqc"], 2, 896, 0)
    load_w_bf16(pb, sb, io["wkc"], 2, 896, 1792)
    load_w_bf16(pb, sb, io["wv"], 2, 1024, 3584)
    bqc = load_brow(pb, sb, io["bqc"], 896)
    bkc = load_brow(pb, sb, io["bkc"], 896)
    bv = sb.vecs.ap()[0:1, 0:1024]

    def fbv(sync):
        pb.wait(sync, "S")
        i = sync.dma_start(out=bv, in_=io["bv"])
        pb.bump(i, "LD", 16)
    pb.on("sync", fbv)
    pb.barrier()
    if stop <= 31:
        return
    proj_mm(pb, sb, qT, cq, 2, 0, 7, bqc, AF.Gelu)
    if stop <= 32:
        return
    ksc = sb.BC.bf16()[:, 0:TPC]
    vlast = [None]
    for j in range(7):
        for tt in range(NT):
            def t(tensor, j=j, tt=tt):
                pb.wait(tensor, "S")
                if vlast[0] is not None:
                    pb.wait(tensor, "ST", vlast[0])
                ps = pb.PSA[:, 1024:1536]
                for k in range(2):
                    tensor.matmul(ps, sb.WBUF.ap()[:, 1792 + (k * 7 + j) * 128:1792 + (k * 7 + j) * 128 + 128],
                                  ckv[:, TPC * k + SUP * tt:TPC * k + SUP * (tt + 1)],
                                  start=(k == 0), stop=False)
                i = tensor.matmul(ps, bkc[:, 128 * j:128 * (j + 1)], sb.onesr.ap(),
                                  start=False, stop=True)
                pb.bump(i, "T")
            pb.on("tensor", t)

            def s(scalar, j=j, tt=tt):
                pb.wait(scalar, "T")
                i = scalar.activation(ksc[:, SUP * tt:SUP * (tt + 1)], pb.PSA[:, 1024:1536], AF.Gelu)
                pb.bump(i, "S")
            pb.on("scalar", s)

        def g2(gp, j=j):
            pb.wait(gp, "S")
            i = gp.dma_start(out=dr.ks.ap()[128 * j:128 * (j + 1), :], in_=ksc)
            vlast[0] = pb.bump(i, "ST", 16)
        pb.on("gpsimd", g2)

    if stop <= 4:
        return
    vsc = sb.BC.bf16()[:, 2048:2560]
    stlast = [None]
    for tc in range(TPC // 128):
        for dv in range(2):
            def t(tensor, tc=tc, dv=dv):
                pb.wait(tensor, "S")
                if stlast[0] is not None:
                    pb.wait(tensor, "ST", stlast[0])
                ps = pb.PSB[:, 0:512]
                for k in range(2):
                    tensor.matmul(ps, ckv[:, TPC * k + 128 * tc:TPC * k + 128 * (tc + 1)],
                                  sb.WBUF.ap()[:, 3584 + k * 1024 + 512 * dv:3584 + k * 1024 + 512 * (dv + 1)],
                                  start=(k == 0), stop=False)
                i = tensor.matmul(ps, sb.onesr.ap()[:, 0:128],
                                  bv[:, 512 * dv:512 * (dv + 1)], start=False, stop=True)
                pb.bump(i, "T")
            pb.on("tensor", t)

            def s(scalar, tc=tc, dv=dv):
                pb.wait(scalar, "T")
                i = scalar.copy(vsc, pb.PSB[:, 0:512])
                pb.bump(i, "S")
            pb.on("scalar", s)

            def g3(gp, tc=tc, dv=dv):
                pb.wait(gp, "S")
                i = gp.dma_start(out=dr.vs.ap()[128 * tc:128 * (tc + 1), 512 * dv:512 * (dv + 1)],
                                 in_=vsc)
                stlast[0] = pb.bump(i, "ST", 16)
            pb.on("gpsimd", g3)

    if stop <= 5:
        return
    def g4(gp):
        pb.wait(gp, "ST")
        i = gp.collective_compute("AllGather", ALU.bypass, replica_groups=dr.grp,
                                  ins=[dr.ks.ap()], outs=[dr.kgat.ap()])
        pb.bump(i, "cckg", 1)
        i = gp.collective_compute("AllGather", ALU.bypass, replica_groups=dr.grp,
                                  ins=[dr.vs.ap()], outs=[dr.vgat.ap()])
        pb.bump(i, "ccvg", 1)
    pb.on("gpsimd", g4)
    pb.barrier()
    sb.row_reset()


def emit_attention(pb, io, dr, sb):
    NT = TPC // SUP
    ostack = sb.BA.bf16()
    qT = sb.BB.bf16()
    vth = sb.BC.bf16()[:, 0:4160]
    pT = sb.BC.bf16()[:, 4160:6208]
    obuf = sb.vecs.ap().bitcast(BF16)[0:64, 3204:3716]
    rlb = sb.vec_carve(512, 512)
    negmv = sb.vecs.ap()[:, 1024:1025]
    rr = sb.vecs.ap()[64:65, 1026:1538]
    ones64 = sb.vecs.ap()[64:65, 1538:1602]

    def vv(vector):
        pb.wait(vector, "ST")
        pb.wait(vector, "T")
        pb.wait(vector, "S")
        vector.memset(ones64, 1.0)
        vector.memset(sb.BC.bf16()[:, 6720:6980].rearrange("p (j c) -> p j c", c=65)[:, :, 64:65], 1.0)
        i = vector.memset(vth.rearrange("p (j c) -> p j c", c=65)[:, :, 64:65], 1.0)
        pb.bump(i, "V")
    pb.on("vector", vv)

    kdg = sb.BC.bf16()
    vdg = sb.BC.bf16()[:, 6720:6980]
    lastqk = {}
    lastpv = {}
    lastsq = {}
    for h in range(H):
        ch, hh = h // 2, h % 2
        kc = sb.BD.bf16()[64 * hh:64 * hh + 64, 0:8192]
        kdiag = kdg[64 * hh:64 * hh + 64, 6208:6720]
        nsc = sb.vec_carve(0, 512)[64 * hh:64 * hh + 64, :]
        olast = [None]

        def ldh(sync, h=h, ch=ch, hh=hh, kc=kc):
            if h == 0:
                pb.wait(sync, "cckg", 1)
                pb.wait(sync, "ccvg", 1)
                pb.wait(sync, "V")
            if h - 2 in lastqk:
                pb.wait(sync, "T", lastqk[h - 2])
            if h - 2 in lastsq:
                pb.wait(sync, "S", lastsq[h - 2])
            if h - 1 in lastpv:
                pb.wait(sync, "T", lastpv[h - 1])
            for r in range(4):
                i = sync.dma_start(out=kc[:, 2048 * r:2048 * (r + 1)],
                                   in_=dr.kgat.ap()[1024 * r + 128 * ch + 64 * hh:
                                                    1024 * r + 128 * ch + 64 * hh + 64, :])
                pb.bump(i, "LD", 16)
                vsrc = dr.vgat.ap()[TPC * r:TPC * (r + 1), 64 * h:64 * (h + 1)]
                vdst = vth.rearrange("p (j c) -> p j c", c=65)[:, 16 * r:16 * (r + 1), 0:64]
                i = sync.dma_start(out=vdst, in_=vsrc.rearrange("(j p) d -> p j d", p=128))
                pb.bump(i, "LD", 16)
        pb.on("sync", ldh)

        for m in (3, 2, 1, 0):
            nst = LMAX[m]

            def ld(sync, h=h, ch=ch, hh=hh, m=m, kdiag=kdiag):
                pb.wait(sync, "T")
                i = sync.dma_start(out=kdiag,
                                   in_=dr.ks.ap()[128 * ch + 64 * hh:128 * ch + 64 * hh + 64,
                                                  SUP * m:SUP * (m + 1)])
                pb.bump(i, "LD", 16)
                vsrc = dr.vs.ap()[SUP * m:SUP * (m + 1), 64 * h:64 * (h + 1)]
                vdst = vdg.rearrange("p (j c) -> p j c", c=65)[:, :, 0:64]
                i = sync.dma_start(out=vdst, in_=vsrc.rearrange("(j p) d -> p j d", p=128))
                pb.bump(i, "LD", 16)
            pb.on("sync", ld)

            if m == 3:
                r_rk = sb.pool_row_at(0, 512)
                r_mx = sb.pool_row_at(512, 2)
                r_sc = sb.pool_row_at(514, 6)
                for kind, ntile in (("k", 16), ("q", 4)):
                    for tile in range(ntile):
                        def s(scalar, kind=kind, tile=tile, ch=ch, hh=hh, kc=kc, nsc=nsc,
                              h=h, lastsq=lastsq):
                            if kind == "k" and tile == 0:
                                pb.wait(scalar, "LD")
                            pb.wait(scalar, "T")
                            if kind == "k":
                                srcq = kc[:, SUP * tile:SUP * (tile + 1)]
                            else:
                                srcq = qT[64 * hh:64 * hh + 64,
                                          TPC * ch + SUP * tile:TPC * ch + SUP * (tile + 1)]
                            i = scalar.activation(nsc, srcq, AF.Square)
                            b_ = pb.bump(i, "S")
                            if kind == "k" and tile == 15:
                                lastsq[h] = b_
                        pb.on("scalar", s)

                        def t(tensor, hh=hh, nsc=nsc):
                            pb.wait(tensor, "S")
                            pb.wait(tensor, "V")
                            i = tensor.matmul(pb.PSB[0:1, 1024:1536],
                                              sb.onescol.ap()[64 * hh:64 * hh + 64, :],
                                              nsc, start=True, stop=True)
                            pb.bump(i, "T")
                        pb.on("tensor", t)

                        def v(vector, kind=kind, tile=tile, r_rk=r_rk):
                            pb.wait(vector, "T")
                            if tile == 0:
                                i = vector.tensor_copy(r_rk, pb.PSB[0:1, 1024:1536])
                            else:
                                i = vector.tensor_max(r_rk, r_rk, pb.PSB[0:1, 1024:1536])
                            pb.bump(i, "V")
                        pb.on("vector", v)

                    col = 0 if kind == "k" else 1

                    def v2(vector, col=col, r_rk=r_rk, r_mx=r_mx):
                        i = vector.reduce_max(r_mx[:, col:col + 1], r_rk, AX.X)
                        pb.bump(i, "V")
                    pb.on("vector", v2)

                def s3(scalar, r_mx=r_mx, r_sc=r_sc):
                    pb.wait(scalar, "V")
                    i = scalar.activation(r_sc[:, 0:2], r_mx, AF.Ln)
                    pb.bump(i, "S")
                pb.on("scalar", s3)

                def v3(vector, r_sc=r_sc):
                    pb.wait(vector, "S")
                    i = vector.tensor_add(r_sc[:, 2:3], r_sc[:, 0:1], r_sc[:, 1:2])
                    pb.bump(i, "V")
                pb.on("vector", v3)

                def s4(scalar, r_sc=r_sc):
                    pb.wait(scalar, "V")
                    i = scalar.activation(r_sc[:, 3:4], r_sc[:, 2:3], AF.Exp, scale=sb.cv(0.5, 1))
                    pb.bump(i, "S")
                pb.on("scalar", s4)

                def v4(vector, r_sc=r_sc):
                    pb.wait(vector, "S")
                    i = vector.tensor_scalar_mul(r_sc[:, 4:5], r_sc[:, 3:4], -0.125)
                    pb.bump(i, "V")
                pb.on("vector", v4)

                def t3(tensor, r_sc=r_sc):
                    pb.wait(tensor, "V")
                    i = tensor.matmul(pb.PSB[:, 1536:1537], sb.onesr.ap()[:, 0:128],
                                      r_sc[:, 4:5], start=True, stop=True)
                    pb.bump(i, "T")
                pb.on("tensor", t3)

                def v5(vector):
                    pb.wait(vector, "T")
                    pb.wait(vector, "S")
                    vector.tensor_copy(negmv, pb.PSB[:, 1536:1537])
                    i = vector.tensor_scalar_add(sb.btab.ap(), sb.padtab.ap(), negmv)
                    pb.bump(i, "V")
                pb.on("vector", v5)

                if DBG_NORM is not None:
                    def vdbg(gp, h=h, r_sc=r_sc, r_mx=r_mx):
                        pb.wait(gp, "V")
                        i = gp.dma_start(out=DBG_NORM[0][0:1, 8 * h:8 * h + 6], in_=r_sc)
                        pb.bump(i, "ST", 16)
                        i = gp.dma_start(out=DBG_NORM[0][0:1, 128 + 8 * h:128 + 8 * h + 2], in_=r_mx)
                        pb.bump(i, "ST", 16)
                    pb.on("gpsimd", vdbg)

            nsteps = 2 * nst
            expb = [None] * nsteps
            pvb = [None] * nsteps
            qkb = [None] * nsteps
            qrhs = qT[64 * hh:64 * hh + 64, TPC * ch + SUP * m:TPC * ch + SUP * (m + 1)]

            def emit_pv(tensor, j_s, m=m, nst=nst, nsteps=nsteps, expb=expb,
                        pvb=pvb, olast=olast, h=h, lastpv=lastpv):
                pb.wait(tensor, "S", expb[j_s])
                if j_s == 0 and olast[0] is not None:
                    pb.wait(tensor, "V", olast[0])
                tp, hp = j_s // 2, j_s % 2
                dg = (tp == nst - 1)
                pt = pT[:, 1024 * (j_s % 2):1024 * (j_s % 2) + 1024]
                for i2 in range(2):
                    kb = 2 * hp + i2
                    if dg:
                        vt = vdg[:, 65 * kb:65 * kb + 65]
                    else:
                        r, mi = sup_addr(tp)
                        pos = 4 * r + mi
                        vt = vth[:, 65 * (4 * pos + kb):65 * (4 * pos + kb) + 65]
                    i = tensor.matmul(pb.PSB[0:65, 0:512], vt, pt[:, 512 * i2:512 * (i2 + 1)],
                                      start=(j_s == 0 and i2 == 0),
                                      stop=(j_s == nsteps - 1 and i2 == 1))
                pvb[j_s] = pb.bump(i, "T")
                if m == 0:
                    lastpv[h] = pvb[j_s]

            for i_s in range(nsteps):
                t_sup, half = i_s // 2, i_s % 2
                diag = (t_sup == nst - 1)
                sbank = pb.PSA[:, 1024 * (i_s % 2):1024 * (i_s % 2) + 1024]
                ptile = pT[:, 1024 * (i_s % 2):1024 * (i_s % 2) + 1024]

                def tq(tensor, i_s=i_s, t_sup=t_sup, half=half, diag=diag,
                       sbank=sbank, m=m, qrhs=qrhs, expb=expb, emit_pv=emit_pv,
                       kc=kc, kdiag=kdiag, qkb=qkb, h=h, lastqk=lastqk):
                    if i_s == 0:
                        pb.wait(tensor, "LD")
                        pb.wait(tensor, "V")
                    if i_s >= 2 and expb[i_s - 2] is not None:
                        pb.wait(tensor, "S", expb[i_s - 2])
                    for i2 in range(2):
                        kb = 2 * half + i2
                        if diag:
                            lhs = kdiag[:, 128 * kb:128 * (kb + 1)]
                        else:
                            r, mi = sup_addr(t_sup)
                            pos = 4 * r + mi
                            lhs = kc[:, SUP * pos + 128 * kb:SUP * pos + 128 * (kb + 1)]
                        i = tensor.matmul(sbank[:, 512 * i2:512 * (i2 + 1)], lhs, qrhs,
                                          start=True, stop=(not diag))
                        if diag:
                            i = tensor.matmul(sbank[:, 512 * i2:512 * (i2 + 1)], sb.ident.ap(),
                                              sb.masks.ap()[:, SUP * kb:SUP * (kb + 1)],
                                              start=False, stop=True)
                    qkb[i_s] = pb.bump(i, "T")
                    if m == 0:
                        lastqk[h] = qkb[i_s]
                    if i_s >= 1:
                        emit_pv(tensor, i_s - 1)
                pb.on("tensor", tq)

                def se(scalar, i_s=i_s, t_sup=t_sup, diag=diag, sbank=sbank,
                       ptile=ptile, m=m, expb=expb, qkb=qkb, pvb=pvb):
                    pb.wait(scalar, "T", qkb[i_s])
                    if i_s >= 2 and pvb[i_s - 2] is not None:
                        pb.wait(scalar, "T", pvb[i_s - 2])
                    col = bt_diag(m) if diag else bt_col(m, t_sup)
                    btab_src = sb.padtab.ap() if DBG_CONST_BIAS else sb.btab.ap()
                    i = scalar.activation(ptile, sbank, AF.Exp, scale=sb.cv(0.125),
                                          bias=btab_src[:, col:col + 1])
                    expb[i_s] = pb.bump(i, "S")
                pb.on("scalar", se)

            def tl(tensor, nsteps=nsteps, emit_pv=emit_pv):
                emit_pv(tensor, nsteps - 1)
            pb.on("tensor", tl)

            def vo(vector):
                pb.wait(vector, "T")
                vector.tensor_scalar_add(rr, pb.PSB[64:65, 0:512], 1e-30)
                i = vector.reciprocal(rr, rr)
                pb.bump(i, "V")
            pb.on("vector", vo)

            def to(tensor):
                pb.wait(tensor, "V")
                i = tensor.matmul(pb.PSB[0:64, 512:1024], ones64[:, 0:64],
                                  rr, start=True, stop=True)
                pb.bump(i, "T")
            pb.on("tensor", to)

            def vo2(vector, m=m, ch=ch, hh=hh, olast=olast):
                pb.wait(vector, "T")
                vector.tensor_copy(rlb[0:64, :], pb.PSB[0:64, 512:1024])
                if hh == 0:
                    dst = ostack[0:64, TPC * ch + SUP * m:TPC * ch + SUP * (m + 1)]
                else:
                    dst = obuf[0:64, :]
                i = vector.tensor_mul(dst, pb.PSB[0:64, 0:512], rlb[0:64, :])
                olast[0] = pb.bump(i, "V")
            pb.on("vector", vo2)

            if hh == 1:
                def gmove(gp, m=m, ch=ch):
                    pb.wait(gp, "V")
                    i = gp.dma_start(
                        out=ostack[64:128, TPC * ch + SUP * m:TPC * ch + SUP * (m + 1)],
                        in_=obuf[0:64, :])
                    pb.bump(i, "ST", 16)
                pb.on("gpsimd", gmove)

                def vguard3(vector):
                    pb.wait(vector, "ST")
                pb.on("vector", vguard3)

    pb.barrier()
    pb.barrier()
    load_w_bf16(pb, sb, io["wo"], 8, 1024, 0)
    bo = load_brow(pb, sb, io["bo"], 1024)
    pb.barrier()
    vlast = [None] * 4
    for tt in range(NT):
        for j in range(NCH):
            bank = (tt * NCH + j) % 4

            def t(tensor, tt=tt, j=j, bank=bank):
                if vlast[bank] is not None:
                    pb.wait(tensor, "V", vlast[bank])
                ps = pb.PSA[:, SUP * bank:SUP * (bank + 1)]
                for k in range(NCH):
                    tensor.matmul(ps, sb.WBUF.ap()[:, 1024 * k + 128 * j:1024 * k + 128 * (j + 1)],
                                  ostack[:, TPC * k + SUP * tt:TPC * k + SUP * (tt + 1)],
                                  start=(k == 0), stop=False)
                i = tensor.matmul(ps, bo[:, 128 * j:128 * (j + 1)], sb.onesr.ap(),
                                  start=False, stop=True)
                pb.bump(i, "T")
            pb.on("tensor", t)

            def v(vector, tt=tt, j=j, bank=bank):
                pb.wait(vector, "T")
                ps = pb.PSA[:, SUP * bank:SUP * (bank + 1)]
                xsl = sb.xs.ap()[:, TPC * j + SUP * tt:TPC * j + SUP * (tt + 1)]
                i = vector.tensor_add(xsl, xsl, ps)
                vlast[bank] = pb.bump(i, "V")
            pb.on("vector", v)


def emit_ffn1(pb, io, dr, sb):
    NT = TPC // SUP
    lnb = sb.BA.bf16()
    w2buf = sb.BB.bf16()[:, 0:8192]
    htile = sb.BC.bf16()[:, 0:4096]
    b2sb = sb.pv(8)

    def f0(sync):
        i = sync.dma_start(out=b2sb, in_=io["b2col"])
        pb.bump(i, "LD", 16)
    pb.on("sync", f0)

    def v0(vector):
        pb.wait(vector, "LD")
        vector.tensor_scalar_mul(b2sb, b2sb, 0.5)
        for k in range(NCH):
            i = vector.tensor_scalar_add(sb.xs.ap()[:, TPC * k:TPC * (k + 1)],
                                         sb.xs.ap()[:, TPC * k:TPC * (k + 1)],
                                         b2sb[:, k:k + 1])
        pb.bump(i, "V")
    pb.on("vector", v0)

    b1sl = sb.pool_row_at(1024, 1024)
    for ffq in range(4):
        def fw(sync, ffq=ffq):
            pb.wait(sync, "T")
            for f in range(8):
                F = 8 * ffq + f
                r, bl = F // 4, F % 4
                src = dr.w1tg.ap()[1024 * r:1024 * (r + 1), 128 * bl:128 * (bl + 1)]
                src = src.rearrange("(k p) c -> p k c", p=128)
                dst = sb.WBUF.ap().rearrange("p (k f c) -> p k f c", k=8, f=8)[:, :, f, :]
                i = sync.dma_start(out=dst, in_=src)
                pb.bump(i, "LD", 16)
                r2, bl2 = F // 4, F % 4
                src2 = dr.w2tg.ap()[512 * r2 + 128 * bl2:512 * r2 + 128 * (bl2 + 1), :]
                src2 = src2.rearrange("p (e c) -> p e c", c=128)
                dst2 = w2buf.rearrange("p (f e c) -> p f e c", f=8, e=8)[:, f, :, :]
                i = sync.dma_start(out=dst2, in_=src2)
                pb.bump(i, "LD", 16)
            i = sync.dma_start(out=b1sl, in_=io["b1row"][:, 1024 * ffq:1024 * (ffq + 1)])
            pb.bump(i, "LD", 16)
        pb.on("sync", fw)

        for tt in range(NT):
            slast = [None] * 4
            for f in range(8):
                bank = f % 4

                def t(tensor, ffq=ffq, tt=tt, f=f, bank=bank, slast=slast):
                    if ffq == 0 and tt == 0 and f == 0:
                        pb.wait(tensor, "ccw1", 1)
                        pb.wait(tensor, "ccw2", 1)
                        pb.wait(tensor, "V")
                    if f == 0 and tt == 0:
                        pb.wait(tensor, "LD")
                    if slast[bank] is not None:
                        pb.wait(tensor, "S", slast[bank])
                    ps = pb.PSA[:, SUP * bank:SUP * (bank + 1)]
                    for k in range(NCH):
                        tensor.matmul(ps, sb.WBUF.ap()[:, (k * 8 + f) * 128:(k * 8 + f) * 128 + 128],
                                      lnb[:, TPC * k + SUP * tt:TPC * k + SUP * (tt + 1)],
                                      start=(k == 0), stop=False)
                    i = tensor.matmul(ps, b1sl[:, 128 * f:128 * (f + 1)], sb.onesr.ap(),
                                      start=False, stop=True)
                    pb.bump(i, "T")
                pb.on("tensor", t)

                def s(scalar, tt=tt, f=f, bank=bank, slast=slast):
                    pb.wait(scalar, "T")
                    ps = pb.PSA[:, SUP * bank:SUP * (bank + 1)]
                    i = scalar.activation(htile[:, 512 * f:512 * (f + 1)], ps, AF.Gelu)
                    slast[bank] = pb.bump(i, "S")
                pb.on("scalar", s)

            vlast2 = [None] * 2
            for e in range(8):
                bank = e % 2

                def t2(tensor, tt=tt, e=e, bank=bank, vlast2=vlast2):
                    pb.wait(tensor, "S")
                    if vlast2[bank] is not None:
                        pb.wait(tensor, "V", vlast2[bank])
                    ps = pb.PSB[:, SUP * bank:SUP * (bank + 1)]
                    for f in range(8):
                        i = tensor.matmul(ps, w2buf[:, (f * 8 + e) * 128:(f * 8 + e) * 128 + 128],
                                          htile[:, 512 * f:512 * (f + 1)],
                                          start=(f == 0), stop=(f == 7))
                    pb.bump(i, "T")
                pb.on("tensor", t2)

                def v2(vector, tt=tt, e=e, bank=bank, vlast2=vlast2):
                    pb.wait(vector, "T")
                    ps = pb.PSB[:, SUP * bank:SUP * (bank + 1)]
                    xsl = sb.xs.ap()[:, TPC * e + SUP * tt:TPC * e + SUP * (tt + 1)]
                    i = vector.scalar_tensor_tensor(xsl, ps, 0.5, xsl, ALU.mult, ALU.add)
                    vlast2[bank] = pb.bump(i, "V")
                pb.on("vector", v2)


def emit_ssm(pb, io, dr, sb):
    nc = pb.nc
    lnb = sb.BA.bf16()

    emit_ln(pb, sb, sb.xs.ap(), lnb)

    def g0(gp):
        pb.wait(gp, "V")
        i = gp.dma_start(out=dr.ln5s.ap().rearrange("(k p) t -> p k t", p=128), in_=lnb.rearrange("p (k t) -> p k t", t=TPC))
        pb.bump(i, "ST", 16)
        i = gp.collective_compute("AllGather", ALU.bypass, replica_groups=dr.grp,
                                  ins=[dr.ln5s.ap()], outs=[dr.ln5g.ap()])
        pb.bump(i, "ccln5", 1)
    pb.on("gpsimd", g0)
    pb.barrier()
    sb.row_reset()

    prm = sb.pv(36)
    prm2 = sb.pv(8)
    xproj_sb = sb.pv(132 + 4)

    def fp(sync):
        i = sync.dma_start(out=prm[:, 0:16], in_=io["convw"])
        pb.bump(i, "LD", 16)
        for name, off in [("convb", 16), ("dtw", 20), ("dtb", 24), ("Dpar", 28), ("lng", 32)]:
            i = sync.dma_start(out=prm[:, off:off + 4], in_=io[name])
            pb.bump(i, "LD", 16)
        i = sync.dma_start(out=prm2[:, 0:4], in_=io["lnb"])
        pb.bump(i, "LD", 16)
        i = sync.dma_start(out=sb.Aneg.ap(), in_=io["Alog"])
        pb.bump(i, "LD", 16)
        i = sync.dma_start(out=xproj_sb[:, 0:132].rearrange("p (k c) -> p k c", c=33), in_=io["xprojT"].rearrange("(k p) c -> p k c", p=128))
        pb.bump(i, "LD", 16)
    pb.on("sync", fp)

    def sA(scalar):
        pb.wait(scalar, "LD")
        i = scalar.activation(sb.Aneg.ap(), sb.Aneg.ap(), AF.Exp)
        pb.bump(i, "S")
    pb.on("scalar", sA)

    def vA(vector):
        pb.wait(vector, "S")
        vector.tensor_scalar_mul(sb.Aneg.ap(), sb.Aneg.ap(), -1.0)
        i = vector.memset(sb.states.ap(), 0.0)
        pb.bump(i, "V")
    pb.on("vector", vA)

    load_w_bf16(pb, sb, io["inxbw"], 8, 512, 0)
    load_w_bf16(pb, sb, io["inzw"], 8, 512, 4096)
    pb.barrier()

    xbp = sb.BA.f32()
    xc = sb.BB.f32()
    rhs0 = sb.BD.bf16()[:, 0:4096]
    rhs1 = sb.BD.bf16()[:, 4096:8192]
    zev = sb.BC.f32()[:, 0:512]
    rhsb = [rhs0, rhs1]
    frb = {}
    for kx in range(KCH):
        mmb = {}
        sevb = {}
        for tt in range(16):
            r_tt, m_tt = sup_addr(tt)
            buf = rhsb[tt % 2]

            def fr(sync, kx=kx, tt=tt, buf=buf, r_tt=r_tt, m_tt=m_tt, mmb=mmb, frb=frb):
                if kx == 0 and tt == 0:
                    pb.wait(sync, "ccln5", 1)
                prev = mmb.get(tt - 2) if tt >= 2 else (frb.get(kx - 1) if kx >= 1 else None)
                if prev is not None:
                    pb.wait(sync, "T", prev)
                src3 = dr.ln5g.ap()[1024 * r_tt:1024 * (r_tt + 1),
                                    SUP * m_tt:SUP * (m_tt + 1)]
                i = sync.dma_start(out=buf.rearrange("p (k t) -> p k t", t=SUP),
                                   in_=src3.rearrange("(k p) t -> p k t", p=128))
                pb.bump(i, "LD", 16)
            pb.on("sync", fr)

            def tm(tensor, kx=kx, tt=tt, buf=buf, mmb=mmb, sevb=sevb, frb=frb):
                pb.wait(tensor, "LD")
                if tt >= 2 and (tt - 2) in sevb:
                    pb.wait(tensor, "S", sevb[tt - 2])
                psx = pb.PSA[:, SUP * (tt % 2):SUP * (tt % 2 + 1)]
                psz = pb.PSA[:, 1024 + SUP * (tt % 2):1024 + SUP * (tt % 2 + 1)]
                for kdm in range(NCH):
                    tensor.matmul(psx, sb.WBUF.ap()[:, (kdm * 4 + kx) * 128:(kdm * 4 + kx) * 128 + 128],
                                  buf[:, 512 * kdm:512 * (kdm + 1)], start=(kdm == 0), stop=(kdm == 7))
                    i = tensor.matmul(psz, sb.WBUF.ap()[:, 4096 + (kdm * 4 + kx) * 128:4096 + (kdm * 4 + kx) * 128 + 128],
                                      buf[:, 512 * kdm:512 * (kdm + 1)], start=(kdm == 0), stop=(kdm == 7))
                mmb[tt] = pb.bump(i, "T")
                if tt == 15:
                    frb[kx] = mmb[tt]
            pb.on("tensor", tm)

            def sev(scalar, kx=kx, tt=tt, sevb=sevb):
                pb.wait(scalar, "T")
                if tt == 0:
                    pb.wait(scalar, "ST")
                psx = pb.PSA[:, SUP * (tt % 2):SUP * (tt % 2 + 1)]
                psz = pb.PSA[:, 1024 + SUP * (tt % 2):1024 + SUP * (tt % 2 + 1)]
                scalar.copy(xbp[:, SUP * tt:SUP * (tt + 1)], psx)
                i = scalar.copy(zev, psz)
                sevb[tt] = pb.bump(i, "S")
            pb.on("scalar", sev)

            def gz(gp, kx=kx, tt=tt):
                pb.wait(gp, "S")
                i = gp.dma_start(out=dr.zst.ap()[kx, :, SUP * tt:SUP * (tt + 1)], in_=zev)
                pb.bump(i, "ST", 16)
            pb.on("gpsimd", gz)
            def sguard(scalar):
                pb.wait(scalar, "ST")
            pb.on("scalar", sguard)

        def vc(vector, kx=kx):
            pb.wait(vector, "S")
            vector.tensor_scalar(xc, xbp, prm[:, 4 * kx + 3:4 * kx + 4],
                                 prm[:, 16 + kx:17 + kx], ALU.mult, ALU.add)
            vector.scalar_tensor_tensor(xc[:, 1:L], xbp[:, 0:L - 1], prm[:, 4 * kx + 2:4 * kx + 3],
                                        xc[:, 1:L], ALU.mult, ALU.add)
            vector.scalar_tensor_tensor(xc[:, 2:L], xbp[:, 0:L - 2], prm[:, 4 * kx + 1:4 * kx + 2],
                                        xc[:, 2:L], ALU.mult, ALU.add)
            i = vector.scalar_tensor_tensor(xc[:, 3:L], xbp[:, 0:L - 3], prm[:, 4 * kx + 0:4 * kx + 1],
                                            xc[:, 3:L], ALU.mult, ALU.add)
            pb.bump(i, "V")
        pb.on("vector", vc)

        def sc(scalar, kx=kx):
            pb.wait(scalar, "V")
            i = scalar.activation(xbp, xc, AF.Silu)
            pb.bump(i, "S")
        pb.on("scalar", sc)

        def gx(gp, kx=kx):
            pb.wait(gp, "S")
            i = gp.dma_start(out=dr.xbcst.ap()[kx], in_=xbp)
            pb.bump(i, "ST", 16)
        pb.on("gpsimd", gx)

        xpe = sb.BC.f32()[0:33, 512:1024]
        for tt in range(16):
            def tp(tensor, kx=kx, tt=tt):
                pb.wait(tensor, "S")
                i = tensor.matmul(pb.PSB[0:33, 0:512], xproj_sb[:, 33 * kx:33 * (kx + 1)],
                                  xbp[:, SUP * tt:SUP * (tt + 1)], start=True, stop=True)
                pb.bump(i, "T")
            pb.on("tensor", tp)

            def sxe(scalar, kx=kx, tt=tt, xpe=xpe):
                pb.wait(scalar, "T")
                pb.wait(scalar, "ST")
                i = scalar.copy(xpe, pb.PSB[0:33, 0:512])
                pb.bump(i, "S")
            pb.on("scalar", sxe)

            def gp2(gp, kx=kx, tt=tt, xpe=xpe):
                pb.wait(gp, "S")
                i = gp.dma_start(out=dr.xpk.ap()[kx, :, SUP * tt:SUP * (tt + 1)], in_=xpe)
                pb.bump(i, "ST", 16)
            pb.on("gpsimd", gp2)

    pb.barrier()

    acc = sb.BC.f32()[0:33, 0:2048]
    ld = sb.BD.f32()[0:33, 0:2048]
    for lc in range(4):
        for kx in range(KCH):
            def fl(sync, lc=lc, kx=kx):
                pb.wait(sync, "V")
                i = sync.dma_start(out=ld, in_=dr.xpk.ap()[kx, :, TPC * lc:TPC * (lc + 1)])
                pb.bump(i, "LD", 16)
            pb.on("sync", fl)

            def va(vector, lc=lc, kx=kx):
                pb.wait(vector, "LD")
                if kx == 0:
                    i = vector.tensor_copy(acc, ld)
                else:
                    i = vector.tensor_add(acc, acc, ld)
                pb.bump(i, "V")
            pb.on("vector", va)

        def gs(gp, lc=lc):
            pb.wait(gp, "V")
            i = gp.dma_start(out=dr.xprojp.ap()[:, TPC * lc:TPC * (lc + 1)], in_=acc)
            pb.bump(i, "ST", 16)
        pb.on("gpsimd", gs)

        def vguard(vector):
            pb.wait(vector, "ST")
        pb.on("vector", vguard)

    def gar(gp):
        pb.wait(gp, "ST")
        i = gp.collective_compute("AllReduce", ALU.add, replica_groups=dr.grp,
                                  ins=[dr.xprojp.ap()], outs=[dr.xprojr.ap()])
        pb.bump(i, "ccxp", 1)
    pb.on("gpsimd", gar)
    pb.barrier()

    dtinrow = sb.pool_row_at(0, 2048)
    ubuf = sb.BB.f32()[:, 0:2048]
    rbuf = sb.BB.f32()[:, 2048:4096]
    abuf = sb.BA.f32()[:, 0:2048]
    for lc in range(4):
        def fd(sync, lc=lc):
            pb.wait(sync, "ccxp", 1)
            pb.wait(sync, "ST")
            pb.wait(sync, "V")
            i = sync.dma_start(out=dtinrow, in_=dr.xprojr.ap()[0:1, TPC * lc:TPC * (lc + 1)])
            pb.bump(i, "LD", 16)
        pb.on("sync", fd)

        def td(tensor, lc=lc):
            pb.wait(tensor, "LD")
            pb.wait(tensor, "V")
            for q in range(4):
                i = tensor.matmul(pb.PSA[:, 512 * q:512 * (q + 1)], sb.onesr.ap()[:, 0:128],
                                  dtinrow[:, 512 * q:512 * (q + 1)], start=True, stop=True)
            pb.bump(i, "T")
        pb.on("tensor", td)

        for kx in range(KCH):
            def vu(vector, lc=lc, kx=kx):
                pb.wait(vector, "T")
                pb.wait(vector, "ST")
                i = vector.tensor_scalar(ubuf, pb.PSA[:, 0:2048], prm[:, 20 + kx:21 + kx],
                                         prm[:, 24 + kx:25 + kx], ALU.mult, ALU.add)
                pb.bump(i, "V")
            pb.on("vector", vu)

            def vd0(vector, lc=lc, kx=kx):
                i = vector.tensor_scalar_max(rbuf, ubuf, 0.0)
                pb.bump(i, "V")
            pb.on("vector", vd0)

            def sd(scalar, lc=lc, kx=kx):
                pb.wait(scalar, "V")
                scalar.activation(abuf, ubuf, AF.Abs)
                scalar.activation(abuf, abuf, AF.Exp, scale=sb.cv(-1.0))
                i = scalar.activation(abuf, abuf, AF.Ln, bias=sb.cv(1.0))
                pb.bump(i, "S")
            pb.on("scalar", sd)

            def vd(vector, lc=lc, kx=kx):
                pb.wait(vector, "S")
                i = vector.tensor_add(ubuf, rbuf, abuf)
                pb.bump(i, "V")
            pb.on("vector", vd)

            def gd(gp, lc=lc, kx=kx):
                pb.wait(gp, "V")
                i = gp.dma_start(out=dr.dtst.ap()[kx, :, TPC * lc:TPC * (lc + 1)], in_=ubuf)
                pb.bump(i, "ST", 16)
            pb.on("gpsimd", gd)
    pb.barrier()

    LC = 1024
    xbct = [sb.BA.f32()[:, LC * k:LC * (k + 1)] for k in range(4)]
    dtt = [sb.BA.f32()[:, 4096 + LC * k:4096 + LC * (k + 1)] for k in range(4)]
    wt = [sb.BB.f32()[:, LC * k:LC * (k + 1)] for k in range(4)]
    yt = [sb.BB.f32()[:, 4096 + LC * k:4096 + LC * (k + 1)] for k in range(4)]
    dA0 = sb.BC.f32()[:, 0:1024]
    dA1 = sb.BC.f32()[:, 1024:2048]
    dAb = [dA0, dA1]
    xst = sb.BC.f32()[:, 2048:3072]
    ht = sb.BD.f32()[:, 2048:3072]
    hctb = [sb.BD.f32()[:, 0:1024], sb.BD.f32()[:, 3072:4096]]
    sqt = sb.BD.f32()[:, 1024:2048]
    rowB = sb.pool_row_at(0, 1024)
    rowC = sb.pool_row_at(1024, 1024)

    for lc in range(8):
        def fl2(sync, lc=lc):
            pb.wait(sync, "T")
            pb.wait(sync, "V")
            pb.wait(sync, "G2")
            pb.wait(sync, "ST")
            for k in range(4):
                i = sync.dma_start(out=xbct[k], in_=dr.xbcst.ap()[k, :, LC * lc:LC * (lc + 1)])
                pb.bump(i, "LD", 16)
                i = sync.dma_start(out=dtt[k], in_=dr.dtst.ap()[k, :, LC * lc:LC * (lc + 1)])
                pb.bump(i, "LD", 16)
        pb.on("sync", fl2)

        def vw(vector, lc=lc):
            pb.wait(vector, "LD")
            for k in range(4):
                vector.tensor_mul(wt[k], xbct[k], dtt[k])
                i = vector.tensor_scalar(yt[k], xbct[k], prm[:, 28 + k:29 + k], None, ALU.mult)
            pb.bump(i, "V")
        pb.on("vector", vw)

        for n in range(DST):
            def fr2(sync, lc=lc, n=n):
                pb.wait(sync, "T")
                i = sync.dma_start(out=rowB, in_=dr.xprojr.ap()[1 + n:2 + n, LC * lc:LC * (lc + 1)])
                pb.bump(i, "LD", 16)
                i = sync.dma_start(out=rowC, in_=dr.xprojr.ap()[17 + n:18 + n, LC * lc:LC * (lc + 1)])
                pb.bump(i, "LD", 16)
            pb.on("sync", fr2)

            def tb(tensor, lc=lc, n=n):
                pb.wait(tensor, "LD")
                pb.wait(tensor, "V")
                tensor.matmul(pb.PSA[:, 0:512], sb.onesr.ap()[:, 0:128], rowB[:, 0:512],
                              start=True, stop=True)
                tensor.matmul(pb.PSA[:, 512:1024], sb.onesr.ap()[:, 0:128], rowB[:, 512:1024],
                              start=True, stop=True)
                tensor.matmul(pb.PSA[:, 1024:1536], sb.onesr.ap()[:, 0:128], rowC[:, 0:512],
                              start=True, stop=True)
                i = tensor.matmul(pb.PSA[:, 1536:2048], sb.onesr.ap()[:, 0:128], rowC[:, 512:1024],
                                  start=True, stop=True)
                pb.bump(i, "T")
            pb.on("tensor", tb)

            vsb = [None] * 4
            for k in range(4):
                col = DST * k + n

                def sda(scalar, lc=lc, n=n, k=k, col=col, vsb=vsb):
                    if k == 0:
                        pb.wait(scalar, "LD")
                    if k >= 2 and vsb[k - 2] is not None:
                        pb.wait(scalar, "V", vsb[k - 2])
                    elif k < 2:
                        pb.wait(scalar, "V")
                    i = scalar.activation(dAb[k % 2], dtt[k], AF.Exp,
                                          scale=sb.Aneg.ap()[:, col:col + 1])
                    pb.bump(i, "S")
                pb.on("scalar", sda)

                def vs(vector, lc=lc, n=n, k=k, col=col, vsb=vsb):
                    pb.wait(vector, "T")
                    pb.wait(vector, "S")
                    if pb.cnt.get("G2", 0) >= 2:
                        pb.wait(vector, "G2", pb.cnt["G2"] - 1)
                    vector.tensor_mul(xst, wt[k], pb.PSA[:, 0:1024])
                    vector.tensor_tensor_scan(ht, dAb[k % 2], xst,
                                              sb.states.ap()[:, col:col + 1],
                                              ALU.mult, ALU.add)
                    vector.tensor_copy(sb.states.ap()[:, col:col + 1], ht[:, LC - 1:LC])
                    i = vector.tensor_mul(hctb[k % 2], ht, pb.PSA[:, 1024:2048])
                    vsb[k] = pb.bump(i, "V")
                pb.on("vector", vs)

                def gadd(gp, lc=lc, n=n, k=k, vsb=vsb):
                    pb.wait(gp, "V", vsb[k])
                    i = gp.tensor_add(yt[k], yt[k], hctb[k % 2])
                    pb.bump(i, "G2")
                pb.on("gpsimd", gadd)

        for k in range(4):
            def sy(scalar, lc=lc, k=k):
                pb.wait(scalar, "V")
                pb.wait(scalar, "T")
                pb.wait(scalar, "G2")
                i = scalar.activation(sqt, yt[k], AF.Square)
                pb.bump(i, "S")
            pb.on("scalar", sy)

            def ty(tensor, lc=lc, k=k):
                pb.wait(tensor, "S")
                for q in range(2):
                    tensor.matmul(pb.PSB[0:1, 512 * q:512 * (q + 1)], sb.onescol.ap(),
                                  yt[k][:, 512 * q:512 * (q + 1)], start=(k == 0), stop=(k == 3))
                    i = tensor.matmul(pb.PSB[0:1, 1024 + 512 * q:1024 + 512 * (q + 1)], sb.onescol.ap(),
                                      sqt[:, 512 * q:512 * (q + 1)], start=(k == 0), stop=(k == 3))
                pb.bump(i, "T")
            pb.on("tensor", ty)

            def gy(gp, lc=lc, k=k):
                pb.wait(gp, "V")
                i = gp.dma_start(out=dr.yst.ap()[k, :, LC * lc:LC * (lc + 1)], in_=yt[k])
                pb.bump(i, "ST", 16)
            pb.on("gpsimd", gy)

        statsrow = sb.vecs.ap()[0:1, 0:2048]

        def vst(vector, lc=lc):
            pb.wait(vector, "T")
            pb.wait(vector, "ST")
            vector.tensor_copy(statsrow[:, 0:1024], pb.PSB[0:1, 0:1024])
            i = vector.tensor_copy(statsrow[:, 1024:2048], pb.PSB[0:1, 1024:2048])
            pb.bump(i, "V")
        pb.on("vector", vst)

        def gst(gp, lc=lc, statsrow=statsrow):
            pb.wait(gp, "V")
            i = gp.dma_start(out=dr.statsp.ap()[0:1, LC * lc:LC * (lc + 1)], in_=statsrow[:, 0:1024])
            pb.bump(i, "ST", 16)
            i = gp.dma_start(out=dr.statsp.ap()[0:1, L + LC * lc:L + LC * (lc + 1)],
                             in_=statsrow[:, 1024:2048])
            pb.bump(i, "ST", 16)
        pb.on("gpsimd", gst)

        def tguard2(tensor):
            pb.wait(tensor, "V")
        pb.on("tensor", tguard2)

    def gar2(gp):
        pb.wait(gp, "ST")
        i = gp.collective_compute("AllReduce", ALU.add, replica_groups=dr.grp,
                                  ins=[dr.statsp.ap()], outs=[dr.statsr.ap()])
        pb.bump(i, "ccst", 1)
    pb.on("gpsimd", gar2)
    pb.barrier()
    sb.row_reset()

    load_w_bf16(pb, sb, io["outwT"], 4, 1024, 0)
    pb.barrier()

    rsum = sb.pool_row_at(0, 2048)
    yti = [sb.BA.f32()[:, 512 * k:512 * (k + 1)] for k in range(4)]
    zti = [sb.BA.f32()[:, 2048 + 512 * k:2048 + 512 * (k + 1)] for k in range(4)]
    yzb = [sb.BB.bf16()[:, 512 * k:512 * (k + 1)] for k in range(4)]
    szt = sb.BA.f32()[:, 4096:4608]
    ynt = sb.BA.f32()[:, 4608:5120]
    oev = sb.BA.f32()[:, 5120:5632]
    mrow = sb.vecs.ap()[0:1, 0:2048]
    vrow = sb.rows2.ap()

    for lc in range(4):
        def fs(sync, lc=lc):
            pb.wait(sync, "ccst", 1)
            pb.wait(sync, "V")
            pb.wait(sync, "T")
            i = sync.dma_start(out=rsum, in_=dr.statsr.ap()[0:1, TPC * lc:TPC * (lc + 1)])
            pb.bump(i, "LD", 16)
            i = sync.dma_start(out=vrow, in_=dr.statsr.ap()[0:1, L + TPC * lc:L + TPC * (lc + 1)])
            pb.bump(i, "LD", 16)
        pb.on("sync", fs)

        def vr(vector, lc=lc):
            pb.wait(vector, "LD")
            vector.tensor_scalar_mul(mrow, rsum, 1.0 / DIN)
            vector.tensor_scalar_mul(vrow, vrow, 1.0 / DIN)
            vector.tensor_mul(rsum, mrow, mrow)
            i = vector.tensor_sub(vrow, vrow, rsum)
            pb.bump(i, "V")
        pb.on("vector", vr)

        def sr(scalar, lc=lc):
            pb.wait(scalar, "V")
            scalar.activation(vrow, vrow, AF.Ln, bias=sb.cv(1e-5, 1))
            i = scalar.activation(vrow, vrow, AF.Exp, scale=sb.cv(-0.5, 1))
            pb.bump(i, "S")
        pb.on("scalar", sr)

        for tile in range(4):
            s_glob = 4 * lc + tile
            r_s, m_s = sup_addr(s_glob)

            def tbm(tensor, lc=lc, tile=tile):
                pb.wait(tensor, "S")
                pb.wait(tensor, "V")
                tensor.matmul(pb.PSA[:, 0:512], sb.onesr.ap()[:, 0:128],
                              mrow[:, 512 * tile:512 * (tile + 1)], start=True, stop=True)
                i = tensor.matmul(pb.PSA[:, 512:1024], sb.onesr.ap()[:, 0:128],
                                  vrow[:, 512 * tile:512 * (tile + 1)], start=True, stop=True)
                pb.bump(i, "T")
            pb.on("tensor", tbm)

            for k in range(KCH):
                def fy(sync, lc=lc, tile=tile, k=k):
                    pb.wait(sync, "V")
                    i = sync.dma_start(out=yti[k], in_=dr.yst.ap()[k, :, TPC * lc + 512 * tile:TPC * lc + 512 * (tile + 1)])
                    pb.bump(i, "LD", 16)
                    i = sync.dma_start(out=zti[k], in_=dr.zst.ap()[k, :, TPC * lc + 512 * tile:TPC * lc + 512 * (tile + 1)])
                    pb.bump(i, "LD", 16)
                pb.on("sync", fy)

                def sz(scalar, lc=lc, tile=tile, k=k):
                    pb.wait(scalar, "LD")
                    pb.wait(scalar, "V")
                    i = scalar.activation(szt, zti[k], AF.Silu)
                    pb.bump(i, "S")
                pb.on("scalar", sz)

                def vn(vector, lc=lc, tile=tile, k=k):
                    pb.wait(vector, "T")
                    pb.wait(vector, "S")
                    vector.tensor_sub(ynt, yti[k], pb.PSA[:, 0:512])
                    vector.tensor_mul(ynt, ynt, pb.PSA[:, 512:1024])
                    vector.tensor_scalar(ynt, ynt, prm[:, 32 + k:33 + k],
                                         prm2[:, k:k + 1], ALU.mult, ALU.add)
                    i = vector.tensor_mul(yzb[k], ynt, szt)
                    pb.bump(i, "V")
                pb.on("vector", vn)

            for e in range(8):
                def te(tensor, lc=lc, tile=tile, e=e, r_s=r_s, m_s=m_s):
                    pb.wait(tensor, "V")
                    pb.wait(tensor, "S")
                    ps = pb.PSB[:, 512 * (e % 2):512 * (e % 2) + 512]
                    for k in range(KCH):
                        i = tensor.matmul(ps, sb.WBUF.ap()[:, (k * 8 + e) * 128:(k * 8 + e) * 128 + 128],
                                          yzb[k], start=(k == 0), stop=(k == 3))
                    pb.bump(i, "T")
                pb.on("tensor", te)

                def so(scalar, lc=lc, tile=tile, e=e):
                    pb.wait(scalar, "T")
                    pb.wait(scalar, "ST")
                    ps = pb.PSB[:, 512 * (e % 2):512 * (e % 2) + 512]
                    i = scalar.copy(oev, ps)
                    pb.bump(i, "S")
                pb.on("scalar", so)

                def go(gp, lc=lc, tile=tile, e=e, r_s=r_s, m_s=m_s):
                    pb.wait(gp, "S")
                    i = gp.dma_start(out=dr.poutb.ap()[r_s, 128 * e:128 * (e + 1),
                                                       SUP * m_s:SUP * (m_s + 1)], in_=oev)
                    pb.bump(i, "ST", 16)
                pb.on("gpsimd", go)

    def grs(gp):
        pb.wait(gp, "ST")
        i = gp.collective_compute("ReduceScatter", ALU.add, replica_groups=dr.grp,
                                  ins=[dr.poutb.ap().rearrange("r e t -> (r e) t")],
                                  outs=[dr.prsout.ap()])
        pb.bump(i, "ccrs", 1)
    pb.on("gpsimd", grs)
    pb.barrier()

    for half in range(2):
        buf = (sb.BA.f32() if half == 0 else sb.BB.f32())

        def fres(sync, half=half, buf=buf):
            pb.wait(sync, "ccrs", 1)
            i = sync.dma_start(out=buf.rearrange("p (k t) -> p k t", t=TPC),
                               in_=dr.prsout.ap()[512 * half:512 * (half + 1), :]
                               .rearrange("(k p) t -> p k t", p=128))
            pb.bump(i, "LD", 16)
        pb.on("sync", fres)

        def vres(vector, half=half, buf=buf):
            pb.wait(vector, "LD")
            for k in range(4):
                kk = 4 * half + k
                i = vector.tensor_add(sb.xs.ap()[:, TPC * kk:TPC * (kk + 1)],
                                      sb.xs.ap()[:, TPC * kk:TPC * (kk + 1)],
                                      buf[:, TPC * k:TPC * (k + 1)])
            pb.bump(i, "V")
        pb.on("vector", vres)


def emit_weff_load(pb, dr, sb):
    beff = sb.pool_row_at(1024, 1024)

    def f(sync):
        pb.wait(sync, "ccweff", 1)
        pb.wait(sync, "ST")
        pb.wait(sync, "T")
        pb.wait(sync, "V")
        for r in range(8):
            i = sync.dma_start(out=sb.WBUF.ap()[:, 1024 * r:1024 * (r + 1)],
                               in_=dr.weffg.ap()[128 * r:128 * (r + 1), :])
            pb.bump(i, "LD", 16)
        i = sync.dma_start(out=beff, in_=dr.beffd.ap())
        pb.bump(i, "LD", 16)
    pb.on("sync", f)
    return beff


IO_SPECS = {
    "xin": ([D, TPC], F32), "w0slice": ([8, 128, 1024], F32), "b0all": ([8, 1024], F32),
    "logits": ([1, 8], F32), "uu": ([1, 8], F32),
    "wq": ([1024, 256], F32), "bq": ([1, 256], F32),
    "wqc": ([256, 896], F32), "bqc": ([1, 896], F32),
    "wqr": ([1024, 128], F32), "bqr": ([1, 128], F32),
    "wkv": ([1024, 256], F32), "bkv": ([1, 256], F32),
    "wkr": ([1024, 128], F32), "bkr": ([1, 128], F32),
    "wkc": ([256, 896], F32), "bkc": ([1, 896], F32),
    "wv": ([256, 1024], F32), "bv": ([1, 1024], F32),
    "wo": ([1024, 1024], F32), "bo": ([1, 1024], F32),
    "cosT": ([128, TPC], F32), "sinT": ([128, TPC], F32),
    "masks": ([4, 128, SUP], BF16), "padtab": ([128, NBT], F32),
    "ident": ([128, 128], BF16), "rotT": ([128, 128], F32),
    "w1Ts": ([1024, 512], F32), "w2Ts": ([512, 1024], F32),
    "b1row": ([1, 4096], F32), "b2col": ([128, 8], F32),
    "inxbw": ([1024, 512], F32), "inzw": ([1024, 512], F32),
    "xprojT": ([512, 33], F32),
    "convw": ([128, 16], F32), "convb": ([128, 4], F32),
    "Alog": ([128, 64], F32), "Dpar": ([128, 4], F32),
    "dtw": ([128, 4], F32), "dtb": ([128, 4], F32),
    "lng": ([128, 4], F32), "lnb": ([128, 4], F32),
    "outwT": ([512, 1024], F32),
}


def build_program(upto=6):
    import contextlib

    nc = bass.Bass("TRN2", target_bir_lowering=False, dynamic_dma_scratch_size=16384)
    stack = contextlib.ExitStack()
    io = {}
    for name, (shape, dt) in IO_SPECS.items():
        io[name] = nc.dram_tensor(name, shape, dt, kind="ExternalInput").ap()
    xout = nc.dram_tensor("xout", [D, TPC], F32, kind="ExternalOutput").ap()

    sb = SB(nc)
    dr = DRT(nc, [[0, 1, 2, 3], [4, 5, 6, 7]])
    pb = PB(nc, stack)
    pb.PSA = nc.alloc_psum_tensor("PSA", [128, 2048], F32).ap()
    pb.PSB = nc.alloc_psum_tensor("PSB", [128, 2048], F32).ap()

    emit_consts(pb, io, sb)
    emit_xin(pb, io, sb)
    emit_expert_prep(pb, io, dr, sb)
    pb.barrier()
    emit_ternarize(pb, io, dr, sb)
    pb.barrier()
    sb.row_reset()

    if upto >= 1:
        emit_ln(pb, sb, sb.xs.ap(), sb.BA.bf16())
        pb.barrier()
        beff = emit_weff_load(pb, dr, sb)
        pb.barrier()
        emit_ffn0(pb, sb, sb.BA.bf16(), beff)
        pb.barrier()

    if upto >= 2:
        emit_mla(pb, io, dr, sb)
        emit_attention(pb, io, dr, sb)
        pb.barrier()
        sb.row_reset()

    if upto >= 3:
        sb.row_reset()
        emit_ln(pb, sb, sb.xs.ap(), sb.BA.bf16())
        pb.barrier()
        beff = emit_weff_load(pb, dr, sb)
        pb.barrier()
        emit_ffn0(pb, sb, sb.BA.bf16(), beff)
        pb.barrier()

    if upto >= 4:
        emit_ln(pb, sb, sb.xs.ap(), sb.BA.bf16())
        pb.barrier()
        sb.row_reset()
        emit_ffn1(pb, io, dr, sb)
        pb.barrier()
        sb.row_reset()

    if upto >= 5:
        emit_ssm(pb, io, dr, sb)
        pb.barrier()
        sb.row_reset()

    if upto >= 6:
        emit_ln(pb, sb, sb.xs.ap(), sb.BA.bf16())
        pb.barrier()
        emit_ffn1(pb, io, dr, sb)
        pb.barrier()

    snap = dict(pb.cnt)

    def fo(gp):
        for name, v in snap.items():
            if v > 0:
                gp.wait_ge(pb.sems[name], v)
        i = gp.dma_start(out=xout.rearrange("(k p) t -> p k t", p=128), in_=sb.xs.ap().rearrange("p (k t) -> p k t", t=TPC))
        pb.bump(i, "ST", 16)
        pb.wait(gp, "ST")
    pb.on("gpsimd", fo)

    stack.close()
    return nc


DBG_NORM = None
DBG_CONST_BIAS = False
DBG_ROPE_BARRIER = False
_NC_CACHE = {}


def kernel(**inputs):
    maps = make_inputs(inputs)
    upto = int(_NC_CACHE.get("upto", 6))
    if _NC_CACHE.get("nc_upto") != upto:
        _NC_CACHE["nc"] = build_program(upto)
        _NC_CACHE["nc_upto"] = upto
    nc = _NC_CACHE["nc"]
    res = run_bass_kernel_spmd(nc, maps, core_ids=list(range(8)))
    return assemble_output(res.results)


# version 14 (final)
# speedup vs baseline: 1.3042x; 1.0738x over previous
import math

import numpy as np

try:
    import concourse.bass as bass
    import concourse.mybir as mybir
    from concourse.bass_utils import run_bass_kernel_spmd
except ImportError:
    import sys

    sys.path.insert(0, "/opt/trn_rl_repo")
    import concourse.bass as bass
    import concourse.mybir as mybir
    from concourse.bass_utils import run_bass_kernel_spmd

import ml_dtypes

F32 = mybir.dt.float32
BF16 = mybir.dt.bfloat16
AF = mybir.ActivationFunctionType
ALU = mybir.AluOpType
AX = mybir.AxisListType
BF16NP = ml_dtypes.bfloat16

B = 2
L = 8192
D = 1024
TPC = 2048
NCH = 8
SUP = 512
NSUP = 16
PAT = [0, 1, 2, 3, 3, 2, 1, 0, 0, 1, 2, 3, 3, 2, 1, 0]
SLOTS = [[s for s in range(NSUP) if PAT[s] == p] for p in range(4)]
LMAX = [max(SLOTS[p][m] + 1 for p in range(4)) for m in range(4)]
_BT_OFF = [0]
for m in range(4):
    _BT_OFF.append(_BT_OFF[-1] + (LMAX[m] - 1))
NBT_NONDIAG = _BT_OFF[-1]
NBT = NBT_NONDIAG + 4


def bt_col(m, t):
    return _BT_OFF[m] + t


def bt_diag(m):
    return NBT_NONDIAG + m


def sup_addr(s):
    r = PAT[s]
    return r, SLOTS[r].index(s)


H = 16
HD = 64
DC = 256
DR = 128
EH = 896
DIN = 2048
DST = 16
DCONV = 4
CHPC = 512
KCH = 4
FF = 4096
LN_EPS = 1e-5
MAGIC = 12582912.0


def _rope_tables():
    inv = 1.0 / 10000.0 ** (np.arange(0, DR, 2, dtype=np.float64) / DR)
    t = np.arange(L, dtype=np.float64)
    fr = t[:, None] * inv[None, :]
    emb = np.concatenate([fr, fr], axis=1)
    return (np.cos(emb).T.astype(np.float32).copy(),
            np.sin(emb).T.astype(np.float32).copy())


def _diag_masks():
    out = np.zeros((4, 128, SUP), dtype=np.float32)
    for i in range(4):
        kk = 128 * i + np.arange(128)[:, None]
        qq = np.arange(SUP)[None, :]
        out[i] = np.where(kk <= qq, 0.0, -1e9).astype(np.float32)
    return out


def _padtab(p):
    tab = np.zeros((128, NBT), dtype=np.float32)
    for m in range(4):
        s_mine = SLOTS[p][m]
        for t in range(LMAX[m] - 1):
            if t >= s_mine:
                tab[:, bt_col(m, t)] = -1e9
    return tab


def _tokens_of(p):
    idx = []
    for m in range(4):
        s = SLOTS[p][m]
        idx.extend(range(SUP * s, SUP * s + SUP))
    return np.array(idx, dtype=np.int64)


def _rot_lhsT():
    R = np.zeros((DR, DR), dtype=np.float32)
    for i in range(64):
        R[i, i + 64] = -1.0
        R[i + 64, i] = 1.0
    return np.ascontiguousarray(R.T)


def make_inputs(inp):
    cosT, sinT = _rope_tables()
    masks = _diag_masks()
    rotT = _rot_lhsT()
    ident = np.eye(128, dtype=np.float32)
    x = np.asarray(inp["L0_attn_x"])

    def T(a):
        return np.ascontiguousarray(np.asarray(a).T)

    def _pp(a):
        K_, c = a.shape[0] // 128, a.shape[1]
        return np.ascontiguousarray(a.reshape(K_, 128, c).transpose(1, 0, 2).reshape(128, K_ * c))

    w0 = np.asarray(inp["L0_ffn_W"])
    maps = []
    for c in range(8):
        b, p = c // 4, c % 4
        tok = _tokens_of(p)
        m = {
            "xin": np.ascontiguousarray(x[b][tok].T),
            "w0slice": np.ascontiguousarray(w0[:, 128 * c:128 * (c + 1), :]),
            "b0all": np.asarray(inp["L0_ffn_b"]),
            "logits": np.asarray(inp["L0_ffn_logits"]).reshape(1, 8),
            "uu": np.asarray(inp["L0_ffn_u"]).reshape(1, 8),
            "wq": np.asarray(inp["L0_attn_wq"]),
            "bq": np.asarray(inp["L0_attn_bq"]).reshape(1, -1),
            "wqc": np.asarray(inp["L0_attn_wqc"]),
            "bqc": np.asarray(inp["L0_attn_bqc"]).reshape(1, -1),
            "wqr": np.asarray(inp["L0_attn_wqr"]),
            "bqr": np.asarray(inp["L0_attn_bqr"]).reshape(1, -1),
            "wkv": np.asarray(inp["L0_attn_wkv"]),
            "bkv": np.asarray(inp["L0_attn_bkv"]).reshape(1, -1),
            "wkr": np.asarray(inp["L0_attn_wkr"]),
            "bkr": np.asarray(inp["L0_attn_bkr"]).reshape(1, -1),
            "wkc": np.asarray(inp["L0_attn_wkc"]),
            "bkc": np.asarray(inp["L0_attn_bkc"]).reshape(1, -1),
            "wv": np.asarray(inp["L0_attn_wv"]),
            "bv": np.asarray(inp["L0_attn_bv"]).reshape(1, -1),
            "wo": np.asarray(inp["L0_attn_wo"]),
            "bo": np.asarray(inp["L0_attn_bo"]).reshape(1, -1),
            "cosT": np.ascontiguousarray(cosT[:, tok]),
            "sinT": np.ascontiguousarray(sinT[:, tok]),
            "masks": masks.astype(BF16NP),
            "padtab": _padtab(p),
            "ident": ident.astype(BF16NP),
            "rotT": rotT,
            "w1Ts": np.ascontiguousarray(T(inp["L1_ffn_W1"])[:, 512 * c:512 * (c + 1)]),
            "w2Ts": np.ascontiguousarray(T(inp["L1_ffn_W2"])[512 * c:512 * (c + 1), :]),
            "b1row": np.asarray(inp["L1_ffn_b1"]).reshape(1, -1),
            "b2col": _pp(np.asarray(inp["L1_ffn_b2"]).reshape(-1, 1)),
            "inxbw": np.ascontiguousarray(T(inp["L1_attn_in_proj_w"])[:, 512 * p:512 * (p + 1)]),
            "inzw": np.ascontiguousarray(T(inp["L1_attn_in_proj_w"])[:, 2048 + 512 * p:2048 + 512 * (p + 1)]),
            "xprojT": np.ascontiguousarray(T(inp["L1_attn_x_proj_w"])[512 * p:512 * (p + 1), :]),
            "convw": _pp(np.asarray(inp["L1_attn_conv_w"])[512 * p:512 * (p + 1), :]),
            "convb": _pp(np.asarray(inp["L1_attn_conv_b"])[512 * p:512 * (p + 1)].reshape(-1, 1)),
            "Alog": _pp(np.asarray(inp["L1_attn_A_log"])[512 * p:512 * (p + 1), :]),
            "Dpar": _pp(np.asarray(inp["L1_attn_D_param"])[512 * p:512 * (p + 1)].reshape(-1, 1)),
            "dtw": _pp(np.asarray(inp["L1_attn_dt_w"])[512 * p:512 * (p + 1), :]),
            "dtb": _pp(np.asarray(inp["L1_attn_dt_b"])[512 * p:512 * (p + 1)].reshape(-1, 1)),
            "lng": _pp(np.asarray(inp["L1_attn_ln_g"])[512 * p:512 * (p + 1)].reshape(-1, 1)),
            "lnb": _pp(np.asarray(inp["L1_attn_ln_b"])[512 * p:512 * (p + 1)].reshape(-1, 1)),
            "outwT": np.ascontiguousarray(T(inp["L1_attn_out_w"])[512 * p:512 * (p + 1), :]),
        }
        maps.append(m)
    return maps


def assemble_output(results):
    out = np.empty((B, L, D), dtype=np.float32)
    for c in range(8):
        b, p = c // 4, c % 4
        tok = _tokens_of(p)
        out[b][tok] = np.asarray(results[c]["xout"]).T
    return out


_EMIT_OPS = {
    "tensor_copy", "tensor_tensor", "tensor_add", "tensor_sub", "tensor_mul",
    "tensor_max", "tensor_scalar", "tensor_single_scalar", "tensor_scalar_add",
    "tensor_scalar_sub", "tensor_scalar_mul", "tensor_scalar_max",
    "tensor_scalar_min", "tensor_relu", "tensor_reduce", "reduce_max",
    "reduce_sum", "scalar_tensor_tensor", "tensor_tensor_scan", "memset",
    "memzero", "reciprocal", "select", "copy_predicated", "activation",
    "copy", "add", "mul", "square", "sqrt", "sign", "bn_stats", "bn_aggr",
    "transpose", "tensor_tensor_reduce",
}


class _SyncedEng:

    def __init__(self, eng, pb, semname):
        self._eng = eng
        self._pb = pb
        self._sem = semname

    def __getattr__(self, a):
        f = getattr(self._eng, a)
        if a not in _EMIT_OPS:
            return f
        pb, sem = self._pb, self._sem

        def wrapped(*args, **kw):
            s = pb.sem(sem)
            big = False
            try:
                a0 = args[0]
                big = a0.free_size() >= 256
            except Exception:
                big = False
            if pb.cnt[sem] > 0 and not big:
                self._eng.wait_ge(s, pb.cnt[sem])
            inst = f(*args, **kw)
            pb.cnt[sem] += 1
            inst.then_inc(s, 1)
            return inst
        return wrapped


class PB:

    ENGS = ("tensor", "vector", "scalar", "gpsimd", "sync")

    def __init__(self, nc, stack):
        self.nc = nc
        self.stack = stack
        self.block = stack.enter_context(nc.Block())
        self.sems = {}
        self.cnt = {}

    def sem(self, name):
        if name not in self.sems:
            self.sems[name] = self.stack.enter_context(self.nc.semaphore(name))
            self.cnt[name] = 0
        return self.sems[name]

    def bump(self, inst, name, amt=1):
        if name in ("V", "S"):
            self.sem(name)
            return self.cnt[name]
        s = self.sem(name)
        self.cnt[name] += amt
        inst.then_inc(s, amt)
        return self.cnt[name]

    def val(self, name):
        self.sem(name)
        return self.cnt[name]

    def wait(self, eng, name, v=None):
        s = self.sem(name)
        eng.wait_ge(s, self.cnt[name] if v is None else v)

    def on(self, which, fn):
        if which == "vector":
            getattr(self.block, which)(lambda e: fn(_SyncedEng(e, self, "V")))
        elif which == "scalar":
            getattr(self.block, which)(lambda e: fn(_SyncedEng(e, self, "S")))
        else:
            getattr(self.block, which)(fn)

    def barrier(self):
        snap = dict(self.cnt)

        def mk(eng):
            def f(e):
                for name, v in snap.items():
                    if v > 0:
                        e.wait_ge(self.sems[name], v)
            return f
        for w in self.ENGS:
            self.on(w, mk(w))


def ceil_div(a, b):
    return (a + b - 1) // b



def emit_consts(pb, io, sb):
    def f(sync):
        for a4 in range(4):
            i = sync.dma_start(out=sb.masks.ap()[:, SUP * a4:SUP * (a4 + 1)],
                               in_=io["masks"][a4])
            pb.bump(i, "LD", 16)
        for dst, src in [
            (sb.padtab.ap(), io["padtab"]), (sb.ident.ap(), io["ident"]),
            (sb.rotT.ap(), io["rotT"]),
        ]:
            i = sync.dma_start(out=dst, in_=src)
            pb.bump(i, "LD", 16)
    pb.on("sync", f)

    def g(vector):
        vector.memset(sb.onesr.ap(), 1.0)
        vector.memset(sb.onescol.ap(), 1.0)
        for val, col in sb.CVALS.items():
            i = vector.memset(sb.cpool.ap()[:, col:col + 1], val)
        pb.bump(i, "V")
    pb.on("vector", g)


def emit_xin(pb, io, sb):
    def f(sync):
        i = sync.dma_start(out=sb.xs.ap().rearrange("p (k t) -> p k t", t=TPC), in_=io["xin"].rearrange("(k p) t -> p k t", p=128))
        pb.bump(i, "LD", 16)
    pb.on("sync", f)


def emit_expert_prep(pb, io, dr, sb):
    nc = pb.nc
    lrow = sb.pool_row(8)
    urow = sb.pool_row(8)
    grow = sb.pool_row(8)
    srow = sb.pool_row(8)
    ohrow = sb.pool_row(8)
    mx = sb.pool_row(1)
    negmx = sb.pool_row(1)
    lse = sb.pool_row(1)
    shift = sb.pool_row(1)
    b0sb = sb.BC.f32()[0:8, 2048:3072]

    def f(sync):
        i = sync.dma_start(out=lrow, in_=io["logits"])
        pb.bump(i, "LD", 16)
        i = sync.dma_start(out=urow, in_=io["uu"])
        pb.bump(i, "LD", 16)
        i = sync.dma_start(out=b0sb, in_=io["b0all"])
        pb.bump(i, "LD", 16)
    pb.on("sync", f)

    def v(vector):
        pb.wait(vector, "LD")
        i = vector.reduce_max(mx, lrow, AX.X, negate=False)
        i = vector.tensor_scalar_mul(negmx, mx, -1.0)
        pb.bump(i, "V")
    pb.on("vector", v)

    erow = sb.pool_row(8)
    sesum = sb.pool_row(1)

    def s(scalar):
        pb.wait(scalar, "V")
        scalar.activation(erow, lrow, AF.Exp, bias=negmx, accum_out=sesum)
        i = scalar.activation(lse, sesum, AF.Ln)
        scalar.activation(grow, urow, AF.Ln)
        pb.bump(i, "S")
    pb.on("scalar", s)

    def v2(vector):
        pb.wait(vector, "S")
        i = vector.tensor_scalar_mul(grow, grow, -1.0)
        pb.bump(i, "V")
    pb.on("vector", v2)

    def s2(scalar):
        pb.wait(scalar, "V")
        i = scalar.activation(grow, grow, AF.Ln)
        pb.bump(i, "S")
    pb.on("scalar", s2)

    mxs = sb.pool_row(1)

    def v3(vector):
        pb.wait(vector, "S")
        vector.tensor_sub(srow, lrow, grow)
        vector.tensor_add(shift, mx, lse)
        vector.tensor_scalar_mul(shift, shift, -1.0)
        vector.tensor_scalar_add(srow, srow, shift)
        vector.reduce_max(mxs, srow, AX.X)
        i = vector.tensor_single_scalar(ohrow, srow, mxs, ALU.is_ge)
        pb.bump(i, "V")
    pb.on("vector", v3)

    def g1(gp):
        pb.wait(gp, "V")
        i = gp.dma_start(out=dr.ohd.ap(), in_=ohrow)
        pb.bump(i, "ST", 16)
    pb.on("gpsimd", g1)

    ohcol = sb.ohcol
    ohb = sb.ohb

    def f2(sync):
        pb.wait(sync, "ST")
        i = sync.dma_start(out=ohcol.ap(), in_=dr.ohd.ap().rearrange("a b -> b a"))
        pb.bump(i, "LD", 16)
    pb.on("sync", f2)

    def t1(tensor):
        pb.wait(tensor, "LD")
        pb.wait(tensor, "V")
        tensor.matmul(pb.PSA[0:1, 0:512], ohcol.ap(), b0sb[:, 0:512], start=True, stop=True)
        i = tensor.matmul(pb.PSA[0:1, 512:1024], ohcol.ap(), b0sb[:, 512:1024], start=True, stop=True)
        i = tensor.matmul(pb.PSB[:, 0:8], sb.onesr.ap()[:, 0:128], ohrow, start=True, stop=True)
        pb.bump(i, "T")
    pb.on("tensor", t1)

    beffr = sb.pool_row_at(1024, 1024)

    def v4(vector):
        pb.wait(vector, "T")
        vector.tensor_copy(beffr, pb.PSA[0:1, 0:1024])
        i = vector.tensor_copy(ohb.ap(), pb.PSB[:, 0:8])
        pb.bump(i, "V")
    pb.on("vector", v4)

    def gbe(gp):
        pb.wait(gp, "V")
        i = gp.dma_start(out=dr.beffd.ap(), in_=beffr)
        pb.bump(i, "ST", 16)
    pb.on("gpsimd", gbe)

    wstage = sb.BC.f32()
    acc = wstage[:, 1024:2048]
    for i8 in range(8):
        st = wstage[:, 0:1024]

        def fl(sync, i8=i8, st=st):
            if i8 >= 1:
                pb.wait(sync, "V")
            i = sync.dma_start(out=st, in_=io["w0slice"][i8])
            pb.bump(i, "LD", 16)
        pb.on("sync", fl)

        def vv(vector, i8=i8, st=st):
            pb.wait(vector, "LD")
            if i8 == 0:
                i = vector.tensor_scalar(acc, st, ohb.ap()[:, 0:1], None, ALU.mult)
            else:
                i = vector.scalar_tensor_tensor(acc, st, ohb.ap()[:, i8:i8 + 1], acc,
                                                ALU.mult, ALU.add)
            pb.bump(i, "V")
        pb.on("vector", vv)

    weffbf = sb.BD.bf16()[:, 0:1024]

    def v5(vector):
        i = vector.tensor_copy(weffbf, acc)
        pb.bump(i, "V")
    pb.on("vector", v5)

    def g2(gp):
        pb.wait(gp, "V")
        i = gp.dma_start(out=dr.weffs.ap(), in_=weffbf)
        pb.bump(i, "ST", 16)
        i = gp.collective_compute(
            "AllGather", ALU.bypass, replica_groups=[list(range(8))],
            ins=[dr.weffs.ap()], outs=[dr.weffg.ap()])
        pb.bump(i, "ccweff", 1)
    pb.on("gpsimd", g2)


def emit_ternarize(pb, io, dr, sb, stop=99):
    w1st = sb.BA.f32()[:, 0:4096]
    w2st = sb.BB.f32()[:, 0:4096]
    asum = sb.pool_vec(4)
    s1p = sb.s1p

    def f(sync):
        i = sync.dma_start(out=w1st.rearrange("p (k c) -> p k c", c=512), in_=io["w1Ts"].rearrange("(k p) c -> p k c", p=128))
        pb.bump(i, "LD", 16)
        i = sync.dma_start(out=w2st.rearrange("p (k c) -> p k c", c=1024), in_=io["w2Ts"].rearrange("(k p) c -> p k c", p=128))
        pb.bump(i, "LD", 16)
    pb.on("sync", f)

    if stop <= 1:
        return

    def v(vector):
        pb.wait(vector, "LD")
        vector.memset(s1p.ap(), 0.0)
        vector.tensor_reduce(asum[:, 0:1], w1st, AX.X, ALU.add, apply_absolute_value=True)
        i = vector.tensor_reduce(asum[:, 1:2], w2st, AX.X, ALU.add, apply_absolute_value=True)
        pb.bump(i, "V")
    pb.on("vector", v)

    def t(tensor):
        pb.wait(tensor, "V")
        tensor.matmul(pb.PSA[0:1, 0:1], asum[:, 0:1], sb.onescol.ap(), start=True, stop=True)
        i = tensor.matmul(pb.PSA[0:1, 1:2], asum[:, 1:2], sb.onescol.ap(), start=True, stop=True)
        pb.bump(i, "T")
    pb.on("tensor", t)

    def v2(vector):
        pb.wait(vector, "T")
        i = vector.tensor_copy(s1p.ap()[:, 0:2], pb.PSA[0:1, 0:2])
        pb.bump(i, "V")
    pb.on("vector", v2)

    if stop <= 2:
        return

    def g(gp):
        pb.wait(gp, "V")
        i = gp.dma_start(out=dr.s1pd.ap(), in_=s1p.ap())
        pb.bump(i, "ST", 16)
        i = gp.collective_compute(
            "AllReduce", ALU.add, replica_groups=[list(range(8))],
            ins=[dr.s1pd.ap()], outs=[dr.s1rd.ap()])
        pb.bump(i, "ccs1", 1)
    pb.on("gpsimd", g)

    if stop <= 3:
        return
    stot = sb.pool_row(2)

    def f2(sync):
        pb.wait(sync, "ccs1", 1)
        i = sync.dma_start(out=stot, in_=dr.s1rd.ap()[:, 0:2])
        pb.bump(i, "LD", 16)
    pb.on("sync", f2)

    srow = sb.pool_row(2)

    def v3(vector):
        pb.wait(vector, "LD")
        vector.tensor_scalar_mul(srow, stot, 1.0 / (FF * D))
        i = vector.tensor_scalar_max(srow, srow, 1e-8)
        pb.bump(i, "V")
    pb.on("vector", v3)

    def t2(tensor):
        pb.wait(tensor, "V")
        i = tensor.matmul(pb.PSB[:, 0:2], sb.onesr.ap()[:, 0:128], srow, start=True, stop=True)
        pb.bump(i, "T")
    pb.on("tensor", t2)

    sv = sb.pool_vec(4)

    def v4(vector):
        pb.wait(vector, "T")
        vector.tensor_copy(sv[:, 0:2], pb.PSB[:, 0:2])
        i = vector.reciprocal(sv[:, 2:4], sv[:, 0:2])
        pb.bump(i, "V")
    pb.on("vector", v4)

    if stop <= 4:
        return
    w1bf = sb.BD.bf16()[:, 0:4096]
    w2bf = sb.BD.bf16()[:, 4096:8192]
    for (wst, wbf, scol, icol) in [(w1st, w1bf, 0, 2), (w2st, w2bf, 1, 3)]:
        def v5(vector, wst=wst, wbf=wbf, scol=scol, icol=icol):
            vector.tensor_scalar(wst, wst, sv[:, icol:icol + 1], MAGIC, ALU.mult, ALU.add)
            vector.tensor_scalar_add(wst, wst, -MAGIC)
            vector.tensor_scalar_min(wst, wst, 1.0)
            vector.tensor_scalar_max(wst, wst, -1.0)
            i = vector.tensor_copy(wbf, wst)
            pb.bump(i, "V")
        pb.on("vector", v5)

    if stop <= 5:
        return

    def g2(gp):
        pb.wait(gp, "V")
        i = gp.dma_start(out=dr.w1ts.ap().rearrange("(k p) c -> p k c", p=128), in_=w1bf.rearrange("p (k c) -> p k c", c=512))
        pb.bump(i, "ST", 16)
        i = gp.dma_start(out=dr.w2ts.ap().rearrange("(k p) c -> p k c", p=128), in_=w2bf.rearrange("p (k c) -> p k c", c=1024))
        pb.bump(i, "ST", 16)
        if stop <= 6:
            return
        gp.wait_ge(pb.sems["ST"], pb.cnt["ST"])
        i = gp.collective_compute(
            "AllGather", ALU.bypass, replica_groups=[list(range(8))],
            ins=[dr.w1ts.ap()], outs=[dr.w1tg.ap()])
        pb.bump(i, "ccw1", 1)
        if stop <= 7:
            return
        i = gp.collective_compute(
            "AllGather", ALU.bypass, replica_groups=[list(range(8))],
            ins=[dr.w2ts.ap()], outs=[dr.w2tg.ap()])
        pb.bump(i, "ccw2", 1)
    pb.on("gpsimd", g2)


class Carve:

    def __init__(self, nc, name, cols_f32):
        self.t = nc.alloc_sbuf_tensor("sb_" + name, [128, cols_f32], F32)
        self.cols = cols_f32

    def f32(self):
        return self.t.ap()

    def bf16(self):
        return self.t.ap().bitcast(BF16)


class SB:

    def __init__(self, nc):
        self.xs = nc.alloc_sbuf_tensor("sb_xs", [128, NCH * TPC], F32)
        self.BA = Carve(nc, "BA", 8192)
        self.BB = Carve(nc, "BB", 8192)
        self.WBUF = nc.alloc_sbuf_tensor("sb_WBUF", [128, 8192], BF16)
        self.BC = Carve(nc, "BC", 3584)
        self.BD = Carve(nc, "BD", 4096)
        self.onesr = nc.alloc_sbuf_tensor("sb_onesr", [1, 512], F32)
        self.onescol = nc.alloc_sbuf_tensor("sb_onescol", [128, 1], F32)
        self.masks = nc.alloc_sbuf_tensor("sb_masks", [128, 4 * SUP], BF16)
        self.padtab = nc.alloc_sbuf_tensor("sb_padtab", [128, NBT], F32)
        self.btab = nc.alloc_sbuf_tensor("sb_btab", [128, NBT], F32)
        self.ident = nc.alloc_sbuf_tensor("sb_ident", [128, 128], BF16)
        self.rotT = nc.alloc_sbuf_tensor("sb_rotT", [128, 128], F32)
        self.ohcol = nc.alloc_sbuf_tensor("sb_ohcol", [8, 1], F32)
        self.ohb = nc.alloc_sbuf_tensor("sb_ohb", [128, 8], F32)
        self.s1p = nc.alloc_sbuf_tensor("sb_s1p", [1, 128], F32)
        self.rows = nc.alloc_sbuf_tensor("sb_rows", [1, 2048], F32)
        self._rowoff = 0
        self.vecs = nc.alloc_sbuf_tensor("sb_vecs", [128, 2048], F32)
        self._vecoff = 0
        self.states = nc.alloc_sbuf_tensor("sb_states", [128, KCH * DST], F32)
        self.Aneg = nc.alloc_sbuf_tensor("sb_Aneg", [128, KCH * DST], F32)
        self.pvec = nc.alloc_sbuf_tensor("sb_pvec", [128, 256], F32)
        self._pvoff = 0
        self.cpool = nc.alloc_sbuf_tensor("sb_cpool", [128, 8], F32)
        self.rows2 = nc.alloc_sbuf_tensor("sb_rows2", [1, TPC], F32)
        self.CVALS = {1e-5: 0, -0.5: 1, -1.0: 2, 1.0: 3, 0.125: 4, 0.5: 5}

    def pool_row(self, n):
        ap = self.rows.ap()[:, self._rowoff:self._rowoff + n]
        self._rowoff += n
        assert self._rowoff <= 2048
        return ap

    def pool_row_at(self, off, n):
        return self.rows.ap()[:, off:off + n]

    def row_reset(self):
        self._rowoff = 0

    def vec_carve(self, off, n):
        return self.vecs.ap()[:, off:off + n]

    def cv(self, val, P=128):
        return self.cpool.ap()[0:P, self.CVALS[val]:self.CVALS[val] + 1]

    def pool_vec(self, n):
        ap = self.vecs.ap()[:, self._vecoff:self._vecoff + n]
        self._vecoff += n
        assert self._vecoff <= 512
        return ap

    def pv(self, n):
        ap = self.pvec.ap()[:, self._pvoff:self._pvoff + n]
        self._pvoff += n
        assert self._pvoff <= 256
        return ap


class DRT:

    def __init__(self, nc, grp):
        g8 = [list(range(8))]
        self.grp = grp
        self.ohd = nc.dram_tensor("ohd", [1, 8], F32)
        self.beffd = nc.dram_tensor("beffd", [1, 1024], F32)
        self.weffs = nc.dram_tensor("weffs", [128, 1024], BF16)
        self.weffg = nc.dram_tensor("weffg", [8 * 128, 1024], BF16)
        self.s1pd = nc.dram_tensor("s1pd", [1, 128], F32)
        self.s1rd = nc.dram_tensor("s1rd", [1, 128], F32)
        self.w1ts = nc.dram_tensor("w1ts", [1024, 512], BF16)
        self.w1tg = nc.dram_tensor("w1tg", [8 * 1024, 512], BF16)
        self.w2ts = nc.dram_tensor("w2ts", [512, 1024], BF16)
        self.w2tg = nc.dram_tensor("w2tg", [8 * 512, 1024], BF16)
        self.ks = nc.dram_tensor("ks", [1024, TPC], BF16)
        self.kgat = nc.dram_tensor("kgat", [4 * 1024, TPC], BF16)
        self.vs = nc.dram_tensor("vs", [TPC, 1024], BF16)
        self.vgat = nc.dram_tensor("vgat", [4 * TPC, 1024], BF16)
        self.ln5s = nc.dram_tensor("ln5s", [1024, TPC], BF16)
        self.ln5g = nc.dram_tensor("ln5g", [4 * 1024, TPC], BF16)
        self.xbcst = nc.dram_tensor("xbcst", [KCH, 128, L], F32)
        self.dtst = nc.dram_tensor("dtst", [KCH, 128, L], F32)
        self.zst = nc.dram_tensor("zst", [KCH, 128, L], F32)
        self.yst = nc.dram_tensor("yst", [KCH, 128, L], F32)
        self.xpk = nc.dram_tensor("xpk", [KCH, 33, L], F32)
        self.xprojp = nc.dram_tensor("xprojp", [33, L], F32)
        self.xprojr = nc.dram_tensor("xprojr", [33, L], F32)
        self.statsp = nc.dram_tensor("statsp", [1, 2 * L], F32)
        self.statsr = nc.dram_tensor("statsr", [1, 2 * L], F32)
        self.poutb = nc.dram_tensor("poutb", [4, 1024, TPC], F32)
        self.prsout = nc.dram_tensor("prsout", [1024, TPC], F32)


def emit_ln(pb, sb, src_ap, dst_ap, dim=D, nch=NCH):
    sq0 = sb.BC.f32()[:, 0:2048]
    sqs = [sq0, sq0]
    NT = TPC // SUP

    mmq_bump = [None] * nch

    for k in range(nch):
        def s(scalar, k=k):
            if k == 0:
                pb.wait(scalar, "V")
                pb.wait(scalar, "T")
            if k >= 1 and mmq_bump[k - 1] is not None:
                pb.wait(scalar, "T", mmq_bump[k - 1])
            i = scalar.activation(sqs[k % 2], src_ap[:, TPC * k:TPC * (k + 1)], AF.Square)
            pb.bump(i, "S")
        pb.on("scalar", s)

        def t(tensor, k=k):
            pb.wait(tensor, "S")
            for tt in range(NT):
                tensor.matmul(pb.PSA[0:1, SUP * tt:SUP * (tt + 1)], sb.onescol.ap(),
                              src_ap[:, TPC * k + SUP * tt: TPC * k + SUP * (tt + 1)],
                              start=(k == 0), stop=(k == nch - 1))
                i = tensor.matmul(pb.PSB[0:1, SUP * tt:SUP * (tt + 1)], sb.onescol.ap(),
                                  sqs[k % 2][:, SUP * tt:SUP * (tt + 1)],
                                  start=(k == 0), stop=(k == nch - 1))
            mmq_bump[k] = pb.bump(i, "T")
        pb.on("tensor", t)

    rA = sb.rows.ap()[:, 0:TPC]
    rB = sb.vecs.ap()[0:1, 0:TPC]
    rC = sb.rows2.ap()

    def v(vector):
        pb.wait(vector, "T")
        vector.tensor_copy(rA, pb.PSA[0:1, 0:TPC])
        vector.tensor_copy(rB, pb.PSB[0:1, 0:TPC])
        vector.tensor_scalar_mul(rA, rA, 1.0 / dim)
        vector.tensor_scalar_mul(rB, rB, 1.0 / dim)
        vector.tensor_mul(rC, rA, rA)
        i = vector.tensor_sub(rB, rB, rC)
        pb.bump(i, "V")
    pb.on("vector", v)

    def s2(scalar):
        pb.wait(scalar, "V")
        scalar.activation(rB, rB, AF.Ln, bias=sb.cv(1e-5, rB.shape[0]))
        i = scalar.activation(rB, rB, AF.Exp, scale=sb.cv(-0.5, rB.shape[0]))
        pb.bump(i, "S")
    pb.on("scalar", s2)

    def v2(vector):
        pb.wait(vector, "S")
        i = vector.tensor_mul(rC, rA, rB)
        pb.bump(i, "V")
    pb.on("vector", v2)

    for tt in range(NT):
        def t2(tensor, tt=tt):
            pb.wait(tensor, "V")
            tensor.matmul(pb.PSA[:, SUP * tt:SUP * (tt + 1)], sb.onesr.ap()[:, 0:128],
                          rB[:, SUP * tt:SUP * (tt + 1)], start=True, stop=True)
            i = tensor.matmul(pb.PSB[:, SUP * tt:SUP * (tt + 1)], sb.onesr.ap()[:, 0:128],
                              rC[:, SUP * tt:SUP * (tt + 1)], start=True, stop=True)
            pb.bump(i, "T")
        pb.on("tensor", t2)

    t512a = sb.BC.f32()[:, 0:512]
    t512b = sb.BC.f32()[:, 512:1024]

    def v3(vector):
        pb.wait(vector, "T")
        for k in range(nch):
            for tt in range(NT):
                sl = slice(TPC * k + SUP * tt, TPC * k + SUP * (tt + 1))
                tb = t512a if (k % 2 == 0) else t512b
                vector.tensor_mul(tb, src_ap[:, sl], pb.PSA[:, SUP * tt:SUP * (tt + 1)])
                i = vector.tensor_sub(dst_ap[:, sl], tb, pb.PSB[:, SUP * tt:SUP * (tt + 1)])
        pb.bump(i, "V")
    pb.on("vector", v3)


def emit_ffn0(pb, sb, lnsrc_bf16, beff):
    NT = TPC // SUP
    vlastf = [None] * 4
    for tt in range(NT):
        for j in range(NCH):
            bank = (tt * NCH + j) % 4

            def t(tensor, tt=tt, j=j, bank=bank):
                if tt == 0 and j == 0:
                    pb.wait(tensor, "ccweff", 1)
                    pb.wait(tensor, "LD")
                    pb.wait(tensor, "V")
                if vlastf[bank] is not None:
                    pb.wait(tensor, "V", vlastf[bank])
                ps = pb.PSA[:, SUP * bank:SUP * (bank + 1)]
                for k in range(NCH):
                    tensor.matmul(ps, sb.WBUF.ap()[:, 1024 * k + 128 * j:1024 * k + 128 * (j + 1)],
                                  lnsrc_bf16[:, TPC * k + SUP * tt:TPC * k + SUP * (tt + 1)],
                                  start=(k == 0), stop=False)
                i = tensor.matmul(ps, beff[:, 128 * j:128 * (j + 1)],
                                  sb.onesr.ap(), start=False, stop=True)
                pb.bump(i, "T")
            pb.on("tensor", t)

            def v(vector, tt=tt, j=j, bank=bank):
                pb.wait(vector, "T")
                ps = pb.PSA[:, SUP * bank:SUP * (bank + 1)]
                xsl = sb.xs.ap()[:, TPC * j + SUP * tt:TPC * j + SUP * (tt + 1)]
                i = vector.scalar_tensor_tensor(xsl, ps, 0.5, xsl, ALU.mult, ALU.add)
                vlastf[bank] = pb.bump(i, "V")
            pb.on("vector", v)


def load_w_bf16(pb, sb, dram_ap, KK, C, dst_cols_off):
    src3 = dram_ap.rearrange("(k p) c -> p k c", p=128)
    kq = max(1, 3584 // C)
    k0 = 0
    while k0 < KK:
        kn = min(kq, KK - k0)
        n = kn * C
        st = sb.BC.f32()[:, 0:n]

        def f(sync, st=st, k0=k0, kn=kn, n=n):
            pb.wait(sync, "V")
            i = sync.dma_start(out=st.rearrange("p (k c) -> p k c", c=C),
                               in_=src3[:, k0:k0 + kn, :])
            pb.bump(i, "LD", 16)
        pb.on("sync", f)

        def v(vector, st=st, k0=k0, n=n):
            pb.wait(vector, "LD")
            off = dst_cols_off + k0 * C
            i = vector.tensor_copy(sb.WBUF.ap()[:, off:off + n], st)
            pb.bump(i, "V")
        pb.on("vector", v)
        k0 += kn


def load_brow(pb, sb, src, n):
    dst = sb.pool_row(n)

    def f(sync):
        i = sync.dma_start(out=dst, in_=src)
        pb.bump(i, "LD", 16)
    pb.on("sync", f)
    return dst


def proj_mm(pb, sb, dst_bf16_cols, rhs_bf16, rhs_nch, wb_off, ncol_chunks,
            brow_ap, act, extra_ps=None):
    NT = TPC // SUP
    vlast = [None] * 4
    for j in range(ncol_chunks):
        for tt in range(NT):
            bank = (j * NT + tt) % 4

            def t(tensor, j=j, tt=tt, bank=bank):
                if vlast[bank] is not None:
                    pb.wait(tensor, "S", vlast[bank])
                else:
                    pb.wait(tensor, "S")
                    pb.wait(tensor, "V")
                ps = pb.PSA[:, SUP * bank:SUP * (bank + 1)]
                for k in range(rhs_nch):
                    tensor.matmul(ps, sb.WBUF.ap()[:, wb_off + (k * ncol_chunks + j) * 128:
                                                   wb_off + (k * ncol_chunks + j) * 128 + 128],
                                  rhs_bf16[:, TPC * k + SUP * tt:TPC * k + SUP * (tt + 1)],
                                  start=(k == 0), stop=False)
                i = tensor.matmul(ps, brow_ap[:, 128 * j:128 * (j + 1)],
                                  sb.onesr.ap(), start=False, stop=True)
                pb.bump(i, "T")
            pb.on("tensor", t)

            def s(scalar, j=j, tt=tt, bank=bank):
                pb.wait(scalar, "T")
                ps = pb.PSA[:, SUP * bank:SUP * (bank + 1)]
                dst = dst_bf16_cols[:, TPC * j + SUP * tt:TPC * j + SUP * (tt + 1)]
                if act is None:
                    i = scalar.copy(dst, ps)
                else:
                    i = scalar.activation(dst, ps, act)
                vlast[bank] = pb.bump(i, "S")
            pb.on("scalar", s)


def emit_rot_weights(pb, sb, src_off, dst_off, nk=8):
    def v(vector):
        pb.wait(vector, "LD")
        for k in range(nk):
            s0 = sb.WBUF.ap()[:, src_off + 128 * k:src_off + 128 * k + 64]
            s1 = sb.WBUF.ap()[:, src_off + 128 * k + 64:src_off + 128 * k + 128]
            d0 = sb.WBUF.ap()[:, dst_off + 128 * k:dst_off + 128 * k + 64]
            d1 = sb.WBUF.ap()[:, dst_off + 128 * k + 64:dst_off + 128 * k + 128]
            vector.tensor_scalar(d0, s1, -1.0, None, ALU.mult)
            i = vector.tensor_copy(d1, s0)
        pb.bump(i, "V")
    pb.on("vector", v)


def emit_rot_brow(pb, sb, brow):
    out = sb.pool_row(128)

    def v(vector):
        pb.wait(vector, "LD")
        vector.tensor_scalar(out[:, 0:64], brow[:, 64:128], -1.0, None, ALU.mult)
        i = vector.tensor_copy(out[:, 64:128], brow[:, 0:64])
        pb.bump(i, "V")
    pb.on("vector", v)
    return out


def emit_rope_chunk(pb, sb, dst_bf16, wb_off, wb_rot_off, brow_ap, brow_rot_ap,
                    lnsrc, cosT, sinT):
    NT = TPC // SUP
    t1 = sb.vec_carve(0, 512)
    t2s = sb.vec_carve(512, 512)
    for tt in range(NT):
        def t(tensor, tt=tt):
            pb.wait(tensor, "V")
            for k in range(NCH):
                tensor.matmul(pb.PSA[:, 0:512],
                              sb.WBUF.ap()[:, wb_off + k * 128:wb_off + k * 128 + 128],
                              lnsrc[:, TPC * k + SUP * tt:TPC * k + SUP * (tt + 1)],
                              start=(k == 0), stop=False)
            tensor.matmul(pb.PSA[:, 0:512], brow_ap[:, 0:128], sb.onesr.ap(),
                          start=False, stop=True)
            for k in range(NCH):
                tensor.matmul(pb.PSA[:, 512:1024],
                              sb.WBUF.ap()[:, wb_rot_off + k * 128:wb_rot_off + k * 128 + 128],
                              lnsrc[:, TPC * k + SUP * tt:TPC * k + SUP * (tt + 1)],
                              start=(k == 0), stop=False)
            i = tensor.matmul(pb.PSA[:, 512:1024], brow_rot_ap[:, 0:128], sb.onesr.ap(),
                              start=False, stop=True)
            pb.bump(i, "T")
        pb.on("tensor", t)

        def v(vector, tt=tt):
            pb.wait(vector, "T")
            pb.wait(vector, "S")
            cs = slice(SUP * tt, SUP * (tt + 1))
            vector.tensor_mul(t1, pb.PSA[:, 0:512], cosT[:, cs])
            vector.tensor_mul(t2s, pb.PSA[:, 512:1024], sinT[:, cs])
            i = vector.tensor_add(t1, t1, t2s)
            pb.bump(i, "V")
        pb.on("vector", v)

        def s2(scalar, tt=tt):
            pb.wait(scalar, "V")
            i = scalar.activation(dst_bf16[:, SUP * tt:SUP * (tt + 1)], t1, AF.Gelu)
            pb.bump(i, "S")
        pb.on("scalar", s2)
        if DBG_ROPE_BARRIER:
            pb.barrier()


def emit_mla(pb, io, dr, sb, stop=99):
    nc = pb.nc
    NT = TPC // SUP
    lnb = sb.BA.bf16()
    cq = sb.BD.bf16()[:, 0:4096]
    ckv = sb.BD.bf16()[:, 4096:8192]
    qT = sb.BB.bf16()

    emit_ln(pb, sb, sb.xs.ap(), lnb)
    pb.barrier()
    sb.row_reset()

    load_w_bf16(pb, sb, io["wq"], 8, 256, 0)
    load_w_bf16(pb, sb, io["wkv"], 8, 256, 2048)
    load_w_bf16(pb, sb, io["wqr"], 8, 128, 4096)
    load_w_bf16(pb, sb, io["wkr"], 8, 128, 5120)
    bq = load_brow(pb, sb, io["bq"], 256)
    bkv = load_brow(pb, sb, io["bkv"], 256)
    bqr = load_brow(pb, sb, io["bqr"], 128)
    bkr = load_brow(pb, sb, io["bkr"], 128)
    cosT = sb.BB.f32()[:, 0:TPC]
    sinT = sb.BB.f32()[:, TPC:2 * TPC]
    emit_rot_weights(pb, sb, 4096, 6144)
    emit_rot_weights(pb, sb, 5120, 7168)

    def fcs(sync):
        i = sync.dma_start(out=cosT, in_=io["cosT"])
        pb.bump(i, "LD", 16)
        i = sync.dma_start(out=sinT, in_=io["sinT"])
        pb.bump(i, "LD", 16)
    pb.on("sync", fcs)
    pb.barrier()
    if stop <= 1:
        return
    proj_mm(pb, sb, cq, lnb, NCH, 0, 2, bq, AF.Gelu)
    proj_mm(pb, sb, ckv, lnb, NCH, 2048, 2, bkv, AF.Gelu)
    if stop <= 2:
        return
    bqr_rot = emit_rot_brow(pb, sb, bqr)
    bkr_rot = emit_rot_brow(pb, sb, bkr)
    pb.barrier()
    emit_rope_chunk(pb, sb, qT[:, 7 * TPC:8 * TPC], 4096, 6144, bqr, bqr_rot, lnb, cosT, sinT)
    k7 = sb.BC.bf16()[:, 0:TPC]
    emit_rope_chunk(pb, sb, k7, 5120, 7168, bkr, bkr_rot, lnb, cosT, sinT)
    pb.barrier()
    sb.row_reset()

    def g(gp):
        pb.wait(gp, "S")
        i = gp.dma_start(out=dr.ks.ap()[128 * 7:128 * 8, :], in_=k7)
        pb.bump(i, "ST", 16)
    pb.on("gpsimd", g)
    pb.barrier()

    if stop <= 3:
        return
    if stop <= 35:
        return
    load_w_bf16(pb, sb, io["wqc"], 2, 896, 0)
    load_w_bf16(pb, sb, io["wkc"], 2, 896, 1792)
    load_w_bf16(pb, sb, io["wv"], 2, 1024, 3584)
    bqc = load_brow(pb, sb, io["bqc"], 896)
    bkc = load_brow(pb, sb, io["bkc"], 896)
    bv = sb.vecs.ap()[0:1, 0:1024]

    def fbv(sync):
        pb.wait(sync, "S")
        i = sync.dma_start(out=bv, in_=io["bv"])
        pb.bump(i, "LD", 16)
    pb.on("sync", fbv)
    pb.barrier()
    if stop <= 31:
        return
    proj_mm(pb, sb, qT, cq, 2, 0, 7, bqc, AF.Gelu)
    if stop <= 32:
        return
    ksc = sb.BC.bf16()[:, 0:TPC]
    vlast = [None]
    for j in range(7):
        for tt in range(NT):
            def t(tensor, j=j, tt=tt):
                pb.wait(tensor, "S")
                if vlast[0] is not None:
                    pb.wait(tensor, "ST", vlast[0])
                ps = pb.PSA[:, 1024:1536]
                for k in range(2):
                    tensor.matmul(ps, sb.WBUF.ap()[:, 1792 + (k * 7 + j) * 128:1792 + (k * 7 + j) * 128 + 128],
                                  ckv[:, TPC * k + SUP * tt:TPC * k + SUP * (tt + 1)],
                                  start=(k == 0), stop=False)
                i = tensor.matmul(ps, bkc[:, 128 * j:128 * (j + 1)], sb.onesr.ap(),
                                  start=False, stop=True)
                pb.bump(i, "T")
            pb.on("tensor", t)

            def s(scalar, j=j, tt=tt):
                pb.wait(scalar, "T")
                i = scalar.activation(ksc[:, SUP * tt:SUP * (tt + 1)], pb.PSA[:, 1024:1536], AF.Gelu)
                pb.bump(i, "S")
            pb.on("scalar", s)

        def g2(gp, j=j):
            pb.wait(gp, "S")
            i = gp.dma_start(out=dr.ks.ap()[128 * j:128 * (j + 1), :], in_=ksc)
            vlast[0] = pb.bump(i, "ST", 16)
        pb.on("gpsimd", g2)

    if stop <= 4:
        return
    vsc = sb.BC.bf16()[:, 2048:2560]
    stlast = [None]
    for tc in range(TPC // 128):
        for dv in range(2):
            def t(tensor, tc=tc, dv=dv):
                pb.wait(tensor, "S")
                if stlast[0] is not None:
                    pb.wait(tensor, "ST", stlast[0])
                ps = pb.PSB[:, 0:512]
                for k in range(2):
                    tensor.matmul(ps, ckv[:, TPC * k + 128 * tc:TPC * k + 128 * (tc + 1)],
                                  sb.WBUF.ap()[:, 3584 + k * 1024 + 512 * dv:3584 + k * 1024 + 512 * (dv + 1)],
                                  start=(k == 0), stop=False)
                i = tensor.matmul(ps, sb.onesr.ap()[:, 0:128],
                                  bv[:, 512 * dv:512 * (dv + 1)], start=False, stop=True)
                pb.bump(i, "T")
            pb.on("tensor", t)

            def s(scalar, tc=tc, dv=dv):
                pb.wait(scalar, "T")
                i = scalar.copy(vsc, pb.PSB[:, 0:512])
                pb.bump(i, "S")
            pb.on("scalar", s)

            def g3(gp, tc=tc, dv=dv):
                pb.wait(gp, "S")
                i = gp.dma_start(out=dr.vs.ap()[128 * tc:128 * (tc + 1), 512 * dv:512 * (dv + 1)],
                                 in_=vsc)
                stlast[0] = pb.bump(i, "ST", 16)
            pb.on("gpsimd", g3)

    if stop <= 5:
        return
    def g4(gp):
        pb.wait(gp, "ST")
        i = gp.collective_compute("AllGather", ALU.bypass, replica_groups=dr.grp,
                                  ins=[dr.ks.ap()], outs=[dr.kgat.ap()])
        pb.bump(i, "cckg", 1)
        i = gp.collective_compute("AllGather", ALU.bypass, replica_groups=dr.grp,
                                  ins=[dr.vs.ap()], outs=[dr.vgat.ap()])
        pb.bump(i, "ccvg", 1)
    pb.on("gpsimd", g4)
    pb.barrier()
    sb.row_reset()


def emit_attention(pb, io, dr, sb):
    NT = TPC // SUP
    ostack = sb.BA.bf16()
    qT = sb.BB.bf16()
    vth = sb.BC.bf16()[:, 0:4160]
    pT = sb.BC.bf16()[:, 4160:6208]
    obuf = sb.vecs.ap().bitcast(BF16)[0:64, 3204:3716]
    rlb = sb.vec_carve(512, 512)
    negmv = sb.vecs.ap()[:, 1024:1025]
    rr = sb.vecs.ap()[64:65, 1026:1538]
    ones64 = sb.vecs.ap()[64:65, 1538:1602]

    def vv(vector):
        pb.wait(vector, "ST")
        pb.wait(vector, "T")
        pb.wait(vector, "S")
        vector.memset(ones64, 1.0)
        vector.memset(sb.BC.bf16()[:, 6720:6980].rearrange("p (j c) -> p j c", c=65)[:, :, 64:65], 1.0)
        i = vector.memset(vth.rearrange("p (j c) -> p j c", c=65)[:, :, 64:65], 1.0)
        pb.bump(i, "V")
    pb.on("vector", vv)

    kdg = sb.BC.bf16()
    vdg = sb.BC.bf16()[:, 6720:6980]
    lastqk = {}
    lastpv = {}
    lastsq = {}
    for h in range(H):
        ch, hh = h // 2, h % 2
        kc = sb.BD.bf16()[64 * hh:64 * hh + 64, 0:8192]
        kdiag = kdg[64 * hh:64 * hh + 64, 6208:6720]
        nsc = sb.vec_carve(0, 512)[64 * hh:64 * hh + 64, :]
        olast = [None]

        def ldh(sync, h=h, ch=ch, hh=hh, kc=kc):
            if h == 0:
                pb.wait(sync, "cckg", 1)
                pb.wait(sync, "ccvg", 1)
                pb.wait(sync, "V")
            if h - 2 in lastqk:
                pb.wait(sync, "T", lastqk[h - 2])
            if h - 2 in lastsq:
                pb.wait(sync, "S", lastsq[h - 2])
            if h - 1 in lastpv:
                pb.wait(sync, "T", lastpv[h - 1])
            for r in range(4):
                i = sync.dma_start(out=kc[:, 2048 * r:2048 * (r + 1)],
                                   in_=dr.kgat.ap()[1024 * r + 128 * ch + 64 * hh:
                                                    1024 * r + 128 * ch + 64 * hh + 64, :])
                pb.bump(i, "LD", 16)
                vsrc = dr.vgat.ap()[TPC * r:TPC * (r + 1), 64 * h:64 * (h + 1)]
                vdst = vth.rearrange("p (j c) -> p j c", c=65)[:, 16 * r:16 * (r + 1), 0:64]
                i = sync.dma_start(out=vdst, in_=vsrc.rearrange("(j p) d -> p j d", p=128))
                pb.bump(i, "LD", 16)
        pb.on("sync", ldh)

        for m in (3, 2, 1, 0):
            nst = LMAX[m]

            def ld(sync, h=h, ch=ch, hh=hh, m=m, kdiag=kdiag):
                pb.wait(sync, "T")
                i = sync.dma_start(out=kdiag,
                                   in_=dr.ks.ap()[128 * ch + 64 * hh:128 * ch + 64 * hh + 64,
                                                  SUP * m:SUP * (m + 1)])
                pb.bump(i, "LD", 16)
                vsrc = dr.vs.ap()[SUP * m:SUP * (m + 1), 64 * h:64 * (h + 1)]
                vdst = vdg.rearrange("p (j c) -> p j c", c=65)[:, :, 0:64]
                i = sync.dma_start(out=vdst, in_=vsrc.rearrange("(j p) d -> p j d", p=128))
                pb.bump(i, "LD", 16)
            pb.on("sync", ld)

            if m == 3:
                r_rk = sb.pool_row_at(0, 512)
                r_mx = sb.pool_row_at(512, 2)
                r_sc = sb.pool_row_at(514, 6)
                nscb = [sb.vec_carve(0, 512)[64 * hh:64 * hh + 64, :],
                        sb.vec_carve(512, 512)[64 * hh:64 * hh + 64, :]]
                sqb = {}
                mmb2 = {}
                vb2 = {}
                for gi, (kind, ntile) in enumerate((("k", 16), ("q", 4))):
                    for tile in range(ntile):
                        gt = 16 * gi + tile
                        def s(scalar, kind=kind, tile=tile, gt=gt, ch=ch, hh=hh, kc=kc,
                              nscb=nscb, sqb=sqb, mmb2=mmb2, h=h, lastsq=lastsq):
                            if kind == "k" and tile == 0:
                                pb.wait(scalar, "LD")
                            if gt - 2 in mmb2:
                                pb.wait(scalar, "T", mmb2[gt - 2])
                            elif gt == 0:
                                pb.wait(scalar, "T")
                            if kind == "k":
                                srcq = kc[:, SUP * tile:SUP * (tile + 1)]
                            else:
                                srcq = qT[64 * hh:64 * hh + 64,
                                          TPC * ch + SUP * tile:TPC * ch + SUP * (tile + 1)]
                            i = scalar.activation(nscb[gt % 2], srcq, AF.Square)
                            sqb[gt] = pb.bump(i, "S")
                            if kind == "k" and tile == 15:
                                lastsq[h] = sqb[gt]
                        pb.on("scalar", s)

                        def t(tensor, gt=gt, hh=hh, nscb=nscb, sqb=sqb, mmb2=mmb2, vb2=vb2):
                            pb.wait(tensor, "S", sqb[gt])
                            if gt - 2 in vb2:
                                pb.wait(tensor, "V", vb2[gt - 2])
                            elif gt == 0:
                                pb.wait(tensor, "V")
                            ps1 = pb.PSB[0:1, 1024 + 512 * (gt % 2):1536 + 512 * (gt % 2)]
                            i = tensor.matmul(ps1, sb.onescol.ap()[64 * hh:64 * hh + 64, :],
                                              nscb[gt % 2], start=True, stop=True)
                            mmb2[gt] = pb.bump(i, "T")
                        pb.on("tensor", t)

                        def v(vector, kind=kind, tile=tile, gt=gt, r_rk=r_rk, mmb2=mmb2, vb2=vb2):
                            pb.wait(vector, "T", mmb2[gt])
                            ps1 = pb.PSB[0:1, 1024 + 512 * (gt % 2):1536 + 512 * (gt % 2)]
                            if tile == 0:
                                i = vector.tensor_copy(r_rk, ps1)
                            else:
                                i = vector.tensor_max(r_rk, r_rk, ps1)
                            vb2[gt] = pb.bump(i, "V")
                        pb.on("vector", v)

                    col = 0 if kind == "k" else 1

                    def v2(vector, col=col, r_rk=r_rk, r_mx=r_mx):
                        i = vector.reduce_max(r_mx[:, col:col + 1], r_rk, AX.X)
                        pb.bump(i, "V")
                    pb.on("vector", v2)

                def s3(scalar, r_mx=r_mx, r_sc=r_sc):
                    pb.wait(scalar, "V")
                    i = scalar.activation(r_sc[:, 0:2], r_mx, AF.Ln)
                    pb.bump(i, "S")
                pb.on("scalar", s3)

                def v3(vector, r_sc=r_sc):
                    pb.wait(vector, "S")
                    i = vector.tensor_add(r_sc[:, 2:3], r_sc[:, 0:1], r_sc[:, 1:2])
                    pb.bump(i, "V")
                pb.on("vector", v3)

                def s4(scalar, r_sc=r_sc):
                    pb.wait(scalar, "V")
                    i = scalar.activation(r_sc[:, 3:4], r_sc[:, 2:3], AF.Exp, scale=sb.cv(0.5, 1))
                    pb.bump(i, "S")
                pb.on("scalar", s4)

                def v4(vector, r_sc=r_sc):
                    pb.wait(vector, "S")
                    i = vector.tensor_scalar_mul(r_sc[:, 4:5], r_sc[:, 3:4], -0.125)
                    pb.bump(i, "V")
                pb.on("vector", v4)

                def t3(tensor, r_sc=r_sc):
                    pb.wait(tensor, "V")
                    i = tensor.matmul(pb.PSB[:, 1536:1537], sb.onesr.ap()[:, 0:128],
                                      r_sc[:, 4:5], start=True, stop=True)
                    pb.bump(i, "T")
                pb.on("tensor", t3)

                def v5(vector):
                    pb.wait(vector, "T")
                    pb.wait(vector, "S")
                    vector.tensor_copy(negmv, pb.PSB[:, 1536:1537])
                    i = vector.tensor_scalar_add(sb.btab.ap(), sb.padtab.ap(), negmv)
                    pb.bump(i, "V")
                pb.on("vector", v5)

                if DBG_NORM is not None:
                    def vdbg(gp, h=h, r_sc=r_sc, r_mx=r_mx):
                        pb.wait(gp, "V")
                        i = gp.dma_start(out=DBG_NORM[0][0:1, 8 * h:8 * h + 6], in_=r_sc)
                        pb.bump(i, "ST", 16)
                        i = gp.dma_start(out=DBG_NORM[0][0:1, 128 + 8 * h:128 + 8 * h + 2], in_=r_mx)
                        pb.bump(i, "ST", 16)
                    pb.on("gpsimd", vdbg)

            nsteps = 2 * nst
            expb = [None] * nsteps
            pvb = [None] * nsteps
            qkb = [None] * nsteps
            qrhs = qT[64 * hh:64 * hh + 64, TPC * ch + SUP * m:TPC * ch + SUP * (m + 1)]

            def emit_pv(tensor, j_s, m=m, nst=nst, nsteps=nsteps, expb=expb,
                        pvb=pvb, olast=olast, h=h, lastpv=lastpv):
                pb.wait(tensor, "S", expb[j_s])
                if j_s == 0 and olast[0] is not None:
                    pb.wait(tensor, "V", olast[0])
                tp, hp = j_s // 2, j_s % 2
                dg = (tp == nst - 1)
                pt = pT[:, 1024 * (j_s % 2):1024 * (j_s % 2) + 1024]
                for i2 in range(2):
                    kb = 2 * hp + i2
                    if dg:
                        vt = vdg[:, 65 * kb:65 * kb + 65]
                    else:
                        r, mi = sup_addr(tp)
                        pos = 4 * r + mi
                        vt = vth[:, 65 * (4 * pos + kb):65 * (4 * pos + kb) + 65]
                    i = tensor.matmul(pb.PSB[0:65, 0:512], vt, pt[:, 512 * i2:512 * (i2 + 1)],
                                      start=(j_s == 0 and i2 == 0),
                                      stop=(j_s == nsteps - 1 and i2 == 1))
                pvb[j_s] = pb.bump(i, "T")
                if m == 0:
                    lastpv[h] = pvb[j_s]

            for i_s in range(nsteps):
                t_sup, half = i_s // 2, i_s % 2
                diag = (t_sup == nst - 1)
                sbank = pb.PSA[:, 1024 * (i_s % 2):1024 * (i_s % 2) + 1024]
                ptile = pT[:, 1024 * (i_s % 2):1024 * (i_s % 2) + 1024]

                def tq(tensor, i_s=i_s, t_sup=t_sup, half=half, diag=diag,
                       sbank=sbank, m=m, qrhs=qrhs, expb=expb, emit_pv=emit_pv,
                       kc=kc, kdiag=kdiag, qkb=qkb, h=h, lastqk=lastqk):
                    if i_s == 0:
                        pb.wait(tensor, "LD")
                        pb.wait(tensor, "V")
                    if i_s >= 2 and expb[i_s - 2] is not None:
                        pb.wait(tensor, "S", expb[i_s - 2])
                    for i2 in range(2):
                        kb = 2 * half + i2
                        if diag:
                            lhs = kdiag[:, 128 * kb:128 * (kb + 1)]
                        else:
                            r, mi = sup_addr(t_sup)
                            pos = 4 * r + mi
                            lhs = kc[:, SUP * pos + 128 * kb:SUP * pos + 128 * (kb + 1)]
                        i = tensor.matmul(sbank[:, 512 * i2:512 * (i2 + 1)], lhs, qrhs,
                                          start=True, stop=(not diag))
                        if diag:
                            i = tensor.matmul(sbank[:, 512 * i2:512 * (i2 + 1)], sb.ident.ap(),
                                              sb.masks.ap()[:, SUP * kb:SUP * (kb + 1)],
                                              start=False, stop=True)
                    qkb[i_s] = pb.bump(i, "T")
                    if m == 0:
                        lastqk[h] = qkb[i_s]
                    if i_s >= 1:
                        emit_pv(tensor, i_s - 1)
                pb.on("tensor", tq)

                def se(scalar, i_s=i_s, t_sup=t_sup, diag=diag, sbank=sbank,
                       ptile=ptile, m=m, expb=expb, qkb=qkb, pvb=pvb):
                    pb.wait(scalar, "T", qkb[i_s])
                    if i_s >= 2 and pvb[i_s - 2] is not None:
                        pb.wait(scalar, "T", pvb[i_s - 2])
                    col = bt_diag(m) if diag else bt_col(m, t_sup)
                    btab_src = sb.padtab.ap() if DBG_CONST_BIAS else sb.btab.ap()
                    i = scalar.activation(ptile, sbank, AF.Exp, scale=sb.cv(0.125),
                                          bias=btab_src[:, col:col + 1])
                    expb[i_s] = pb.bump(i, "S")
                pb.on("scalar", se)

            def tl(tensor, nsteps=nsteps, emit_pv=emit_pv):
                emit_pv(tensor, nsteps - 1)
            pb.on("tensor", tl)

            def vo(vector):
                pb.wait(vector, "T")
                vector.tensor_scalar_add(rr, pb.PSB[64:65, 0:512], 1e-30)
                i = vector.reciprocal(rr, rr)
                pb.bump(i, "V")
            pb.on("vector", vo)

            def to(tensor):
                pb.wait(tensor, "V")
                i = tensor.matmul(pb.PSB[0:64, 512:1024], ones64[:, 0:64],
                                  rr, start=True, stop=True)
                pb.bump(i, "T")
            pb.on("tensor", to)

            def vo2(vector, m=m, ch=ch, hh=hh, olast=olast):
                pb.wait(vector, "T")
                vector.tensor_copy(rlb[0:64, :], pb.PSB[0:64, 512:1024])
                if hh == 0:
                    dst = ostack[0:64, TPC * ch + SUP * m:TPC * ch + SUP * (m + 1)]
                else:
                    dst = obuf[0:64, :]
                i = vector.tensor_mul(dst, pb.PSB[0:64, 0:512], rlb[0:64, :])
                olast[0] = pb.bump(i, "V")
            pb.on("vector", vo2)

            if hh == 1:
                def gmove(gp, m=m, ch=ch):
                    pb.wait(gp, "V")
                    i = gp.dma_start(
                        out=ostack[64:128, TPC * ch + SUP * m:TPC * ch + SUP * (m + 1)],
                        in_=obuf[0:64, :])
                    pb.bump(i, "ST", 16)
                pb.on("gpsimd", gmove)

                def vguard3(vector):
                    pb.wait(vector, "ST")
                pb.on("vector", vguard3)

    pb.barrier()
    pb.barrier()
    load_w_bf16(pb, sb, io["wo"], 8, 1024, 0)
    bo = load_brow(pb, sb, io["bo"], 1024)
    pb.barrier()
    vlast = [None] * 4
    for tt in range(NT):
        for j in range(NCH):
            bank = (tt * NCH + j) % 4

            def t(tensor, tt=tt, j=j, bank=bank):
                if vlast[bank] is not None:
                    pb.wait(tensor, "V", vlast[bank])
                ps = pb.PSA[:, SUP * bank:SUP * (bank + 1)]
                for k in range(NCH):
                    tensor.matmul(ps, sb.WBUF.ap()[:, 1024 * k + 128 * j:1024 * k + 128 * (j + 1)],
                                  ostack[:, TPC * k + SUP * tt:TPC * k + SUP * (tt + 1)],
                                  start=(k == 0), stop=False)
                i = tensor.matmul(ps, bo[:, 128 * j:128 * (j + 1)], sb.onesr.ap(),
                                  start=False, stop=True)
                pb.bump(i, "T")
            pb.on("tensor", t)

            def v(vector, tt=tt, j=j, bank=bank):
                pb.wait(vector, "T")
                ps = pb.PSA[:, SUP * bank:SUP * (bank + 1)]
                xsl = sb.xs.ap()[:, TPC * j + SUP * tt:TPC * j + SUP * (tt + 1)]
                i = vector.tensor_add(xsl, xsl, ps)
                vlast[bank] = pb.bump(i, "V")
            pb.on("vector", v)


def emit_ffn1(pb, io, dr, sb):
    NT = TPC // SUP
    lnb = sb.BA.bf16()
    w2buf = sb.BB.bf16()[:, 0:8192]
    htile = sb.BC.bf16()[:, 0:4096]
    b2sb = sb.pv(8)

    def f0(sync):
        i = sync.dma_start(out=b2sb, in_=io["b2col"])
        pb.bump(i, "LD", 16)
    pb.on("sync", f0)

    def v0(vector):
        pb.wait(vector, "LD")
        vector.tensor_scalar_mul(b2sb, b2sb, 0.5)
        for k in range(NCH):
            i = vector.tensor_scalar_add(sb.xs.ap()[:, TPC * k:TPC * (k + 1)],
                                         sb.xs.ap()[:, TPC * k:TPC * (k + 1)],
                                         b2sb[:, k:k + 1])
        pb.bump(i, "V")
    pb.on("vector", v0)

    b1sl = sb.pool_row_at(1024, 1024)
    for ffq in range(4):
        def fw(sync, ffq=ffq):
            pb.wait(sync, "T")
            for f in range(8):
                F = 8 * ffq + f
                r, bl = F // 4, F % 4
                src = dr.w1tg.ap()[1024 * r:1024 * (r + 1), 128 * bl:128 * (bl + 1)]
                src = src.rearrange("(k p) c -> p k c", p=128)
                dst = sb.WBUF.ap().rearrange("p (k f c) -> p k f c", k=8, f=8)[:, :, f, :]
                i = sync.dma_start(out=dst, in_=src)
                pb.bump(i, "LD", 16)
                r2, bl2 = F // 4, F % 4
                src2 = dr.w2tg.ap()[512 * r2 + 128 * bl2:512 * r2 + 128 * (bl2 + 1), :]
                src2 = src2.rearrange("p (e c) -> p e c", c=128)
                dst2 = w2buf.rearrange("p (f e c) -> p f e c", f=8, e=8)[:, f, :, :]
                i = sync.dma_start(out=dst2, in_=src2)
                pb.bump(i, "LD", 16)
            i = sync.dma_start(out=b1sl, in_=io["b1row"][:, 1024 * ffq:1024 * (ffq + 1)])
            pb.bump(i, "LD", 16)
        pb.on("sync", fw)

        for tt in range(NT):
            slast = [None] * 4
            for f in range(8):
                bank = f % 4

                def t(tensor, ffq=ffq, tt=tt, f=f, bank=bank, slast=slast):
                    if ffq == 0 and tt == 0 and f == 0:
                        pb.wait(tensor, "ccw1", 1)
                        pb.wait(tensor, "ccw2", 1)
                        pb.wait(tensor, "V")
                    if f == 0 and tt == 0:
                        pb.wait(tensor, "LD")
                    if slast[bank] is not None:
                        pb.wait(tensor, "S", slast[bank])
                    ps = pb.PSA[:, SUP * bank:SUP * (bank + 1)]
                    for k in range(NCH):
                        tensor.matmul(ps, sb.WBUF.ap()[:, (k * 8 + f) * 128:(k * 8 + f) * 128 + 128],
                                      lnb[:, TPC * k + SUP * tt:TPC * k + SUP * (tt + 1)],
                                      start=(k == 0), stop=False)
                    i = tensor.matmul(ps, b1sl[:, 128 * f:128 * (f + 1)], sb.onesr.ap(),
                                      start=False, stop=True)
                    pb.bump(i, "T")
                pb.on("tensor", t)

                def s(scalar, tt=tt, f=f, bank=bank, slast=slast):
                    pb.wait(scalar, "T")
                    ps = pb.PSA[:, SUP * bank:SUP * (bank + 1)]
                    i = scalar.activation(htile[:, 512 * f:512 * (f + 1)], ps, AF.Gelu)
                    slast[bank] = pb.bump(i, "S")
                pb.on("scalar", s)

            vlast2 = [None] * 2
            for e in range(8):
                bank = e % 2

                def t2(tensor, tt=tt, e=e, bank=bank, vlast2=vlast2):
                    pb.wait(tensor, "S")
                    if vlast2[bank] is not None:
                        pb.wait(tensor, "V", vlast2[bank])
                    ps = pb.PSB[:, SUP * bank:SUP * (bank + 1)]
                    for f in range(8):
                        i = tensor.matmul(ps, w2buf[:, (f * 8 + e) * 128:(f * 8 + e) * 128 + 128],
                                          htile[:, 512 * f:512 * (f + 1)],
                                          start=(f == 0), stop=(f == 7))
                    pb.bump(i, "T")
                pb.on("tensor", t2)

                def v2(vector, tt=tt, e=e, bank=bank, vlast2=vlast2):
                    pb.wait(vector, "T")
                    ps = pb.PSB[:, SUP * bank:SUP * (bank + 1)]
                    xsl = sb.xs.ap()[:, TPC * e + SUP * tt:TPC * e + SUP * (tt + 1)]
                    i = vector.scalar_tensor_tensor(xsl, ps, 0.5, xsl, ALU.mult, ALU.add)
                    vlast2[bank] = pb.bump(i, "V")
                pb.on("vector", v2)


def emit_ssm(pb, io, dr, sb):
    nc = pb.nc
    lnb = sb.BA.bf16()

    emit_ln(pb, sb, sb.xs.ap(), lnb)

    def g0(gp):
        pb.wait(gp, "V")
        i = gp.dma_start(out=dr.ln5s.ap().rearrange("(k p) t -> p k t", p=128), in_=lnb.rearrange("p (k t) -> p k t", t=TPC))
        pb.bump(i, "ST", 16)
        i = gp.collective_compute("AllGather", ALU.bypass, replica_groups=dr.grp,
                                  ins=[dr.ln5s.ap()], outs=[dr.ln5g.ap()])
        pb.bump(i, "ccln5", 1)
    pb.on("gpsimd", g0)
    pb.barrier()
    sb.row_reset()

    prm = sb.pv(36)
    prm2 = sb.pv(8)
    xproj_sb = sb.pv(132 + 4)

    def fp(sync):
        i = sync.dma_start(out=prm[:, 0:16], in_=io["convw"])
        pb.bump(i, "LD", 16)
        for name, off in [("convb", 16), ("dtw", 20), ("dtb", 24), ("Dpar", 28), ("lng", 32)]:
            i = sync.dma_start(out=prm[:, off:off + 4], in_=io[name])
            pb.bump(i, "LD", 16)
        i = sync.dma_start(out=prm2[:, 0:4], in_=io["lnb"])
        pb.bump(i, "LD", 16)
        i = sync.dma_start(out=sb.Aneg.ap(), in_=io["Alog"])
        pb.bump(i, "LD", 16)
        i = sync.dma_start(out=xproj_sb[:, 0:132].rearrange("p (k c) -> p k c", c=33), in_=io["xprojT"].rearrange("(k p) c -> p k c", p=128))
        pb.bump(i, "LD", 16)
    pb.on("sync", fp)

    def sA(scalar):
        pb.wait(scalar, "LD")
        i = scalar.activation(sb.Aneg.ap(), sb.Aneg.ap(), AF.Exp)
        pb.bump(i, "S")
    pb.on("scalar", sA)

    def vA(vector):
        pb.wait(vector, "S")
        vector.tensor_scalar_mul(sb.Aneg.ap(), sb.Aneg.ap(), -1.0)
        i = vector.memset(sb.states.ap(), 0.0)
        pb.bump(i, "V")
    pb.on("vector", vA)

    load_w_bf16(pb, sb, io["inxbw"], 8, 512, 0)
    load_w_bf16(pb, sb, io["inzw"], 8, 512, 4096)
    pb.barrier()

    xbp = sb.BA.f32()
    xc = sb.BB.f32()
    rhs0 = sb.BD.bf16()[:, 0:4096]
    rhs1 = sb.BD.bf16()[:, 4096:8192]
    zev = sb.BC.f32()[:, 0:512]
    rhsb = [rhs0, rhs1]
    frb = {}
    for kx in range(KCH):
        mmb = {}
        sevb = {}
        for tt in range(16):
            r_tt, m_tt = sup_addr(tt)
            buf = rhsb[tt % 2]

            def fr(sync, kx=kx, tt=tt, buf=buf, r_tt=r_tt, m_tt=m_tt, mmb=mmb, frb=frb):
                if kx == 0 and tt == 0:
                    pb.wait(sync, "ccln5", 1)
                prev = mmb.get(tt - 2) if tt >= 2 else (frb.get(kx - 1) if kx >= 1 else None)
                if prev is not None:
                    pb.wait(sync, "T", prev)
                src3 = dr.ln5g.ap()[1024 * r_tt:1024 * (r_tt + 1),
                                    SUP * m_tt:SUP * (m_tt + 1)]
                i = sync.dma_start(out=buf.rearrange("p (k t) -> p k t", t=SUP),
                                   in_=src3.rearrange("(k p) t -> p k t", p=128))
                pb.bump(i, "LD", 16)
            pb.on("sync", fr)

            def tm(tensor, kx=kx, tt=tt, buf=buf, mmb=mmb, sevb=sevb, frb=frb):
                pb.wait(tensor, "LD")
                if tt >= 2 and (tt - 2) in sevb:
                    pb.wait(tensor, "S", sevb[tt - 2])
                psx = pb.PSA[:, SUP * (tt % 2):SUP * (tt % 2 + 1)]
                psz = pb.PSA[:, 1024 + SUP * (tt % 2):1024 + SUP * (tt % 2 + 1)]
                for kdm in range(NCH):
                    tensor.matmul(psx, sb.WBUF.ap()[:, (kdm * 4 + kx) * 128:(kdm * 4 + kx) * 128 + 128],
                                  buf[:, 512 * kdm:512 * (kdm + 1)], start=(kdm == 0), stop=(kdm == 7))
                    i = tensor.matmul(psz, sb.WBUF.ap()[:, 4096 + (kdm * 4 + kx) * 128:4096 + (kdm * 4 + kx) * 128 + 128],
                                      buf[:, 512 * kdm:512 * (kdm + 1)], start=(kdm == 0), stop=(kdm == 7))
                mmb[tt] = pb.bump(i, "T")
                if tt == 15:
                    frb[kx] = mmb[tt]
            pb.on("tensor", tm)

            def sev(scalar, kx=kx, tt=tt, sevb=sevb):
                pb.wait(scalar, "T")
                if tt == 0:
                    pb.wait(scalar, "ST")
                psx = pb.PSA[:, SUP * (tt % 2):SUP * (tt % 2 + 1)]
                psz = pb.PSA[:, 1024 + SUP * (tt % 2):1024 + SUP * (tt % 2 + 1)]
                scalar.copy(xbp[:, SUP * tt:SUP * (tt + 1)], psx)
                i = scalar.copy(zev, psz)
                sevb[tt] = pb.bump(i, "S")
            pb.on("scalar", sev)

            def gz(gp, kx=kx, tt=tt):
                pb.wait(gp, "S")
                i = gp.dma_start(out=dr.zst.ap()[kx, :, SUP * tt:SUP * (tt + 1)], in_=zev)
                pb.bump(i, "ST", 16)
            pb.on("gpsimd", gz)
            def sguard(scalar):
                pb.wait(scalar, "ST")
            pb.on("scalar", sguard)

        def vc(vector, kx=kx):
            pb.wait(vector, "S")
            vector.tensor_scalar(xc, xbp, prm[:, 4 * kx + 3:4 * kx + 4],
                                 prm[:, 16 + kx:17 + kx], ALU.mult, ALU.add)
            vector.scalar_tensor_tensor(xc[:, 1:L], xbp[:, 0:L - 1], prm[:, 4 * kx + 2:4 * kx + 3],
                                        xc[:, 1:L], ALU.mult, ALU.add)
            vector.scalar_tensor_tensor(xc[:, 2:L], xbp[:, 0:L - 2], prm[:, 4 * kx + 1:4 * kx + 2],
                                        xc[:, 2:L], ALU.mult, ALU.add)
            i = vector.scalar_tensor_tensor(xc[:, 3:L], xbp[:, 0:L - 3], prm[:, 4 * kx + 0:4 * kx + 1],
                                            xc[:, 3:L], ALU.mult, ALU.add)
            pb.bump(i, "V")
        pb.on("vector", vc)

        def sc(scalar, kx=kx):
            pb.wait(scalar, "V")
            i = scalar.activation(xbp, xc, AF.Silu)
            pb.bump(i, "S")
        pb.on("scalar", sc)

        def gx(gp, kx=kx):
            pb.wait(gp, "S")
            i = gp.dma_start(out=dr.xbcst.ap()[kx], in_=xbp)
            pb.bump(i, "ST", 16)
        pb.on("gpsimd", gx)

        xpe = sb.BC.f32()[0:33, 512:1024]
        for tt in range(16):
            def tp(tensor, kx=kx, tt=tt):
                pb.wait(tensor, "S")
                i = tensor.matmul(pb.PSB[0:33, 0:512], xproj_sb[:, 33 * kx:33 * (kx + 1)],
                                  xbp[:, SUP * tt:SUP * (tt + 1)], start=True, stop=True)
                pb.bump(i, "T")
            pb.on("tensor", tp)

            def sxe(scalar, kx=kx, tt=tt, xpe=xpe):
                pb.wait(scalar, "T")
                pb.wait(scalar, "ST")
                i = scalar.copy(xpe, pb.PSB[0:33, 0:512])
                pb.bump(i, "S")
            pb.on("scalar", sxe)

            def gp2(gp, kx=kx, tt=tt, xpe=xpe):
                pb.wait(gp, "S")
                i = gp.dma_start(out=dr.xpk.ap()[kx, :, SUP * tt:SUP * (tt + 1)], in_=xpe)
                pb.bump(i, "ST", 16)
            pb.on("gpsimd", gp2)

    pb.barrier()

    acc = sb.BC.f32()[0:33, 0:2048]
    ld = sb.BD.f32()[0:33, 0:2048]
    for lc in range(4):
        for kx in range(KCH):
            def fl(sync, lc=lc, kx=kx):
                pb.wait(sync, "V")
                i = sync.dma_start(out=ld, in_=dr.xpk.ap()[kx, :, TPC * lc:TPC * (lc + 1)])
                pb.bump(i, "LD", 16)
            pb.on("sync", fl)

            def va(vector, lc=lc, kx=kx):
                pb.wait(vector, "LD")
                if kx == 0:
                    i = vector.tensor_copy(acc, ld)
                else:
                    i = vector.tensor_add(acc, acc, ld)
                pb.bump(i, "V")
            pb.on("vector", va)

        def gs(gp, lc=lc):
            pb.wait(gp, "V")
            i = gp.dma_start(out=dr.xprojp.ap()[:, TPC * lc:TPC * (lc + 1)], in_=acc)
            pb.bump(i, "ST", 16)
        pb.on("gpsimd", gs)

        def vguard(vector):
            pb.wait(vector, "ST")
        pb.on("vector", vguard)

    def gar(gp):
        pb.wait(gp, "ST")
        i = gp.collective_compute("AllReduce", ALU.add, replica_groups=dr.grp,
                                  ins=[dr.xprojp.ap()], outs=[dr.xprojr.ap()])
        pb.bump(i, "ccxp", 1)
    pb.on("gpsimd", gar)
    pb.barrier()

    dtinrow = sb.pool_row_at(0, 2048)
    ubuf = sb.BB.f32()[:, 0:2048]
    rbuf = sb.BB.f32()[:, 2048:4096]
    abuf = sb.BA.f32()[:, 0:2048]
    for lc in range(4):
        def fd(sync, lc=lc):
            pb.wait(sync, "ccxp", 1)
            pb.wait(sync, "ST")
            pb.wait(sync, "V")
            i = sync.dma_start(out=dtinrow, in_=dr.xprojr.ap()[0:1, TPC * lc:TPC * (lc + 1)])
            pb.bump(i, "LD", 16)
        pb.on("sync", fd)

        def td(tensor, lc=lc):
            pb.wait(tensor, "LD")
            pb.wait(tensor, "V")
            for q in range(4):
                i = tensor.matmul(pb.PSA[:, 512 * q:512 * (q + 1)], sb.onesr.ap()[:, 0:128],
                                  dtinrow[:, 512 * q:512 * (q + 1)], start=True, stop=True)
            pb.bump(i, "T")
        pb.on("tensor", td)

        for kx in range(KCH):
            def vu(vector, lc=lc, kx=kx):
                pb.wait(vector, "T")
                pb.wait(vector, "ST")
                i = vector.tensor_scalar(ubuf, pb.PSA[:, 0:2048], prm[:, 20 + kx:21 + kx],
                                         prm[:, 24 + kx:25 + kx], ALU.mult, ALU.add)
                pb.bump(i, "V")
            pb.on("vector", vu)

            def vd0(vector, lc=lc, kx=kx):
                i = vector.tensor_scalar_max(rbuf, ubuf, 0.0)
                pb.bump(i, "V")
            pb.on("vector", vd0)

            def sd(scalar, lc=lc, kx=kx):
                pb.wait(scalar, "V")
                scalar.activation(abuf, ubuf, AF.Abs)
                scalar.activation(abuf, abuf, AF.Exp, scale=sb.cv(-1.0))
                i = scalar.activation(abuf, abuf, AF.Ln, bias=sb.cv(1.0))
                pb.bump(i, "S")
            pb.on("scalar", sd)

            def vd(vector, lc=lc, kx=kx):
                pb.wait(vector, "S")
                i = vector.tensor_add(ubuf, rbuf, abuf)
                pb.bump(i, "V")
            pb.on("vector", vd)

            def gd(gp, lc=lc, kx=kx):
                pb.wait(gp, "V")
                i = gp.dma_start(out=dr.dtst.ap()[kx, :, TPC * lc:TPC * (lc + 1)], in_=ubuf)
                pb.bump(i, "ST", 16)
            pb.on("gpsimd", gd)
    pb.barrier()

    LC = 1024
    xbct = [sb.BA.f32()[:, LC * k:LC * (k + 1)] for k in range(4)]
    dtt = [sb.BA.f32()[:, 4096 + LC * k:4096 + LC * (k + 1)] for k in range(4)]
    wt = [sb.BB.f32()[:, LC * k:LC * (k + 1)] for k in range(4)]
    yt = [sb.BB.f32()[:, 4096 + LC * k:4096 + LC * (k + 1)] for k in range(4)]
    dA0 = sb.BC.f32()[:, 0:1024]
    dA1 = sb.BC.f32()[:, 1024:2048]
    dAb = [dA0, dA1]
    xst = sb.BC.f32()[:, 2048:3072]
    ht = sb.BD.f32()[:, 2048:3072]
    hctb = [sb.BD.f32()[:, 0:1024], sb.BD.f32()[:, 3072:4096]]
    sqt = sb.BD.f32()[:, 1024:2048]
    rowB = sb.pool_row_at(0, 1024)
    rowC = sb.pool_row_at(1024, 1024)

    for lc in range(8):
        def fl2(sync, lc=lc):
            pb.wait(sync, "T")
            pb.wait(sync, "V")
            pb.wait(sync, "G2")
            pb.wait(sync, "ST")
            for k in range(4):
                i = sync.dma_start(out=xbct[k], in_=dr.xbcst.ap()[k, :, LC * lc:LC * (lc + 1)])
                pb.bump(i, "LD", 16)
                i = sync.dma_start(out=dtt[k], in_=dr.dtst.ap()[k, :, LC * lc:LC * (lc + 1)])
                pb.bump(i, "LD", 16)
        pb.on("sync", fl2)

        def vw(vector, lc=lc):
            pb.wait(vector, "LD")
            for k in range(4):
                vector.tensor_mul(wt[k], xbct[k], dtt[k])
                i = vector.tensor_scalar(yt[k], xbct[k], prm[:, 28 + k:29 + k], None, ALU.mult)
            pb.bump(i, "V")
        pb.on("vector", vw)

        for n in range(DST):
            def fr2(sync, lc=lc, n=n):
                pb.wait(sync, "T")
                i = sync.dma_start(out=rowB, in_=dr.xprojr.ap()[1 + n:2 + n, LC * lc:LC * (lc + 1)])
                pb.bump(i, "LD", 16)
                i = sync.dma_start(out=rowC, in_=dr.xprojr.ap()[17 + n:18 + n, LC * lc:LC * (lc + 1)])
                pb.bump(i, "LD", 16)
            pb.on("sync", fr2)

            def tb(tensor, lc=lc, n=n):
                pb.wait(tensor, "LD")
                pb.wait(tensor, "V")
                tensor.matmul(pb.PSA[:, 0:512], sb.onesr.ap()[:, 0:128], rowB[:, 0:512],
                              start=True, stop=True)
                tensor.matmul(pb.PSA[:, 512:1024], sb.onesr.ap()[:, 0:128], rowB[:, 512:1024],
                              start=True, stop=True)
                tensor.matmul(pb.PSA[:, 1024:1536], sb.onesr.ap()[:, 0:128], rowC[:, 0:512],
                              start=True, stop=True)
                i = tensor.matmul(pb.PSA[:, 1536:2048], sb.onesr.ap()[:, 0:128], rowC[:, 512:1024],
                                  start=True, stop=True)
                pb.bump(i, "T")
            pb.on("tensor", tb)

            vsb = [None] * 4
            for k in range(4):
                col = DST * k + n

                def sda(scalar, lc=lc, n=n, k=k, col=col, vsb=vsb):
                    if k == 0:
                        pb.wait(scalar, "LD")
                    if k >= 2 and vsb[k - 2] is not None:
                        pb.wait(scalar, "V", vsb[k - 2])
                    elif k < 2:
                        pb.wait(scalar, "V")
                    i = scalar.activation(dAb[k % 2], dtt[k], AF.Exp,
                                          scale=sb.Aneg.ap()[:, col:col + 1])
                    pb.bump(i, "S")
                pb.on("scalar", sda)

                def vs(vector, lc=lc, n=n, k=k, col=col, vsb=vsb):
                    pb.wait(vector, "T")
                    pb.wait(vector, "S")
                    if pb.cnt.get("G2", 0) >= 2:
                        pb.wait(vector, "G2", pb.cnt["G2"] - 1)
                    vector.tensor_mul(xst, wt[k], pb.PSA[:, 0:1024])
                    vector.tensor_tensor_scan(ht, dAb[k % 2], xst,
                                              sb.states.ap()[:, col:col + 1],
                                              ALU.mult, ALU.add)
                    vector.tensor_copy(sb.states.ap()[:, col:col + 1], ht[:, LC - 1:LC])
                    i = vector.tensor_mul(hctb[k % 2], ht, pb.PSA[:, 1024:2048])
                    vsb[k] = pb.bump(i, "V")
                pb.on("vector", vs)

                def gadd(gp, lc=lc, n=n, k=k, vsb=vsb):
                    pb.wait(gp, "V", vsb[k])
                    i = gp.tensor_add(yt[k], yt[k], hctb[k % 2])
                    pb.bump(i, "G2")
                pb.on("gpsimd", gadd)

        for k in range(4):
            def sy(scalar, lc=lc, k=k):
                pb.wait(scalar, "V")
                pb.wait(scalar, "T")
                pb.wait(scalar, "G2")
                i = scalar.activation(sqt, yt[k], AF.Square)
                pb.bump(i, "S")
            pb.on("scalar", sy)

            def ty(tensor, lc=lc, k=k):
                pb.wait(tensor, "S")
                for q in range(2):
                    tensor.matmul(pb.PSB[0:1, 512 * q:512 * (q + 1)], sb.onescol.ap(),
                                  yt[k][:, 512 * q:512 * (q + 1)], start=(k == 0), stop=(k == 3))
                    i = tensor.matmul(pb.PSB[0:1, 1024 + 512 * q:1024 + 512 * (q + 1)], sb.onescol.ap(),
                                      sqt[:, 512 * q:512 * (q + 1)], start=(k == 0), stop=(k == 3))
                pb.bump(i, "T")
            pb.on("tensor", ty)

            def gy(gp, lc=lc, k=k):
                pb.wait(gp, "V")
                i = gp.dma_start(out=dr.yst.ap()[k, :, LC * lc:LC * (lc + 1)], in_=yt[k])
                pb.bump(i, "ST", 16)
            pb.on("gpsimd", gy)

        statsrow = sb.vecs.ap()[0:1, 0:2048]

        def vst(vector, lc=lc):
            pb.wait(vector, "T")
            pb.wait(vector, "ST")
            vector.tensor_copy(statsrow[:, 0:1024], pb.PSB[0:1, 0:1024])
            i = vector.tensor_copy(statsrow[:, 1024:2048], pb.PSB[0:1, 1024:2048])
            pb.bump(i, "V")
        pb.on("vector", vst)

        def gst(gp, lc=lc, statsrow=statsrow):
            pb.wait(gp, "V")
            i = gp.dma_start(out=dr.statsp.ap()[0:1, LC * lc:LC * (lc + 1)], in_=statsrow[:, 0:1024])
            pb.bump(i, "ST", 16)
            i = gp.dma_start(out=dr.statsp.ap()[0:1, L + LC * lc:L + LC * (lc + 1)],
                             in_=statsrow[:, 1024:2048])
            pb.bump(i, "ST", 16)
        pb.on("gpsimd", gst)

        def tguard2(tensor):
            pb.wait(tensor, "V")
        pb.on("tensor", tguard2)

    def gar2(gp):
        pb.wait(gp, "ST")
        i = gp.collective_compute("AllReduce", ALU.add, replica_groups=dr.grp,
                                  ins=[dr.statsp.ap()], outs=[dr.statsr.ap()])
        pb.bump(i, "ccst", 1)
    pb.on("gpsimd", gar2)
    pb.barrier()
    sb.row_reset()

    load_w_bf16(pb, sb, io["outwT"], 4, 1024, 0)
    pb.barrier()

    rsum = sb.pool_row_at(0, 2048)
    yti = [sb.BA.f32()[:, 512 * k:512 * (k + 1)] for k in range(4)]
    zti = [sb.BA.f32()[:, 2048 + 512 * k:2048 + 512 * (k + 1)] for k in range(4)]
    yzb = [sb.BB.bf16()[:, 512 * k:512 * (k + 1)] for k in range(4)]
    szt = sb.BA.f32()[:, 4096:4608]
    ynt = sb.BA.f32()[:, 4608:5120]
    oev = sb.BA.f32()[:, 5120:5632]
    mrow = sb.vecs.ap()[0:1, 0:2048]
    vrow = sb.rows2.ap()

    for lc in range(4):
        def fs(sync, lc=lc):
            pb.wait(sync, "ccst", 1)
            pb.wait(sync, "V")
            pb.wait(sync, "T")
            i = sync.dma_start(out=rsum, in_=dr.statsr.ap()[0:1, TPC * lc:TPC * (lc + 1)])
            pb.bump(i, "LD", 16)
            i = sync.dma_start(out=vrow, in_=dr.statsr.ap()[0:1, L + TPC * lc:L + TPC * (lc + 1)])
            pb.bump(i, "LD", 16)
        pb.on("sync", fs)

        def vr(vector, lc=lc):
            pb.wait(vector, "LD")
            vector.tensor_scalar_mul(mrow, rsum, 1.0 / DIN)
            vector.tensor_scalar_mul(vrow, vrow, 1.0 / DIN)
            vector.tensor_mul(rsum, mrow, mrow)
            i = vector.tensor_sub(vrow, vrow, rsum)
            pb.bump(i, "V")
        pb.on("vector", vr)

        def sr(scalar, lc=lc):
            pb.wait(scalar, "V")
            scalar.activation(vrow, vrow, AF.Ln, bias=sb.cv(1e-5, 1))
            i = scalar.activation(vrow, vrow, AF.Exp, scale=sb.cv(-0.5, 1))
            pb.bump(i, "S")
        pb.on("scalar", sr)

        for tile in range(4):
            s_glob = 4 * lc + tile
            r_s, m_s = sup_addr(s_glob)

            def tbm(tensor, lc=lc, tile=tile):
                pb.wait(tensor, "S")
                pb.wait(tensor, "V")
                tensor.matmul(pb.PSA[:, 0:512], sb.onesr.ap()[:, 0:128],
                              mrow[:, 512 * tile:512 * (tile + 1)], start=True, stop=True)
                i = tensor.matmul(pb.PSA[:, 512:1024], sb.onesr.ap()[:, 0:128],
                                  vrow[:, 512 * tile:512 * (tile + 1)], start=True, stop=True)
                pb.bump(i, "T")
            pb.on("tensor", tbm)

            for k in range(KCH):
                def fy(sync, lc=lc, tile=tile, k=k):
                    pb.wait(sync, "V")
                    i = sync.dma_start(out=yti[k], in_=dr.yst.ap()[k, :, TPC * lc + 512 * tile:TPC * lc + 512 * (tile + 1)])
                    pb.bump(i, "LD", 16)
                    i = sync.dma_start(out=zti[k], in_=dr.zst.ap()[k, :, TPC * lc + 512 * tile:TPC * lc + 512 * (tile + 1)])
                    pb.bump(i, "LD", 16)
                pb.on("sync", fy)

                def sz(scalar, lc=lc, tile=tile, k=k):
                    pb.wait(scalar, "LD")
                    pb.wait(scalar, "V")
                    i = scalar.activation(szt, zti[k], AF.Silu)
                    pb.bump(i, "S")
                pb.on("scalar", sz)

                def vn(vector, lc=lc, tile=tile, k=k):
                    pb.wait(vector, "T")
                    pb.wait(vector, "S")
                    vector.tensor_sub(ynt, yti[k], pb.PSA[:, 0:512])
                    vector.tensor_mul(ynt, ynt, pb.PSA[:, 512:1024])
                    vector.tensor_scalar(ynt, ynt, prm[:, 32 + k:33 + k],
                                         prm2[:, k:k + 1], ALU.mult, ALU.add)
                    i = vector.tensor_mul(yzb[k], ynt, szt)
                    pb.bump(i, "V")
                pb.on("vector", vn)

            for e in range(8):
                def te(tensor, lc=lc, tile=tile, e=e, r_s=r_s, m_s=m_s):
                    pb.wait(tensor, "V")
                    pb.wait(tensor, "S")
                    ps = pb.PSB[:, 512 * (e % 2):512 * (e % 2) + 512]
                    for k in range(KCH):
                        i = tensor.matmul(ps, sb.WBUF.ap()[:, (k * 8 + e) * 128:(k * 8 + e) * 128 + 128],
                                          yzb[k], start=(k == 0), stop=(k == 3))
                    pb.bump(i, "T")
                pb.on("tensor", te)

                def so(scalar, lc=lc, tile=tile, e=e):
                    pb.wait(scalar, "T")
                    pb.wait(scalar, "ST")
                    ps = pb.PSB[:, 512 * (e % 2):512 * (e % 2) + 512]
                    i = scalar.copy(oev, ps)
                    pb.bump(i, "S")
                pb.on("scalar", so)

                def go(gp, lc=lc, tile=tile, e=e, r_s=r_s, m_s=m_s):
                    pb.wait(gp, "S")
                    i = gp.dma_start(out=dr.poutb.ap()[r_s, 128 * e:128 * (e + 1),
                                                       SUP * m_s:SUP * (m_s + 1)], in_=oev)
                    pb.bump(i, "ST", 16)
                pb.on("gpsimd", go)

    def grs(gp):
        pb.wait(gp, "ST")
        i = gp.collective_compute("ReduceScatter", ALU.add, replica_groups=dr.grp,
                                  ins=[dr.poutb.ap().rearrange("r e t -> (r e) t")],
                                  outs=[dr.prsout.ap()])
        pb.bump(i, "ccrs", 1)
    pb.on("gpsimd", grs)
    pb.barrier()

    for half in range(2):
        buf = (sb.BA.f32() if half == 0 else sb.BB.f32())

        def fres(sync, half=half, buf=buf):
            pb.wait(sync, "ccrs", 1)
            i = sync.dma_start(out=buf.rearrange("p (k t) -> p k t", t=TPC),
                               in_=dr.prsout.ap()[512 * half:512 * (half + 1), :]
                               .rearrange("(k p) t -> p k t", p=128))
            pb.bump(i, "LD", 16)
        pb.on("sync", fres)

        def vres(vector, half=half, buf=buf):
            pb.wait(vector, "LD")
            for k in range(4):
                kk = 4 * half + k
                i = vector.tensor_add(sb.xs.ap()[:, TPC * kk:TPC * (kk + 1)],
                                      sb.xs.ap()[:, TPC * kk:TPC * (kk + 1)],
                                      buf[:, TPC * k:TPC * (k + 1)])
            pb.bump(i, "V")
        pb.on("vector", vres)


def emit_weff_load(pb, dr, sb):
    beff = sb.pool_row_at(1024, 1024)

    def f(sync):
        pb.wait(sync, "ccweff", 1)
        pb.wait(sync, "ST")
        pb.wait(sync, "T")
        pb.wait(sync, "V")
        for r in range(8):
            i = sync.dma_start(out=sb.WBUF.ap()[:, 1024 * r:1024 * (r + 1)],
                               in_=dr.weffg.ap()[128 * r:128 * (r + 1), :])
            pb.bump(i, "LD", 16)
        i = sync.dma_start(out=beff, in_=dr.beffd.ap())
        pb.bump(i, "LD", 16)
    pb.on("sync", f)
    return beff


IO_SPECS = {
    "xin": ([D, TPC], F32), "w0slice": ([8, 128, 1024], F32), "b0all": ([8, 1024], F32),
    "logits": ([1, 8], F32), "uu": ([1, 8], F32),
    "wq": ([1024, 256], F32), "bq": ([1, 256], F32),
    "wqc": ([256, 896], F32), "bqc": ([1, 896], F32),
    "wqr": ([1024, 128], F32), "bqr": ([1, 128], F32),
    "wkv": ([1024, 256], F32), "bkv": ([1, 256], F32),
    "wkr": ([1024, 128], F32), "bkr": ([1, 128], F32),
    "wkc": ([256, 896], F32), "bkc": ([1, 896], F32),
    "wv": ([256, 1024], F32), "bv": ([1, 1024], F32),
    "wo": ([1024, 1024], F32), "bo": ([1, 1024], F32),
    "cosT": ([128, TPC], F32), "sinT": ([128, TPC], F32),
    "masks": ([4, 128, SUP], BF16), "padtab": ([128, NBT], F32),
    "ident": ([128, 128], BF16), "rotT": ([128, 128], F32),
    "w1Ts": ([1024, 512], F32), "w2Ts": ([512, 1024], F32),
    "b1row": ([1, 4096], F32), "b2col": ([128, 8], F32),
    "inxbw": ([1024, 512], F32), "inzw": ([1024, 512], F32),
    "xprojT": ([512, 33], F32),
    "convw": ([128, 16], F32), "convb": ([128, 4], F32),
    "Alog": ([128, 64], F32), "Dpar": ([128, 4], F32),
    "dtw": ([128, 4], F32), "dtb": ([128, 4], F32),
    "lng": ([128, 4], F32), "lnb": ([128, 4], F32),
    "outwT": ([512, 1024], F32),
}


def build_program(upto=6):
    import contextlib

    nc = bass.Bass("TRN2", target_bir_lowering=False, dynamic_dma_scratch_size=16384)
    stack = contextlib.ExitStack()
    io = {}
    for name, (shape, dt) in IO_SPECS.items():
        io[name] = nc.dram_tensor(name, shape, dt, kind="ExternalInput").ap()
    xout = nc.dram_tensor("xout", [D, TPC], F32, kind="ExternalOutput").ap()

    sb = SB(nc)
    dr = DRT(nc, [[0, 1, 2, 3], [4, 5, 6, 7]])
    pb = PB(nc, stack)
    pb.PSA = nc.alloc_psum_tensor("PSA", [128, 2048], F32).ap()
    pb.PSB = nc.alloc_psum_tensor("PSB", [128, 2048], F32).ap()

    emit_consts(pb, io, sb)
    emit_xin(pb, io, sb)
    emit_expert_prep(pb, io, dr, sb)
    pb.barrier()
    emit_ternarize(pb, io, dr, sb)
    pb.barrier()
    sb.row_reset()

    if upto >= 1:
        emit_ln(pb, sb, sb.xs.ap(), sb.BA.bf16())
        pb.barrier()
        beff = emit_weff_load(pb, dr, sb)
        pb.barrier()
        emit_ffn0(pb, sb, sb.BA.bf16(), beff)
        pb.barrier()

    if upto >= 2:
        emit_mla(pb, io, dr, sb)
        emit_attention(pb, io, dr, sb)
        pb.barrier()
        sb.row_reset()

    if upto >= 3:
        sb.row_reset()
        emit_ln(pb, sb, sb.xs.ap(), sb.BA.bf16())
        pb.barrier()
        beff = emit_weff_load(pb, dr, sb)
        pb.barrier()
        emit_ffn0(pb, sb, sb.BA.bf16(), beff)
        pb.barrier()

    if upto >= 4:
        emit_ln(pb, sb, sb.xs.ap(), sb.BA.bf16())
        pb.barrier()
        sb.row_reset()
        emit_ffn1(pb, io, dr, sb)
        pb.barrier()
        sb.row_reset()

    if upto >= 5:
        emit_ssm(pb, io, dr, sb)
        pb.barrier()
        sb.row_reset()

    if upto >= 6:
        emit_ln(pb, sb, sb.xs.ap(), sb.BA.bf16())
        pb.barrier()
        emit_ffn1(pb, io, dr, sb)
        pb.barrier()

    snap = dict(pb.cnt)

    def fo(gp):
        for name, v in snap.items():
            if v > 0:
                gp.wait_ge(pb.sems[name], v)
        i = gp.dma_start(out=xout.rearrange("(k p) t -> p k t", p=128), in_=sb.xs.ap().rearrange("p (k t) -> p k t", t=TPC))
        pb.bump(i, "ST", 16)
        pb.wait(gp, "ST")
    pb.on("gpsimd", fo)

    stack.close()
    return nc


DBG_NORM = None
DBG_CONST_BIAS = False
DBG_ROPE_BARRIER = False
_NC_CACHE = {}


def kernel(**inputs):
    maps = make_inputs(inputs)
    upto = int(_NC_CACHE.get("upto", 6))
    if _NC_CACHE.get("nc_upto") != upto:
        _NC_CACHE["nc"] = build_program(upto)
        _NC_CACHE["nc_upto"] = upto
    nc = _NC_CACHE["nc"]
    res = run_bass_kernel_spmd(nc, maps, core_ids=list(range(8)))
    return assemble_output(res.results)
